# version 19
# baseline (speedup 1.0000x reference)
import contextlib
import numpy as np
import ml_dtypes

import concourse.bass as bass
import concourse.mybir as mybir
from concourse.bass_utils import run_bass_kernel_spmd
from concourse.library_config import all_libraries
from concourse.library_overlay import lower_extended_insts

F32 = mybir.dt.float32
BF16 = mybir.dt.bfloat16
I32 = mybir.dt.int32
I16 = mybir.dt.int16
U32 = mybir.dt.uint32
AF = mybir.ActivationFunctionType
ALU = mybir.AluOpType
AX = mybir.AxisListType
LIBS = {l.name: l for l in all_libraries}

P = 128
D = 1024
S = 2048
NSEQ = 2
T = NSEQ * S
NH = 16
HD = 64
DC = D // P
SC = S // P
TT = 512
NTT = S // TT
DH = 4096
DHC = DH // P
LAT = 256
E = 8
CAP = 4096
TTOT = 8 * T
C_L0 = 24.0
C_L1 = 6.0
RND = 256.0

ENGS = ("pe", "act", "dve", "pop", "p1", "pc", "pc2",
        "sp", "s1", "s2", "a1")
ENG_ATTR = {"pe": "tensor", "act": "scalar", "dve": "vector", "pop": "gpsimd",
            "p1": "gpsimd", "pc": "gpsimd", "pc2": "gpsimd",
            "sp": "sync", "s1": "sync", "s2": "sync", "a1": "scalar"}
QKEY = {"pc": "pop", "pc2": "pop", "p1": "pop", "s1": "sp", "s2": "sp",
        "a1": "act"}
QUEUES = ("pe", "act", "dve", "pop", "sp")

STAGES = ("p0", "b1", "b2", "b3", "route", "moe", "full")
UPTO = "full"
TAPS = []
DBG_B2 = False


class _Res:

    def __init__(self, name):
        self.name = name


class Ctx:

    def __init__(self, nc):
        self.nc = nc
        self.stack = contextlib.ExitStack()
        self.sems = {e: self.stack.enter_context(nc.semaphore(f"ch_{e}"))
                     for e in ENGS}
        self.count = {e: 0 for e in ENGS}
        self.q = {e: [] for e in QUEUES}
        self.waited = {e: {e2: 0 for e2 in ENGS} for e in ENGS}
        self.track = {}
        self.alias = {}
        self.extent = {}
        self.overl = {}
        self.bounds = {"sp": [0], "s1": [0], "s2": [0], "a1": [0],
                       "pop": [0], "p1": [0]}

    def _bytes(self, shape, dt):
        import concourse.mybir as _mb
        return shape[1] * _mb.dt.size(dt)

    def _reg(self, name, space, lo, hi):
        self.track[name] = [None, []]
        self.extent[name] = (space, lo, hi)
        self.overl[name] = []
        for o, (sp2, lo2, hi2) in self.extent.items():
            if o != name and sp2 == space and lo < hi2 and lo2 < hi:
                self.overl[name].append(o)
                self.overl[o].append(name)

    def res(self, name):
        r = _Res(name)
        self._reg(name, "res:" + name, 0, 1)
        return r

    def sb(self, name, shape, dt):
        t = self.stack.enter_context(self.nc.sbuf_tensor(name, shape, dt))
        base = self.nc.lookup_mloc(t).addr
        self._reg(t.name, "sb", base, base + self._bytes(shape, dt))
        return t

    def sb_at(self, name, shape, dt, backing, byte_off):
        base = self.nc.lookup_mloc(backing).addr
        v = self.nc.alloc_sbuf_tensor_at(name, shape, dt, offset=base + byte_off)
        self._reg(v.name, "sb", base + byte_off,
                  base + byte_off + self._bytes(shape, dt))
        self.alias[v.name] = v.name
        return v

    def ps(self, name, shape, dt=F32):
        t = self.stack.enter_context(self.nc.psum_tensor(name, shape, dt))
        base = self.nc.lookup_mloc(t).addr
        self._reg(t.name, "ps", base, base + self._bytes(shape, dt))
        return t

    def ps_at(self, name, shape, dt, backing, byte_off):
        v = self.nc.virtual_psum_tensor(name, shape, dt)
        self.nc.back_virtual_tensor_with(v, backing)
        base = self.nc.lookup_mloc(backing).addr
        self._reg(v.name, "ps", base + byte_off,
                  base + byte_off + self._bytes(shape, dt))
        return v

    def _req(self, eng, tok):
        if tok is None:
            return
        pe, pv = tok
        if pe in self.bounds:
            bs = self.bounds[pe]
            b = None
            for x in bs:
                if x >= pv:
                    b = x
                    break
            if b is None:
                cur = self.count[pe]
                sem0 = self.sems[pe]
                if self.waited[pe][pe] < cur:
                    self.waited[pe][pe] = cur
                    self.q[QKEY.get(pe, pe)].append(
                        lambda E_, sem0=sem0, cur=cur: E_.wait_ge(sem0, cur))
                bs.append(cur)
                b = cur
            pv = b
            if QKEY.get(pe, pe) == QKEY.get(eng, eng):
                return
        if self.waited[eng][pe] >= pv:
            return
        self.waited[eng][pe] = pv
        sem = self.sems[pe]
        self.q[QKEY.get(eng, eng)].append(
            lambda E_, sem=sem, pv=pv: E_.wait_ge(sem, pv))

    def _names(self, b):
        return (b.name,) + tuple(self.overl.get(b.name, ()))

    def op(self, eng, emit, writes=(), reads=(), inc=1, pre=()):
        for b in reads:
            for n in self._names(b):
                self._req(eng, self.track[n][0])
        for b in tuple(pre) + tuple(writes):
            for n in self._names(b):
                wt, rts = self.track[n]
                self._req(eng, wt)
                for t in rts:
                    self._req(eng, t)
        tok = None
        if inc:
            self.count[eng] += inc
            tok = (eng, self.count[eng])
            sem = self.sems[eng]
            self.q[QKEY.get(eng, eng)].append(
                lambda E_, emit=emit, sem=sem, inc=inc:
                emit(E_).then_inc(sem, inc))
        else:
            self.q[QKEY.get(eng, eng)].append(lambda E_, emit=emit: emit(E_))
        for b in writes:
            self.track[b.name] = [tok, []]
        if inc:
            for b in reads:
                self.track[b.name][1].append(tok)
        return tok

    def fence(self, eng):
        for e2 in ENGS:
            if e2 != eng:
                self._req(eng, (e2, self.count[e2]))

    def fence_all(self):
        for e in ENGS:
            self.fence(e)

    def emit_all(self):
        with self.nc.Block() as block:
            for e in QUEUES:
                fns = list(self.q[e])

                def make(fns):
                    def body(EE):
                        for fn in fns:
                            fn(EE)
                    return body
                getattr(block, ENG_ATTR[e])(make(fns))


def _host_consts():
    bf = ml_dtypes.bfloat16
    iden = np.eye(P, dtype=np.float32)
    mask = np.zeros((P, 4 * TT), np.float32)
    for i in range(4):
        for k in range(P):
            lo = 128 * i + k
            if lo < TT:
                mask[k, i * TT + lo:(i + 1) * TT] = 1.0
    return {
        "iden_b": iden.astype(bf),
        "iden_f": iden.copy(),
        "triu1_b": np.triu(np.ones((P, P), np.float32), 1).astype(bf),
        "maskl1_b": mask.astype(bf),
    }


def build():
    nc = bass.Bass("TRN2", target_bir_lowering=False)
    cx = Ctx(nc)

    x_in = nc.dram_tensor("x", [NSEQ, S, D], F32, kind="ExternalInput").ap()
    y_out = nc.dram_tensor("y", [NSEQ, S, D], F32, kind="ExternalOutput").ap()
    W = {}
    for name, shape in [
        ("W_qkv", [D, 3 * D]), ("b_qkv", [3 * D]), ("W_o", [D, D]), ("b_o", [D]),
        ("ffn_w1", [DH, D]), ("conv_w", [DH, 1, 3]), ("conv_b", [DH]),
        ("ffn_w2", [D, DH]),
        ("wq_lat", [D, LAT]), ("wk_lat", [D, LAT]), ("wv_lat", [D, LAT]),
        ("wq_heads", [LAT, D]), ("wk_heads", [LAT, D]), ("wv_heads", [LAT, D]),
        ("w_proj", [D, D]), ("b_proj", [D]),
        ("Wr", [D, E]), ("W1_my", [D, DH]), ("W2_my", [DH, D]),
    ]:
        W[name] = nc.dram_tensor(name, shape, F32, kind="ExternalInput").ap()
    HC = {}
    for name, dt, shape in [
        ("iden_b", BF16, [P, P]), ("iden_f", F32, [P, P]),
        ("triu1_b", BF16, [P, P]), ("maskl1_b", BF16, [P, 4 * TT]),
        ("ownsel", F32, [P, 16]), ("cid", F32, [P, 4]),
    ]:
        HC[name] = nc.dram_tensor(name, shape, dt, kind="ExternalInput").ap()

    x4_dram = nc.dram_tensor("x4s", [T, D], F32).ap()
    h_dram = nc.dram_tensor("hs", [DHC, P, S], BF16).ap()
    wq1T_dram = nc.dram_tensor("wq1T", [D, DH], BF16).ap()
    wq2T_dram = nc.dram_tensor("wq2T", [DH, D], BF16).ap()
    in_pool = nc.dram_tensor("in_pool", [T, D], BF16).ap()
    pool = nc.dram_tensor("pool", [TTOT, D], BF16).ap()
    e_own = nc.dram_tensor("e_own", [T, 1], BF16).ap()
    e_pool = nc.dram_tensor("e_pool", [TTOT, 1], BF16).ap()
    ye_own = nc.dram_tensor("ye_own", [CAP, D], BF16).ap()
    ddisp = nc.dram_tensor("ddisp", [CAP + 256, 1], I32).ap()
    ye_pool = nc.dram_tensor("ye_pool", [8 * CAP, D], BF16).ap()

    sb = cx.sb
    stream = sb("stream", [P, SC * D], F32)
    xnfm = sb("xnfm", [P, DC * S], BF16)
    kfm = sb("kfm", [P, DC * S], BF16)
    vt = sb("vt", [P, SC * NH * 65], BF16)
    qtile = sb("qtile", [P, DC * TT], BF16)
    avfm = sb("avfm", [P, DC * TT], BF16)
    wring = sb("wring", [P, 4096], BF16)
    ptile = sb("ptile", [P, 2 * 1024], BF16)
    tmp1 = sb("tmp1", [P, 512], F32)
    tmp2 = sb("tmp2", [P, 512], F32)
    tmpi = sb("tmpi", [P, 512], I32)
    tmpb = sb("tmpb", [P, 512], BF16)
    tmpb2 = sb("tmpb2", [P, 512], BF16)
    maskl1_s = sb("maskl1_s", [P, 4 * TT], BF16)
    bcastb = sb("bcastb", [P, TT], BF16)
    onesb = sb("onesb", [P, P], BF16)
    onesf = sb("onesf", [P, P], F32)
    stats = sb("stats", [P, 96], F32)
    ptok = sb("ptok", [P, NSEQ * SC], F32)
    gate = sb("gate", [P, NSEQ * SC], F32)
    sidx = sb("sidx", [P, NSEQ * SC], I32)
    consts = sb("consts", [P, 16], F32)
    idenb = sb("idenb", [P, P], BF16)
    idenf = sb("idenf", [P, P], F32)
    triu1 = sb("triu1", [P, P], BF16)
    ownsel_s = sb("ownsel_s", [P, 16], F32)
    cid_s = sb("cid_s", [P, 4], F32)
    biases = sb("biases", [P, 48], F32)
    convw = sb("convw", [P, 3 * DHC], F32)
    convb = sb("convb", [P, DHC], F32)
    wrs = sb("wrs", [P, DC * E], BF16)
    diag = sb("diag", [P, 3 * P], BF16)
    swrec = sb("swrec", [P, 2], F32)
    sdq = sb("sdq", [P, 4 * SC], F32)
    lnsq = cx.sb_at("lnsq", [P, D], F32, qtile, 0)
    qw = [cx.sb_at(f"qw{i}", [P, 512], F32, vt, 2048 * i) for i in range(4)]
    qt1 = cx.sb_at("qt1", [P, 512], F32, vt, 8192)
    qt2 = cx.sb_at("qt2", [P, 512], F32, vt, 10240)
    qti = cx.sb_at("qti", [P, 512], I32, vt, 12288)
    qb = [cx.sb_at(f"qb{i}", [P, 512], BF16, vt, 14336 + 1024 * i)
          for i in range(2)]
    qc = [cx.sb_at(f"qc{i}", [P, 512], BF16, vt, 16384 + 1024 * i)
          for i in range(2)]
    qstats = cx.sb_at("qstats", [P, 8], F32, vt, 18432)
    qcq = [[cx.sb_at(f"qcq{i}_{g}", [P, 128], BF16, vt, 16384 + 1024 * i + 256 * g)
            for g in range(4)] for i in range(2)]

    iotap = cx.sb_at("iotap", [P, 256], I32, vt, 0)
    scat = cx.sb_at("scat", [P, 256], F32, vt, 1024)
    scat32 = cx.sb_at("scat32", [P, 256], I32, vt, 2048)
    iotaj = cx.sb_at("iotaj", [P, 256], F32, vt, 3072)
    xetmB = cx.sb_at("xetmB", [P, 4 * D], BF16, vt, 4096)
    ystageB = cx.sb_at("ystageB", [P, D], F32, kfm, 8192)
    xt4B = cx.sb_at("xt4B", [P, D], F32, kfm, 12288)
    hexp = cx.sb_at("hexp", [P, DHC * TT], BF16, kfm, 0)
    xefm = cx.sb_at("xefm", [P, DC * TT], BF16, xnfm, 0)
    xetm = cx.sb_at("xetm", [P, 4 * D], BF16, xnfm, 8192)
    Spool = cx.sb_at("Spool", [P, TTOT // P], F32, xnfm, 16384)
    mske = cx.sb_at("mske", [P, 256], F32, xnfm, 17408)
    incl = cx.sb_at("incl", [P, 256], F32, xnfm, 18432)
    rnkp = cx.sb_at("rnkp", [P, 256], F32, xnfm, 19456)
    srcp = cx.sb_at("srcp", [P, 256], F32, xnfm, 20480)
    keepp = cx.sb_at("keepp", [P, 256], F32, xnfm, 21504)
    zz256 = cx.sb_at("zz256", [P, 256], F32, xnfm, 22528)
    ownv = cx.sb_at("ownv", [16, 2 * 256], F32, xnfm, 23552)
    totb = cx.sb_at("totb", [P, E], BF16, xnfm, 30208)
    offs = cx.sb_at("offs", [P, E], F32, xnfm, 30240)
    tote = cx.sb_at("tote", [P, 1], F32, xnfm, 30272)
    D2 = cx.sb_at("D2", [P, CAP // P], I32, xnfm, 30336)

    psS = cx.ps("psS", [P, 1024], F32)
    psS2 = cx.ps("psS2", [P, 1024], F32)
    psAV = cx.ps("psAV", [P, TT], F32)
    psA = cx.ps("psA", [P, TT], F32)
    psB = cx.ps("psB", [P, TT], F32)
    psM = cx.ps("psM", [P, TT], F32)


    hres = [cx.res("hres0"), cx.res("hres1")]
    hload = [cx.res("hload0"), cx.res("hload1")]
    qres = [cx.res("qres1"), cx.res("qres2")]
    x4res = cx.res("x4res")
    SEQ = [0]

    KNOWN_CONSTS = [-C_L0, -C_L1, 0.125, 1e-5, RND, -RND, 127.0, -1.0,
                    1.0 / D, 0.5, -0.5, 2.0]
    for i, v in enumerate(KNOWN_CONSTS):
        nc.const_aps.register(F32, v, consts[:, i:i + 1])

    def dma(qeng, out_ap, in_ap, writes, reads, slow=False, transpose=False,
            pre=(), ch=None):
        if ch is not None:
            qeng = ch
        elif qeng == "sp":
            if transpose:
                qeng = "s1"
            else:
                t = getattr(out_ap, "tensor", None)
                if t is not None and "DRam" in type(t).__name__:
                    qeng = "s2"

        def em(E_):
            return E_.dma_start(out=out_ap, in_=in_ap, transpose=transpose,
                                allow_slow_non_contiguous=slow)
        return cx.op(qeng, em, writes=writes, reads=reads, inc=16, pre=pre)

    def mm(out_ap, lhsT_ap, rhs_ap, writes, reads, start=True, stop=True, inc=1,
           pre=()):
        def em(E_):
            return E_.matmul(out_ap, lhsT_ap, rhs_ap, start=start, stop=stop)
        return cx.op("pe", em, writes=writes, reads=reads, inc=inc, pre=pre)

    def tr_pe(out_ap, in_ap, ident_ap, writes, reads):
        def em(E_):
            return E_.transpose(out=out_ap, in_=in_ap, identity=ident_ap)
        return cx.op("pe", em, writes=writes, reads=reads, inc=1)

    def act(out_ap, in_ap, func, writes, reads, bias=0.0, scale=1.0, accum=None):
        def em(E_):
            return E_.activation(out=out_ap, in_=in_ap, func=func, bias=bias,
                                 scale=scale, accum_out=accum)
        return cx.op("act", em, writes=writes, reads=reads, inc=1)

    def dve(emit, writes, reads, inc=1):
        return cx.op("dve", emit, writes=writes, reads=reads, inc=inc)

    def pop(emit, writes, reads, inc=1):
        return cx.op("pop", emit, writes=writes, reads=reads, inc=inc)

    def pc(emit, writes, reads, inc=1):
        return cx.op("pc", emit, writes=writes, reads=reads, inc=inc)

    def stream_ap(c, w=D):
        return bass.AP(stream, c * D, [[stream.shape[1], P], [1, w]])

    def quant_round(dst_bf, src_ap, sa, sb_plus_rnd, width, post=None,
                    wlist=(), rlist=()):
        dve(lambda E_: E_.tensor_scalar(
            out=tmp2[:, 0:width], in0=src_ap, scalar1=sa, scalar2=sb_plus_rnd,
            op0=ALU.mult, op1=ALU.add),
            writes=[tmp2], reads=list(rlist) + [stats])
        dve(lambda E_: E_.tensor_copy(out=tmpi[:, 0:width], in_=tmp2[:, 0:width]),
            writes=[tmpi], reads=[tmp2])
        dve(lambda E_: E_.scalar_tensor_tensor(
            out=tmp1[:, 0:width], in0=tmpi[:, 0:width], scalar=0.5,
            in1=tmp2[:, 0:width], op0=ALU.add, op1=ALU.is_le),
            writes=[tmp1], reads=[tmpi, tmp2])
        if post is None:
            dve(lambda E_: E_.scalar_tensor_tensor(
                out=dst_bf, in0=tmpi[:, 0:width], scalar=-RND,
                in1=tmp1[:, 0:width], op0=ALU.add, op1=ALU.add),
                writes=list(wlist), reads=[tmpi, tmp1])
        else:
            dve(lambda E_: E_.scalar_tensor_tensor(
                out=tmp2[:, 0:width], in0=tmpi[:, 0:width], scalar=-RND,
                in1=tmp1[:, 0:width], op0=ALU.add, op1=ALU.add),
                writes=[tmp2], reads=[tmpi, tmp1])
            dve(lambda E_: E_.tensor_scalar(
                out=dst_bf, in0=tmp2[:, 0:width], scalar1=post, scalar2=None,
                op0=ALU.mult),
                writes=list(wlist), reads=[tmp2, stats])

    def phase_p0():
        for i, v in enumerate(KNOWN_CONSTS):
            dve(lambda E_, v=v, i=i: E_.memset(consts[:, i:i + 1], v),
                writes=[consts], reads=[])
        dve(lambda E_: E_.memset(onesb[:, :], 1.0), writes=[onesb], reads=[])
        dve(lambda E_: E_.memset(onesf[:, :], 1.0), writes=[onesf], reads=[])
        for name, buf in [("iden_b", idenb), ("iden_f", idenf),
                          ("triu1_b", triu1), ("maskl1_b", maskl1_s),
                          ("ownsel", ownsel_s), ("cid", cid_s)]:
            dma("sp", buf[:, :], HC[name], writes=[buf], reads=[])
        dma("sp", bass.AP(biases, 0, [[biases.shape[1], P], [1, 24]]),
            bass.AP(W["b_qkv"].tensor, 0, [[1, P], [P, 24]]),
            writes=[biases], reads=[], slow=True)
        dma("sp", bass.AP(biases, 24, [[biases.shape[1], P], [1, 8]]),
            bass.AP(W["b_o"].tensor, 0, [[1, P], [P, 8]]),
            writes=[biases], reads=[], slow=True)
        dma("sp", bass.AP(biases, 32, [[biases.shape[1], P], [1, 8]]),
            bass.AP(W["b_proj"].tensor, 0, [[1, P], [P, 8]]),
            writes=[biases], reads=[], slow=True)
        dma("sp", bass.AP(convw, 0, [[convw.shape[1], P], [3, DHC], [1, 3]]),
            bass.AP(W["conv_w"].tensor, 0, [[3, P], [3 * P, DHC], [1, 3]]),
            writes=[convw], reads=[], slow=True)
        dma("sp", bass.AP(convb, 0, [[convb.shape[1], P], [1, DHC]]),
            bass.AP(W["conv_b"].tensor, 0, [[1, P], [P, DHC]]),
            writes=[convb], reads=[], slow=True)
        dma("pop", bass.AP(wrs, 0, [[wrs.shape[1], P], [E, DC], [1, E]]),
            bass.AP(W["Wr"].tensor, 0, [[E, P], [P * E, DC], [1, E]]),
            writes=[wrs], reads=[])
        vt_ones()

    def vt_ones():
        dve(lambda E_: E_.memset(
            bass.AP(vt, 64, [[vt.shape[1], P], [NH * 65, SC], [65, NH]]), 1.0),
            writes=[vt], reads=[])

    def quant_weight(w_ap, rows, cols, wqT_dram, sw_col):
        nstrip = rows // P
        CW = 512
        ncb = cols // CW
        cw = CW
        dve(lambda E_: E_.memset(qstats[:, 0:1], 0.0), writes=[qstats], reads=[])
        it = 0
        for m in range(nstrip):
            for cb in range(ncb):
                qw_ = qw[it % 4]
                it += 1
                dma("sp", qw_[:, 0:cw],
                    bass.AP(w_ap.tensor, (m * P) * cols + cb * cw,
                            [[cols, P], [1, cw]]),
                    writes=[qw_], reads=[])
                dve(lambda E_, qw_=qw_: E_.tensor_scalar(
                    out=qt2[:, 0:cw], in0=qw_[:, 0:cw], scalar1=-1.0,
                    scalar2=None, op0=ALU.mult), writes=[qt2], reads=[qw_])
                dve(lambda E_, qw_=qw_: E_.tensor_tensor(
                    out=qt2[:, 0:cw], in0=qw_[:, 0:cw], in1=qt2[:, 0:cw],
                    op=ALU.max), writes=[qt2], reads=[qw_, qt2])
                dve(lambda E_: E_.reduce_sum(
                    out=qstats[:, 1:2], in_=qt2[:, 0:cw], axis=AX.X),
                    writes=[qstats], reads=[qt2])
                dve(lambda E_: E_.tensor_tensor(
                    out=qstats[:, 0:1], in0=qstats[:, 0:1],
                    in1=qstats[:, 1:2], op=ALU.add), writes=[qstats],
                    reads=[qstats])
        mm(psM[0:1, 0:1], onesf[:, 0:1], qstats[:, 0:1],
           writes=[psM], pre=[psM], reads=[qstats, onesf])
        dve(lambda E_: E_.tensor_scalar(
            out=qstats[0:1, 2:3], in0=psM[0:1, 0:1],
            scalar1=1.0 / (rows * cols), scalar2=1e-5, op0=ALU.mult, op1=ALU.max),
            writes=[qstats], reads=[psM])
        mm(psM[:, 1:2], onesf[0:1, :], qstats[0:1, 2:3],
           writes=[psM], pre=[psM], reads=[qstats, onesf])
        dve(lambda E_, sw_col=sw_col: E_.tensor_copy(
            out=swrec[:, sw_col:sw_col + 1], in_=psM[:, 1:2]),
            writes=[swrec], reads=[psM])
        dve(lambda E_: E_.reciprocal(out=qstats[:, 3:4],
                                     in_=swrec[:, sw_col:sw_col + 1]),
            writes=[qstats], reads=[swrec])
        for m in range(nstrip):
            for cb in range(ncb):
                qw_ = qw[it % 4]
                qb_ = qb[it % 2]
                qcq_ = qcq[it % 2]
                it += 1
                dma("sp", qw_[:, 0:cw],
                    bass.AP(w_ap.tensor, (m * P) * cols + cb * cw,
                            [[cols, P], [1, cw]]),
                    writes=[qw_], reads=[])
                dve(lambda E_, qw_=qw_: E_.tensor_scalar(
                    out=qt2[:, 0:cw], in0=qw_[:, 0:cw],
                    scalar1=qstats[:, 3:4], scalar2=RND,
                    op0=ALU.mult, op1=ALU.add), writes=[qt2],
                    reads=[qw_, qstats])
                dve(lambda E_: E_.tensor_copy(out=qti[:, 0:cw], in_=qt2[:, 0:cw]),
                    writes=[qti], reads=[qt2])
                dve(lambda E_: E_.scalar_tensor_tensor(
                    out=qt1[:, 0:cw], in0=qti[:, 0:cw], scalar=0.5,
                    in1=qt2[:, 0:cw], op0=ALU.add, op1=ALU.is_le),
                    writes=[qt1], reads=[qti, qt2])
                dve(lambda E_: E_.scalar_tensor_tensor(
                    out=qt2[:, 0:cw], in0=qti[:, 0:cw], scalar=-RND,
                    in1=qt1[:, 0:cw], op0=ALU.add, op1=ALU.add),
                    writes=[qt2], reads=[qti, qt1])
                dve(lambda E_, qb_=qb_: E_.tensor_scalar(
                    out=qb_[:, 0:cw], in0=qt2[:, 0:cw], scalar1=1.0,
                    scalar2=-1.0, op0=ALU.min, op1=ALU.max),
                    writes=[qb_], reads=[qt2])
                for g in range(cw // P):
                    dma("sp", qcq_[g][:, :],
                        qb_[:, g * P:(g + 1) * P],
                        writes=[qcq_[g]], reads=[qb_], transpose=True,
                        ch="a1" if g % 2 == 0 else "s1")
                for g in range(cw // P):
                    dma("sp",
                        bass.AP(wqT_dram.tensor, (cb * cw + g * P) * rows + m * P,
                                [[rows, P], [1, P]]),
                        qcq_[g][:, :],
                        writes=[], reads=[qcq_[g], qres[sw_col]])

    def phase_quantw():
        quant_weight(W["ffn_w1"], DH, D, wq1T_dram, 0)
        quant_weight(W["ffn_w2"], D, DH, wq2T_dram, 1)


    wringA = cx.sb_at("wringA", [P, 2048], BF16, wring, 0)
    wringB = cx.sb_at("wringB", [P, 2048], BF16, wring, 4096)
    wsl = (wringA, wringB)
    xnfm_g = [cx.sb_at(f"xnfm_g{g}", [P, S], BF16, xnfm, g * 2 * S)
              for g in range(DC)]
    xefm_g = [cx.sb_at(f"xefm_g{g}", [P, TT], BF16, xnfm, g * 2 * TT)
              for g in range(DC)]
    xstage = cx.sb_at("xstage", [P, D], BF16, avfm, 0)
    xstage2 = cx.sb_at("xstage2", [P, D], BF16, avfm, 2048)
    ostage = cx.sb_at("ostage", [P, TT], BF16, ptile, 0)
    tstage = cx.sb_at("tstage", [P, TT], BF16, ptile, 2048)
    ostage2 = cx.sb_at("ostage2", [P, TT], BF16, ptile, 1024)
    tstage2 = cx.sb_at("tstage2", [P, TT], BF16, ptile, 3072)
    bvrow = cx.sb_at("bvrow", [1, TT], BF16, tmpb, 0)
    bvb = cx.sb_at("bvb", [P, TT], BF16, tmpb2, 0)
    recipb = cx.sb_at("recipb", [1, TT], BF16, tmpb, 0)

    def layer_norm(quant=False, sw_col=0, dq_block=0):
        for c in range(SC):
            xc = stream_ap(c)
            dve(lambda E_, xc=xc: E_.reduce_sum(out=stats[:, 80:81], in_=xc,
                                                axis=AX.X),
                writes=[stats], reads=[stream])
            dve(lambda E_, xc=xc: E_.scalar_tensor_tensor(
                out=lnsq[:, 0:D], in0=xc, scalar=1.0, in1=xc,
                op0=ALU.mult, op1=ALU.mult, accum_out=stats[:, 81:82]),
                writes=[lnsq, stats], reads=[stream])
            if quant:
                dve(lambda E_, xc=xc: E_.reduce_max(out=stats[:, 82:83], in_=xc,
                                                    axis=AX.X),
                    writes=[stats], reads=[stream])
                dve(lambda E_, xc=xc: E_.tensor_reduce(out=stats[:, 83:84], in_=xc,
                                                       axis=AX.X, op=ALU.min),
                    writes=[stats], reads=[stream])
            dve(lambda E_: E_.tensor_scalar(out=stats[:, 84:85], in0=stats[:, 80:81],
                                            scalar1=1.0 / D, scalar2=None,
                                            op0=ALU.mult),
                writes=[stats], reads=[stats])
            dve(lambda E_: E_.tensor_scalar(out=stats[:, 85:86], in0=stats[:, 81:82],
                                            scalar1=1.0 / D, scalar2=None,
                                            op0=ALU.mult),
                writes=[stats], reads=[stats])
            dve(lambda E_: E_.scalar_tensor_tensor(
                out=stats[:, 86:87], in0=stats[:, 84:85], scalar=1.0,
                in1=stats[:, 84:85], op0=ALU.mult, op1=ALU.mult),
                writes=[stats], reads=[stats])
            dve(lambda E_: E_.tensor_tensor(out=stats[:, 86:87], in0=stats[:, 85:86],
                                            in1=stats[:, 86:87], op=ALU.subtract),
                writes=[stats], reads=[stats])
            act(stats[:, 87:88], stats[:, 86:87], AF.Sqrt, bias=1e-5, scale=1.0,
                writes=[stats], reads=[stats])
            dve(lambda E_: E_.reciprocal(out=stats[:, 88:89], in_=stats[:, 87:88]),
                writes=[stats], reads=[stats])
            dve(lambda E_, c=c: E_.tensor_copy(out=stats[:, c:c + 1],
                                               in_=stats[:, 88:89]),
                writes=[stats], reads=[stats])
            dve(lambda E_, c=c: E_.scalar_tensor_tensor(
                out=stats[:, 16 + c:17 + c], in0=stats[:, 84:85], scalar=-1.0,
                in1=stats[:, 88:89], op0=ALU.mult, op1=ALU.mult),
                writes=[stats], reads=[stats])
            if quant:
                dve(lambda E_: E_.tensor_tensor(out=stats[:, 89:90],
                                                in0=stats[:, 82:83],
                                                in1=stats[:, 84:85],
                                                op=ALU.subtract),
                    writes=[stats], reads=[stats])
                dve(lambda E_: E_.tensor_tensor(out=stats[:, 90:91],
                                                in0=stats[:, 84:85],
                                                in1=stats[:, 83:84],
                                                op=ALU.subtract),
                    writes=[stats], reads=[stats])
                dve(lambda E_: E_.tensor_tensor(out=stats[:, 89:90],
                                                in0=stats[:, 89:90],
                                                in1=stats[:, 90:91], op=ALU.max),
                    writes=[stats], reads=[stats])
                dve(lambda E_: E_.tensor_tensor(out=stats[:, 89:90],
                                                in0=stats[:, 89:90],
                                                in1=stats[:, 88:89], op=ALU.mult),
                    writes=[stats], reads=[stats])
                dve(lambda E_: E_.tensor_scalar(out=stats[:, 89:90],
                                                in0=stats[:, 89:90], scalar1=1e-5,
                                                scalar2=None, op0=ALU.max),
                    writes=[stats], reads=[stats])
                dve(lambda E_: E_.reciprocal(out=stats[:, 90:91],
                                             in_=stats[:, 89:90]),
                    writes=[stats], reads=[stats])
                dve(lambda E_: E_.tensor_scalar(out=stats[:, 91:92],
                                                in0=stats[:, 90:91], scalar1=127.0,
                                                scalar2=None, op0=ALU.mult),
                    writes=[stats], reads=[stats])
                dve(lambda E_, c=c: E_.tensor_tensor(
                    out=stats[:, 32 + c:33 + c], in0=stats[:, c:c + 1],
                    in1=stats[:, 91:92], op=ALU.mult),
                    writes=[stats], reads=[stats])
                dve(lambda E_, c=c: E_.scalar_tensor_tensor(
                    out=stats[:, 48 + c:49 + c], in0=stats[:, 16 + c:17 + c],
                    scalar=1.0, in1=stats[:, 91:92], op0=ALU.mult, op1=ALU.mult),
                    writes=[stats], reads=[stats])
                dve(lambda E_, c=c: E_.tensor_scalar(
                    out=stats[:, 48 + c:49 + c], in0=stats[:, 48 + c:49 + c],
                    scalar1=RND, scalar2=None, op0=ALU.add),
                    writes=[stats], reads=[stats])
                dve(lambda E_, c=c: E_.tensor_scalar(
                    out=stats[:, 92:93], in0=stats[:, 89:90], scalar1=1.0 / 127.0,
                    scalar2=None, op0=ALU.mult), writes=[stats], reads=[stats])
                dve(lambda E_, c=c, sw_col=sw_col, dq_block=dq_block:
                    E_.tensor_tensor(
                        out=sdq[:, dq_block * SC + c:dq_block * SC + c + 1],
                        in0=stats[:, 92:93], in1=swrec[:, sw_col:sw_col + 1],
                        op=ALU.mult),
                    writes=[sdq], reads=[stats, swrec])

    DQBLK = [0]

    def normalize_to_fm(quant=False):
        for c in range(SC):
            xs_ = xstage if c % 2 == 0 else xstage2
            xc = stream_ap(c)
            if not quant:
                dve(lambda E_, xc=xc, c=c, xs_=xs_: E_.tensor_scalar(
                    out=xs_[:, :], in0=xc, scalar1=stats[:, c:c + 1],
                    scalar2=stats[:, 16 + c:17 + c], op0=ALU.mult, op1=ALU.add),
                    writes=[xs_], reads=[stream, stats])
            else:
                for half in range(2):
                    sl = slice(half * TT, (half + 1) * TT)
                    xs_ap = bass.AP(stream, c * D + half * TT,
                                    [[stream.shape[1], P], [1, TT]])
                    dve(lambda E_, xs_ap=xs_ap, c=c: E_.tensor_scalar(
                        out=tmp2[:, 0:TT], in0=xs_ap, scalar1=stats[:, 32 + c:33 + c],
                        scalar2=stats[:, 48 + c:49 + c], op0=ALU.mult, op1=ALU.add),
                        writes=[tmp2], reads=[stream, stats])
                    dve(lambda E_: E_.tensor_copy(out=tmpi[:, 0:TT],
                                                  in_=tmp2[:, 0:TT]),
                        writes=[tmpi], reads=[tmp2])
                    dve(lambda E_: E_.scalar_tensor_tensor(
                        out=tmp1[:, 0:TT], in0=tmpi[:, 0:TT], scalar=0.5,
                        in1=tmp2[:, 0:TT], op0=ALU.add, op1=ALU.is_le),
                        writes=[tmp1], reads=[tmpi, tmp2])
                    dve(lambda E_, sl=sl: E_.scalar_tensor_tensor(
                        out=tmp2[:, 0:TT], in0=tmpi[:, 0:TT], scalar=-RND,
                        in1=tmp1[:, 0:TT], op0=ALU.add, op1=ALU.add),
                        writes=[tmp2], reads=[tmpi, tmp1])
                    dve(lambda E_, sl=sl, c=c, xs_=xs_: E_.tensor_scalar(
                        out=xs_[:, sl], in0=tmp2[:, 0:TT],
                        scalar1=sdq[:, DQBLK[0] * SC + c:DQBLK[0] * SC + c + 1],
                        scalar2=None, op0=ALU.mult),
                        writes=[xs_], reads=[tmp2, sdq])
            for g in range(DC):
                dma("sp", bass.AP(xnfm_g[g], c * P, [[S, P], [1, P]]),
                    bass.AP(xs_, g * P, [[xs_.shape[1], P], [1, P]]),
                    writes=[xnfm_g[g]], reads=[xs_], transpose=True,
                    ch="a1" if g % 2 == 0 else "s1")

    def wcast_strip(src_ap, slot, width=1024):
        return dma("pop", slot[:, 0:width], src_ap,
                   writes=[slot], reads=[])

    def proj_strip_ap(wten, wcols, o, nblk=8):
        return bass.AP(wten.tensor, o * P, [[wcols, P], [P * wcols, nblk], [1, P]])

    def attn_common(is_l0):
        pass

    def block1():
        layer_norm(quant=False)
        normalize_to_fm(quant=False)
        vt_ones()
        for o in range(DC):
            w_ = wsl[o % 2]
            wcast_strip(proj_strip_ap(W["W_qkv"], 3 * D, 8 + o), w_)
            for tt in range(NTT):
                ps_ = psA if tt % 2 == 0 else psB
                for g in range(DC):
                    mm(ps_[:, 0:TT], w_[:, g * P:(g + 1) * P],
                       bass.AP(xnfm_g[g], tt * TT, [[S, P], [1, TT]]),
                       writes=[ps_] if g == DC - 1 else [],
                       pre=[ps_] if g == 0 else (),
                       reads=([w_] + xnfm_g) if g == DC - 1 else
                             [w_, xnfm_g[g]],
                       start=(g == 0), stop=(g == DC - 1),
                       inc=1 if g == DC - 1 else 0)
                act(bass.AP(kfm, o * S + tt * TT, [[kfm.shape[1], P], [1, TT]]),
                    ps_[:, 0:TT], AF.Identity, bias=biases[:, 8 + o:9 + o],
                    writes=[kfm], reads=[ps_, biases])
        for vcb in range(2):
            for hh in range(2):
                dma("pop", wsl[hh][:, 0:2048],
                    bass.AP(W["W_qkv"].tensor, 2 * D + vcb * TT + hh * 4 * P * 3 * D,
                            [[3 * D, P], [P * 3 * D, 4], [1, TT]]),
                    writes=[wsl[hh]], reads=[])
            dma("pop", bvrow[0:1, 0:TT],
                bass.AP(W["b_qkv"].tensor, 2 * D + vcb * TT, [[1, 1], [1, TT]]),
                writes=[bvrow], reads=[], slow=True)
            mm(psM[0:P, 0:TT], onesb[0:1, 0:P], bvrow[0:1, 0:TT],
               writes=[psM], reads=[bvrow, onesb])
            act(bvb[:, 0:TT], psM[:, 0:TT], AF.Identity, writes=[bvb], reads=[psM])
            for tc in range(SC):
                ps_ = psA if tc % 2 == 0 else psB
                for g in range(DC):
                    mm(ps_[:, 0:TT],
                       bass.AP(xnfm_g[g], tc * P, [[S, P], [1, P]]),
                       wsl[g // 4][:, (g % 4) * TT:(g % 4 + 1) * TT],
                       writes=[ps_] if g == DC - 1 else [],
                       pre=[ps_] if g == 0 else (),
                       reads=([wsl[0], wsl[1]] + xnfm_g) if g == DC - 1 else
                             [wsl[g // 4], xnfm_g[g]],
                       start=(g == 0), stop=(g == DC - 1),
                       inc=1 if g == DC - 1 else 0)
                dve(lambda E_, tc=tc, vcb=vcb, ps_=ps_: E_.tensor_tensor(
                    out=bass.AP(vt, tc * (NH * 65) + vcb * 8 * 65,
                                [[vt.shape[1], P], [65, 8], [1, 64]]),
                    in0=ps_[:, 0:TT], in1=bvb[:, 0:TT], op=ALU.add),
                    writes=[vt], reads=[ps_, bvb])
        for qt in range(NTT):
            for o in range(DC):
                w_ = wsl[o % 2]
                wcast_strip(proj_strip_ap(W["W_qkv"], 3 * D, o), w_)
                ps_ = psA if o % 2 == 0 else psB
                for g in range(DC):
                    mm(ps_[:, 0:TT], w_[:, g * P:(g + 1) * P],
                       bass.AP(xnfm_g[g], qt * TT, [[S, P], [1, TT]]),
                       writes=[ps_] if g == DC - 1 else [],
                       pre=[ps_] if g == 0 else (),
                       reads=([w_] + xnfm_g) if g == DC - 1 else
                             [w_, xnfm_g[g]],
                       start=(g == 0), stop=(g == DC - 1),
                       inc=1 if g == DC - 1 else 0)
                act(qtile[:, o * TT:(o + 1) * TT], ps_[:, 0:TT], AF.Identity,
                    bias=biases[:, o:o + 1], writes=[qtile], reads=[ps_, biases])
            flash_heads(qt, causal=False)
            out_projection(qt, W["W_o"], D, 24, add_dq=None)

    def flash_heads(qt, causal):
        nkc = SC if not causal else 4 * qt + 4
        CC = C_L0 if not causal else C_L1
        npairs = nkc // 2
        pctr = [0]

        def emit_av(h, kp, pt):
            for sub in range(2):
                kc = 2 * kp + sub
                lhsv = bass.AP(vt, kc * (NH * 65) + h * 65,
                               [[vt.shape[1], P], [1, 65]])
                mm(psAV[0:65, 0:TT], lhsv, pt[:, sub * TT:(sub + 1) * TT],
                   writes=[psAV] if kc == nkc - 1 else [],
                   pre=[psAV] if kc == 0 else (),
                   reads=[vt, pt],
                   start=(kc == 0), stop=(kc == nkc - 1),
                   inc=1 if kc == nkc - 1 else 0)

        def emit_norm(h):
            def _em_recip(E_):
                with nc.allow_low_precision(reason="softmax denom recip bf16"):
                    return E_.reciprocal(out=recipb[0:1, 0:TT],
                                         in_=tmp1[64:65, 0:TT])
            dve(_em_recip, writes=[recipb], reads=[tmp1])
            mm(psM[0:64, 0:TT], onesb[0:1, 0:64], recipb[0:1, 0:TT],
               writes=[psM], pre=[psM], reads=[recipb, onesb])
            act(bvb[0:64, 0:TT], psM[0:64, 0:TT], AF.Identity, writes=[bvb],
                reads=[psM])
            dve(lambda E_, h=h: E_.tensor_tensor(
                out=bass.AP(avfm, (64 * (h % 2)) * avfm.shape[1] + (h // 2) * TT,
                            [[avfm.shape[1], 64], [1, TT]]),
                in0=tmp1[0:64, 0:TT], in1=bvb[0:64, 0:TT], op=ALU.mult),
                writes=[avfm], reads=[tmp1, bvb])

        prev_head = None
        for h in range(NH):
            pend = None
            norm_at = min(3, npairs - 1)
            for kp in range(npairs):
                psX = psS if pctr[0] % 2 == 0 else psS2
                pctr[0] += 1
                for sub in range(2):
                    kc = 2 * kp + sub
                    lhs = bass.AP(kfm, (64 * (h % 2)) * kfm.shape[1]
                                  + (h // 2) * S + kc * P,
                                  [[kfm.shape[1], 64], [1, P]])
                    rhs = bass.AP(qtile, (64 * (h % 2)) * qtile.shape[1]
                                  + (h // 2) * TT,
                                  [[qtile.shape[1], 64], [1, TT]])
                    mm(psX[:, sub * TT:(sub + 1) * TT], lhs, rhs,
                       writes=[psX] if sub == 1 else [],
                       pre=[psX] if sub == 0 else (),
                       reads=[kfm, qtile],
                       start=True, stop=True, inc=1 if sub == 1 else 0)
                if pend is not None:
                    emit_av(h, *pend)
                if kp == norm_at and prev_head is not None:
                    emit_norm(prev_head)
                    prev_head = None
                pt = ptile if kp % 2 == 0 else ptile2
                act(pt[:, 0:1024], psX[:, 0:1024], AF.Exp, bias=-CC, scale=0.125,
                    writes=[pt], reads=[psX])
                if causal and 2 * kp + 1 >= 4 * qt:
                    for sub in range(2):
                        kc = 2 * kp + sub
                        i = kc - 4 * qt
                        if i >= 0:
                            dve(lambda E_, pt=pt, sub=sub, i=i: E_.tensor_tensor(
                                out=pt[:, sub * TT:(sub + 1) * TT],
                                in0=pt[:, sub * TT:(sub + 1) * TT],
                                in1=maskl1_s[:, i * TT:(i + 1) * TT], op=ALU.mult),
                                writes=[pt], reads=[pt, maskl1_s])
                pend = (kp, pt)
            emit_av(h, *pend)
            if prev_head is not None:
                emit_norm(prev_head)
            act(tmp1[0:65, 0:TT], psAV[0:65, 0:TT], AF.Identity,
                writes=[tmp1], reads=[psAV])
            prev_head = h
        emit_norm(prev_head)

    ptile2 = cx.sb_at("ptile2", [P, 1024], BF16, ptile, 2048)

    def out_projection(qt, wten, wcols, bias_col, add_dq):
        for o in range(DC):
            w_ = wsl[o % 2]
            wcast_strip(proj_strip_ap(wten, wcols, o), w_)
            ps_ = psA if o % 2 == 0 else psB
            for g in range(DC):
                mm(ps_[:, 0:TT], w_[:, g * P:(g + 1) * P],
                   avfm[:, g * TT:(g + 1) * TT],
                   writes=[ps_] if g == DC - 1 else [],
                   pre=[ps_] if g == 0 else (),
                   reads=[w_, avfm], start=(g == 0), stop=(g == DC - 1),
                   inc=1 if g == DC - 1 else 0)
            os_ = ostage if o % 2 == 0 else ostage2
            ts_ = tstage if o % 2 == 0 else tstage2
            act(os_[:, 0:TT], ps_[:, 0:TT], AF.Identity,
                bias=biases[:, bias_col + o:bias_col + o + 1],
                writes=[os_], reads=[ps_, biases])
            for j in range(4):
                dma("sp", ts_[:, j * P:(j + 1) * P],
                    os_[:, j * P:(j + 1) * P],
                    writes=[ts_], reads=[os_], transpose=True,
                    ch="a1" if j % 2 == 0 else "s1")
            dve(lambda E_, qt=qt, o=o, ts_=ts_: E_.tensor_tensor(
                out=bass.AP(stream, (qt * 4) * D + o * P,
                            [[stream.shape[1], P], [D, 4], [1, P]]),
                in0=bass.AP(stream, (qt * 4) * D + o * P,
                            [[stream.shape[1], P], [D, 4], [1, P]]),
                in1=ts_[:, 0:TT],
                op=ALU.add),
                writes=[stream], reads=[stream, ts_])


    hraw = cx.sb_at("hraw", [P, S + 4], BF16, vt, 0)
    hM = cx.sb_at("hM", [P, S], BF16, kfm, 0)
    hm = cx.sb_at("hm", [P, S], BF16, kfm, 4096)
    habs = cx.sb_at("habs", [P, S], BF16, kfm, 8192)
    hsb = cx.sb_at("hsb", [P, TT], BF16, vt, 4128)
    s2sb = cx.sb_at("s2sb", [16, 256], BF16, vt, 5152)
    s2row = cx.sb_at("s2row", [1, 4096], BF16, vt, 5664)
    s2bc = cx.sb_at("s2bc", [P, TT], F32, vt, 13856)
    dq2bc = cx.sb_at("dq2bc", [P, TT], F32, vt, 15904)
    hqA = cx.sb_at("hqA", [P, TT], BF16, vt, 17952)
    hqB = cx.sb_at("hqB", [P, TT], BF16, vt, 18976)
    xq2fm = cx.sb_at("xq2fm", [P, DHC * TT], BF16, kfm, 0)
    io8 = cx.sb_at("io8", [P, 8], F32, diag, 0)
    rev8 = cx.sb_at("rev8", [P, 8], F32, diag, 32)
    eidst = cx.sb_at("eidst", [P, SC], BF16, diag, 64)
    negmax = cx.sb_at("negmax", [P, 1], F32, diag, 96)
    ystage = cx.sb_at("ystage", [P, D], F32, kfm, 0)
    xt4 = cx.sb_at("xt4", [P, D], F32, kfm, 4096)
    yefm = cx.sb_at("yefm", [P, DC * TT], BF16, avfm, 0)

    def block2():
        layer_norm(quant=True, sw_col=0, dq_block=0)
        DQBLK[0] = 0
        normalize_to_fm(quant=True)
        if DBG_B2:
            xq1s = nc.dram_tensor("xq1s", [P, DC * S], BF16,
                                  kind="ExternalOutput").ap()
            hpre = nc.dram_tensor("hpre", [DHC, P, S], BF16,
                                  kind="ExternalOutput").ap()
            hcv = nc.dram_tensor("hcv", [DHC, P, S], BF16,
                                 kind="ExternalOutput").ap()
            hrawd = nc.dram_tensor("hrawd", [DHC, P, S + 4], BF16,
                                   kind="ExternalOutput").ap()
            diagd = nc.dram_tensor("diagd", [DHC, P, 384], BF16,
                                   kind="ExternalOutput").ap()
            dma("sp", xq1s, xnfm[:, :], writes=[], reads=[xnfm])
        else:
            hpre = hcv = hrawd = diagd = None
        first = [True]
        for m in range(DHC):
            w_ = wsl[m % 2]
            dma("sp", w_[:, 0:1024],
                bass.AP(wq1T_dram.tensor, m * P, [[DH, P], [P * DH, DC], [1, P]]),
                writes=[w_], reads=[], pre=[qres[0]])
            for j in range(3):
                dve(lambda E_, m=m, j=j: E_.tensor_scalar(
                    out=diag[:, j * P:(j + 1) * P],
                    in0=idenb[:, :], scalar1=convw[:, 3 * m + j:3 * m + j + 1],
                    scalar2=None, op0=ALU.mult),
                    writes=[diag], reads=[idenb, convw])
            dve(lambda E_: E_.memset(hraw[:, 0:1], 0.0), writes=[hraw], reads=[])
            dve(lambda E_: E_.memset(hraw[:, S + 1:S + 4], 0.0),
                writes=[hraw], reads=[])
            for tt in range(NTT):
                ps_ = psA if tt % 2 == 0 else psB
                for g in range(DC):
                    mm(ps_[:, 0:TT], w_[:, g * P:(g + 1) * P],
                       bass.AP(xnfm_g[g], tt * TT, [[S, P], [1, TT]]),
                       writes=[ps_] if g == DC - 1 else [],
                       pre=[ps_] if g == 0 else (),
                       reads=([w_] + xnfm_g) if g == DC - 1 else
                             [w_, xnfm_g[g]],
                       start=(g == 0), stop=(g == DC - 1),
                       inc=1 if g == DC - 1 else 0)
                act(hraw[:, 1 + tt * TT:1 + (tt + 1) * TT], ps_[:, 0:TT],
                    AF.Identity, writes=[hraw], reads=[ps_])
                if DBG_B2:
                    dma("sp", bass.AP(hpre.tensor, (m * P) * S + tt * TT,
                                      [[S, P], [1, TT]]),
                        hraw[:, 1 + tt * TT:1 + (tt + 1) * TT],
                        writes=[], reads=[hraw])
            if DBG_B2:
                dma("sp", bass.AP(hrawd.tensor, (m * P) * (S + 4),
                                  [[S + 4, P], [1, S + 4]]),
                    hraw[:, :], writes=[], reads=[hraw])
            if DBG_B2:
                cx.fence_all()
                dma("sp", bass.AP(diagd.tensor, (m * P) * 384,
                                  [[384, P], [1, 384]]),
                    diag[:, :], writes=[], reads=[diag])
            for tt in range(NTT):
                ps_ = psS if tt % 2 == 0 else psS2
                for j in range(3):
                    mm(ps_[:, 0:TT], diag[:, j * P:(j + 1) * P],
                       hraw[:, tt * TT + j:tt * TT + j + TT],
                       writes=[ps_] if j == 2 else [],
                       pre=[ps_] if j == 0 else (),
                       reads=[diag, hraw], start=(j == 0), stop=(j == 2),
                       inc=1 if j == 2 else 0)
                if DBG_B2:
                    act(tmpb2[:, 0:TT], ps_[:, 0:TT], AF.Identity,
                        writes=[tmpb2], reads=[ps_])
                    dma("sp", bass.AP(hcv.tensor, (m * P) * S + tt * TT,
                                      [[S, P], [1, TT]]),
                        tmpb2[:, 0:TT], writes=[], reads=[tmpb2])
                act(hsb[:, 0:TT], ps_[:, 0:TT], AF.Silu,
                    bias=convb[:, m:m + 1], writes=[hsb], reads=[ps_, convb])
                dma("sp", bass.AP(h_dram.tensor, (m * P) * S + tt * TT,
                                  [[S, P], [1, TT]]),
                    hsb[:, 0:TT], writes=[], reads=[hsb, hres[SEQ[0]]],
                    pre=[hload[1 - SEQ[0]]])
                if first[0]:
                    dve(lambda E_, tt=tt: E_.tensor_copy(
                        out=hM[:, tt * TT:(tt + 1) * TT], in_=hsb[:, 0:TT]),
                        writes=[hM], reads=[hsb])
                    dve(lambda E_, tt=tt: E_.tensor_copy(
                        out=hm[:, tt * TT:(tt + 1) * TT], in_=hsb[:, 0:TT]),
                        writes=[hM], reads=[hsb])
                else:
                    dve(lambda E_, tt=tt: E_.tensor_tensor(
                        out=hM[:, tt * TT:(tt + 1) * TT],
                        in0=hM[:, tt * TT:(tt + 1) * TT], in1=hsb[:, 0:TT],
                        op=ALU.max), writes=[hM], reads=[hsb])
                    dve(lambda E_, tt=tt: E_.tensor_tensor(
                        out=hm[:, tt * TT:(tt + 1) * TT],
                        in0=hm[:, tt * TT:(tt + 1) * TT], in1=hsb[:, 0:TT],
                        op=ALU.min), writes=[hM], reads=[hsb])
            first[0] = False
        dve(lambda E_: E_.tensor_scalar(out=habs[:, :], in0=hm[:, 0:S],
                                        scalar1=-1.0, scalar2=None, op0=ALU.mult),
            writes=[hM], reads=[hM])
        dve(lambda E_: E_.tensor_tensor(out=habs[:, :], in0=habs[:, :],
                                        in1=hM[:, 0:S], op=ALU.max),
            writes=[hM], reads=[hM])
        for k in range(SC):
            tb_ = tmpb if k % 2 == 0 else tmpb2
            dma("sp", tb_[:, 0:P], habs[:, k * P:(k + 1) * P],
                writes=[tb_], reads=[hM], transpose=True,
                ch="a1" if k % 2 == 0 else "s1")
            dve(lambda E_, k=k, tb_=tb_: E_.reduce_max(
                out=sdq[:, 16 + k:17 + k],
                in_=tb_[:, 0:P], axis=AX.X),
                writes=[sdq], reads=[tb_])
        for k in range(SC):
            dve(lambda E_, k=k: E_.tensor_scalar(
                out=sdq[:, 16 + k:17 + k], in0=sdq[:, 16 + k:17 + k],
                scalar1=1e-5, scalar2=None, op0=ALU.max),
                writes=[sdq], reads=[sdq])
            dve(lambda E_, k=k: E_.tensor_scalar(
                out=sdq[:, 32 + k:33 + k], in0=sdq[:, 16 + k:17 + k],
                scalar1=1.0 / 127.0, scalar2=None, op0=ALU.mult),
                writes=[sdq], reads=[sdq])
            dve(lambda E_, k=k: E_.tensor_tensor(
                out=sdq[:, 32 + k:33 + k], in0=sdq[:, 32 + k:33 + k],
                in1=swrec[:, 1:2], op=ALU.mult), writes=[sdq], reads=[sdq, swrec])
            dve(lambda E_, k=k: E_.reciprocal(
                out=sdq[:, 16 + k:17 + k], in_=sdq[:, 16 + k:17 + k]),
                writes=[sdq], reads=[sdq])
            dve(lambda E_, k=k: E_.tensor_scalar(
                out=sdq[:, 16 + k:17 + k], in0=sdq[:, 16 + k:17 + k],
                scalar1=127.0, scalar2=None, op0=ALU.mult),
                writes=[sdq], reads=[sdq])
        tr_pe(psM[0:16, 0:P], sdq[:, 16:32], idenf[:, :],
              writes=[psM], reads=[sdq, idenf])
        act(s2sb[0:16, 0:P], psM[0:16, 0:P], AF.Identity,
            writes=[s2sb], reads=[psM])
        tr_pe(psM[0:16, 0:P], sdq[:, 32:48], idenf[:, :],
              writes=[psM], reads=[sdq, idenf])
        act(s2sb[0:16, P:2 * P], psM[0:16, 0:P], AF.Identity,
            writes=[s2sb], reads=[psM])
        dma("sp", bass.AP(s2row, 0, [[4096, 1], [128, 16], [1, P]]),
            bass.AP(s2sb, 0, [[256, 16], [1, P]]),
            writes=[s2row], reads=[s2sb], slow=True)
        dma("sp", bass.AP(s2row, 2048, [[4096, 1], [128, 16], [1, P]]),
            bass.AP(s2sb, P, [[256, 16], [1, P]]),
            writes=[s2row], reads=[s2sb], slow=True)
        for tt in range(NTT):
            for j4 in range(4):
                k = 4 * tt + j4
                mm(psS[:, j4 * P:(j4 + 1) * P], onesb[0:1, 0:P],
                   s2row[0:1, k * P:(k + 1) * P],
                   writes=[psS] if j4 == 3 else [], pre=[psS] if j4 == 0 else (),
                   reads=[onesb, s2row], inc=1 if j4 == 3 else 0)
                mm(psS[:, TT + j4 * P:TT + (j4 + 1) * P], onesb[0:1, 0:P],
                   s2row[0:1, 2048 + k * P:2048 + (k + 1) * P],
                   writes=[psS] if j4 == 3 else [], reads=[onesb, s2row],
                   inc=1 if j4 == 3 else 0)
            act(s2bc[:, 0:TT], psS[:, 0:TT], AF.Identity,
                writes=[s2bc], reads=[psS])
            act(dq2bc[:, 0:TT], psS[:, TT:2 * TT], AF.Identity,
                writes=[dq2bc], reads=[psS])
            for m in range(DHC):
                hq = hqA if m % 2 == 0 else hqB
                dma("sp", hq[:, 0:TT],
                    bass.AP(h_dram.tensor, (m * P) * S + tt * TT, [[S, P], [1, TT]]),
                    writes=[hq], reads=[hload[SEQ[0]]], pre=[hres[SEQ[0]]])
                dve(lambda E_, hq=hq: E_.tensor_tensor(
                    out=tmp2[:, 0:TT], in0=hq[:, 0:TT], in1=s2bc[:, 0:TT],
                    op=ALU.mult), writes=[tmp2], reads=[hq, s2bc])
                dve(lambda E_: E_.tensor_scalar(
                    out=tmp2[:, 0:TT], in0=tmp2[:, 0:TT], scalar1=RND,
                    scalar2=None, op0=ALU.add), writes=[tmp2], reads=[tmp2])
                dve(lambda E_: E_.tensor_copy(out=tmpi[:, 0:TT],
                                              in_=tmp2[:, 0:TT]),
                    writes=[tmpi], reads=[tmp2])
                dve(lambda E_: E_.scalar_tensor_tensor(
                    out=tmp1[:, 0:TT], in0=tmpi[:, 0:TT], scalar=0.5,
                    in1=tmp2[:, 0:TT], op0=ALU.add, op1=ALU.is_le),
                    writes=[tmp1], reads=[tmpi, tmp2])
                dve(lambda E_: E_.scalar_tensor_tensor(
                    out=tmp2[:, 0:TT], in0=tmpi[:, 0:TT], scalar=-RND,
                    in1=tmp1[:, 0:TT], op0=ALU.add, op1=ALU.add),
                    writes=[tmp2], reads=[tmpi, tmp1])
                dve(lambda E_, m=m: E_.tensor_tensor(
                    out=bass.AP(kfm, m * TT, [[kfm.shape[1], P], [1, TT]]),
                    in0=tmp2[:, 0:TT], in1=dq2bc[:, 0:TT], op=ALU.mult),
                    writes=[xq2fm], reads=[tmp2, dq2bc])
            for o in range(DC):
                for hh in range(2):
                    dma("sp", wsl[hh][:, 0:2048],
                        bass.AP(wq2T_dram.tensor, o * P + hh * 16 * P * D,
                                [[D, P], [P * D, 16], [1, P]]),
                        writes=[wsl[hh]], reads=[], pre=[qres[1]])
                ps_ = psA if o % 2 == 0 else psB
                for g in range(DHC):
                    mm(ps_[:, 0:TT], wsl[g // 16][:, (g % 16) * P:(g % 16 + 1) * P],
                       bass.AP(kfm, g * TT, [[kfm.shape[1], P], [1, TT]]),
                       writes=[ps_] if g == DHC - 1 else [],
                       pre=[ps_] if g == 0 else (),
                       reads=[wsl[0], wsl[1], xq2fm] if g == DHC - 1 else
                             [wsl[g // 16], xq2fm],
                       start=(g == 0), stop=(g == DHC - 1),
                       inc=1 if g == DHC - 1 else 0)
                os_ = ostage if o % 2 == 0 else ostage2
                ts_ = tstage if o % 2 == 0 else tstage2
                act(os_[:, 0:TT], ps_[:, 0:TT], AF.Identity,
                    writes=[os_], reads=[ps_])
                for j in range(4):
                    dma("sp", ts_[:, j * P:(j + 1) * P],
                        os_[:, j * P:(j + 1) * P],
                        writes=[ts_], reads=[os_], transpose=True,
                        ch="a1" if j % 2 == 0 else "s1")
                dve(lambda E_, tt=tt, o=o, ts_=ts_: E_.tensor_tensor(
                    out=bass.AP(stream, (tt * 4) * D + o * P,
                                [[stream.shape[1], P], [D, 4], [1, P]]),
                    in0=bass.AP(stream, (tt * 4) * D + o * P,
                                [[stream.shape[1], P], [D, 4], [1, P]]),
                    in1=ts_[:, 0:TT], op=ALU.add),
                    writes=[stream], reads=[stream, ts_])

    latbuf = cx.sb_at("latbuf", [P, 2 * S], BF16, qtile, 0)
    latbufB = cx.sb_at("latbufB", [P, 2 * S], BF16, avfm, 0)
    latq = cx.sb_at("latq", [P, 2 * TT], BF16, avfm, 0)

    def lat_proj(wname, dst, dst_off_fn):
        for lm in range(2):
            w_ = wsl[lm % 2]
            dma("pop", w_[:, 0:1024],
                bass.AP(W[wname].tensor, lm * P, [[LAT, P], [P * LAT, DC], [1, P]]),
                writes=[w_], reads=[])
            for tt in range(NTT):
                ps_ = psA if tt % 2 == 0 else psB
                for g in range(DC):
                    mm(ps_[:, 0:TT], w_[:, g * P:(g + 1) * P],
                       bass.AP(xnfm_g[g], tt * TT, [[S, P], [1, TT]]),
                       writes=[ps_] if g == DC - 1 else [],
                       pre=[ps_] if g == 0 else (),
                       reads=([w_] + xnfm_g) if g == DC - 1 else
                             [w_, xnfm_g[g]],
                       start=(g == 0), stop=(g == DC - 1),
                       inc=1 if g == DC - 1 else 0)
                act(dst_off_fn(lm, tt), ps_[:, 0:TT], AF.Identity,
                    writes=[dst], reads=[ps_])

    def block3():
        layer_norm(quant=False)
        normalize_to_fm(quant=False)
        vt_ones()
        lat_proj("wk_lat", latbuf,
                 lambda lm, tt: latbuf[:, lm * S + tt * TT:lm * S + (tt + 1) * TT])
        for o in range(DC):
            w_ = wsl[o % 2]
            dma("pop", w_[:, 0:256],
                bass.AP(W["wk_heads"].tensor, o * P, [[D, P], [P * D, 2], [1, P]]),
                writes=[w_], reads=[])
            for tt in range(NTT):
                ps_ = psA if tt % 2 == 0 else psB
                for lg in range(2):
                    mm(ps_[:, 0:TT], w_[:, lg * P:(lg + 1) * P],
                       bass.AP(latbuf, lg * S + tt * TT, [[latbuf.shape[1], P], [1, TT]]),
                       writes=[ps_] if lg == 1 else [],
                       pre=[ps_] if lg == 0 else (),
                       reads=[w_, latbuf], start=(lg == 0), stop=(lg == 1),
                       inc=1 if lg == 1 else 0)
                act(bass.AP(kfm, o * S + tt * TT, [[kfm.shape[1], P], [1, TT]]),
                    ps_[:, 0:TT], AF.Identity, writes=[kfm], reads=[ps_])
        lat_proj("wv_lat", latbufB,
                 lambda lm, tt: latbufB[:, lm * S + tt * TT:lm * S + (tt + 1) * TT])
        for vcb in range(2):
            w_ = wsl[vcb % 2]
            dma("pop", w_[:, 0:1024],
                bass.AP(W["wv_heads"].tensor, vcb * TT,
                        [[D, P], [P * D, 2], [1, TT]]),
                writes=[w_], reads=[])
            for tc in range(SC):
                ps_ = psA if tc % 2 == 0 else psB
                for lg in range(2):
                    mm(ps_[:, 0:TT],
                       bass.AP(latbufB, lg * S + tc * P,
                               [[latbufB.shape[1], P], [1, P]]),
                       w_[:, lg * TT:(lg + 1) * TT],
                       writes=[ps_] if lg == 1 else [],
                       pre=[ps_] if lg == 0 else (),
                       reads=[w_, latbufB], start=(lg == 0), stop=(lg == 1),
                       inc=1 if lg == 1 else 0)
                dve(lambda E_, tc=tc, vcb=vcb, ps_=ps_: E_.tensor_copy(
                    out=bass.AP(vt, tc * (NH * 65) + vcb * 8 * 65,
                                [[vt.shape[1], P], [65, 8], [1, 64]]),
                    in_=ps_[:, 0:TT]),
                    writes=[vt], reads=[ps_])
        for qt in range(NTT):
            for lm in range(2):
                w_ = wsl[lm % 2]
                dma("pop", w_[:, 0:1024],
                    bass.AP(W["wq_lat"].tensor, lm * P,
                            [[LAT, P], [P * LAT, DC], [1, P]]),
                    writes=[w_], reads=[])
                ps_ = psA if lm % 2 == 0 else psB
                for g in range(DC):
                    mm(ps_[:, 0:TT], w_[:, g * P:(g + 1) * P],
                       bass.AP(xnfm_g[g], qt * TT, [[S, P], [1, TT]]),
                       writes=[ps_] if g == DC - 1 else [],
                       pre=[ps_] if g == 0 else (),
                       reads=([w_] + xnfm_g) if g == DC - 1 else
                             [w_, xnfm_g[g]],
                       start=(g == 0), stop=(g == DC - 1),
                       inc=1 if g == DC - 1 else 0)
                act(latq[:, lm * TT:(lm + 1) * TT], ps_[:, 0:TT], AF.Identity,
                    writes=[latq], reads=[ps_])
            for o in range(DC):
                w_ = wsl[o % 2]
                dma("pop", w_[:, 0:256],
                    bass.AP(W["wq_heads"].tensor, o * P,
                            [[D, P], [P * D, 2], [1, P]]),
                    writes=[w_], reads=[])
                ps_ = psA if o % 2 == 0 else psB
                for lg in range(2):
                    mm(ps_[:, 0:TT], w_[:, lg * P:(lg + 1) * P],
                       latq[:, lg * TT:(lg + 1) * TT],
                       writes=[ps_] if lg == 1 else [],
                       pre=[ps_] if lg == 0 else (),
                       reads=[w_, latq], start=(lg == 0), stop=(lg == 1),
                       inc=1 if lg == 1 else 0)
                act(qtile[:, o * TT:(o + 1) * TT], ps_[:, 0:TT], AF.Identity,
                    writes=[qtile], reads=[ps_])
            flash_heads(qt, causal=True)
            out_projection(qt, W["w_proj"], D, 32, add_dq=None)

    def route_and_spill(s):
        layer_norm(quant=False)
        pc(lambda E_: E_.iota(io8[:, :], [[1, 8]], base=0, channel_multiplier=0,
                              allow_small_or_imprecise_dtypes=True),
           writes=[diag], reads=[])
        dve(lambda E_: E_.tensor_scalar(out=rev8[:, :], in0=io8[:, :],
                                        scalar1=-1.0, scalar2=8.0,
                                        op0=ALU.mult, op1=ALU.add),
            writes=[diag], reads=[diag])
        for c in range(SC):
            xs_ = xstage if c % 2 == 0 else xstage2
            xc = stream_ap(c)
            dve(lambda E_, xc=xc, c=c, xs_=xs_: E_.tensor_scalar(
                out=xs_[:, :], in0=xc, scalar1=stats[:, c:c + 1],
                scalar2=stats[:, 16 + c:17 + c], op0=ALU.mult, op1=ALU.add),
                writes=[xs_], reads=[stream, stats])
            dma("sp", bass.AP(in_pool.tensor, (s * S + c * P) * D,
                              [[D, P], [1, D]]),
                xs_[:, :], writes=[], reads=[xs_])
            for g in range(DC):
                dma("sp", bass.AP(xnfm_g[g], c * P, [[S, P], [1, P]]),
                    bass.AP(xs_, g * P, [[xs_.shape[1], P], [1, P]]),
                    writes=[xnfm_g[g]], reads=[xs_], transpose=True,
                    ch="a1" if g % 2 == 0 else "s1")
        for c in range(SC):
            for g in range(DC):
                mm(psM[:, 0:8],
                   bass.AP(xnfm_g[g], c * P, [[S, P], [1, P]]),
                   wrs[:, g * 8:(g + 1) * 8],
                   writes=[psM] if g == DC - 1 else [],
                   pre=[psM] if g == 0 else (),
                   reads=(xnfm_g + [wrs]) if g == DC - 1 else
                         [xnfm_g[g], wrs],
                   start=(g == 0), stop=(g == DC - 1),
                   inc=1 if g == DC - 1 else 0)
            dve(lambda E_: E_.reduce_max(out=stats[:, 64:65], in_=psM[:, 0:8],
                                         axis=AX.X),
                writes=[stats], reads=[psM])
            dve(lambda E_: E_.tensor_scalar(out=negmax[:, 0:1], in0=stats[:, 64:65],
                                            scalar1=-1.0, scalar2=None,
                                            op0=ALU.mult),
                writes=[diag], reads=[stats])
            act(tmp2[:, 0:8], psM[:, 0:8], AF.Exp, bias=negmax[:, 0:1],
                accum=stats[:, 65:66], writes=[tmp2, stats], reads=[psM, diag])
            dve(lambda E_, s=s, c=c: E_.reciprocal(
                out=ptok[:, s * SC + c:s * SC + c + 1], in_=stats[:, 65:66]),
                writes=[ptok], reads=[stats])
            dve(lambda E_: E_.tensor_scalar(out=tmp1[:, 0:8], in0=psM[:, 0:8],
                                            scalar1=stats[:, 64:65], scalar2=None,
                                            op0=ALU.is_equal),
                writes=[tmp1], reads=[psM, stats])
            dve(lambda E_: E_.tensor_tensor(out=tmp1[:, 0:8], in0=tmp1[:, 0:8],
                                            in1=rev8[:, :], op=ALU.mult),
                writes=[tmp1], reads=[tmp1, diag])
            dve(lambda E_: E_.reduce_max(out=stats[:, 66:67], in_=tmp1[:, 0:8],
                                         axis=AX.X),
                writes=[stats], reads=[tmp1])
            dve(lambda E_, c=c: E_.tensor_scalar(
                out=eidst[:, c:c + 1], in0=stats[:, 66:67], scalar1=-1.0,
                scalar2=8.0, op0=ALU.mult, op1=ALU.add),
                writes=[diag], reads=[stats])
        dma("sp", bass.AP(e_own.tensor, s * S, [[1, P], [P, SC]]),
            eidst[:, 0:SC], writes=[], reads=[diag], slow=True)
        dma("sp", bass.AP(x4_dram.tensor, (s * S) * D, [[D, P], [P * D, SC], [1, D]]),
            bass.AP(stream, 0, [[stream.shape[1], P], [D, SC], [1, D]]),
            writes=[], reads=[stream, x4res])

    def moe():
        cx.fence_all()
        eres = cx.res("epool_res")
        pres = cx.res("pool_res")
        pc(lambda E_: E_.collective_compute(
            "AllGather", ALU.bypass, replica_groups=[list(range(8))],
            ins=[e_own], outs=[e_pool]), writes=[eres], reads=[], inc=1)
        cx.op("pc2", lambda E_: E_.collective_compute(
            "AllGather", ALU.bypass, replica_groups=[list(range(8))],
            ins=[in_pool], outs=[pool]), writes=[pres], reads=[], inc=1)
        dma("pop", Spool[:, :], bass.AP(e_pool.tensor, 0, [[256, P], [1, 256]]),
            writes=[Spool], reads=[eres])
        dve(lambda E_: E_.memset(zz256[:, :], 0.0), writes=[zz256], reads=[])
        dve(lambda E_: E_.memset(rnkp[:, :], 0.0), writes=[rnkp], reads=[])
        for e in range(E):
            dve(lambda E_, e=e: E_.tensor_scalar(
                out=mske[:, :], in0=Spool[:, :], scalar1=float(e), scalar2=0.0,
                op0=ALU.is_equal, op1=ALU.add, accum_out=tote[:, :]),
                writes=[mske, tote], reads=[Spool])
            dve(lambda E_, e=e: E_.tensor_copy(out=totb[:, e:e + 1], in_=tote[:, :]),
                writes=[totb], reads=[tote])
            dve(lambda E_: E_.tensor_tensor_scan(
                out=incl[:, :], data0=mske[:, :], data1=zz256[:, :], initial=0.0,
                op0=ALU.add, op1=ALU.add),
                writes=[incl], reads=[mske, zz256])
            dve(lambda E_: E_.tensor_tensor(out=incl[:, :], in0=incl[:, :],
                                            in1=mske[:, :], op=ALU.subtract),
                writes=[incl], reads=[incl, mske])
            dve(lambda E_: E_.tensor_tensor(out=incl[:, :], in0=incl[:, :],
                                            in1=mske[:, :], op=ALU.mult),
                writes=[incl], reads=[incl, mske])
            dve(lambda E_: E_.tensor_tensor(out=rnkp[:, :], in0=rnkp[:, :],
                                            in1=incl[:, :], op=ALU.add),
                writes=[rnkp], reads=[rnkp, incl])
        mm(psM[:, 0:8], triu1[:, :], totb[:, :], writes=[psM], pre=[psM],
           reads=[triu1, totb])
        dve(lambda E_: E_.tensor_copy(out=offs[:, :], in_=psM[:, 0:8]),
            writes=[offs], reads=[psM])
        for e in range(E):
            dve(lambda E_, e=e: E_.tensor_scalar(
                out=mske[:, :], in0=Spool[:, :], scalar1=float(e), scalar2=None,
                op0=ALU.is_equal),
                writes=[mske], reads=[Spool])
            dve(lambda E_, e=e: E_.tensor_scalar(
                out=incl[:, :], in0=mske[:, :], scalar1=offs[:, e:e + 1],
                scalar2=None, op0=ALU.mult),
                writes=[incl], reads=[mske, offs])
            dve(lambda E_: E_.tensor_tensor(out=rnkp[:, :], in0=rnkp[:, :],
                                            in1=incl[:, :], op=ALU.add),
                writes=[rnkp], reads=[rnkp, incl])
        dve(lambda E_: E_.tensor_scalar(out=keepp[:, :], in0=rnkp[:, :],
                                        scalar1=float(CAP) - 0.5, scalar2=None,
                                        op0=ALU.is_lt),
            writes=[keepp], reads=[rnkp])
        dve(lambda E_: E_.scalar_tensor_tensor(
            out=srcp[:, :], in0=Spool[:, :], scalar=float(CAP), in1=rnkp[:, :],
            op0=ALU.mult, op1=ALU.add),
            writes=[srcp], reads=[Spool, rnkp])
        dve(lambda E_: E_.tensor_tensor(out=srcp[:, :], in0=srcp[:, :],
                                        in1=keepp[:, :], op=ALU.mult),
            writes=[srcp], reads=[srcp, keepp])
        pc(lambda E_: E_.iota(iotap[:, :], [[1, 256]], base=0,
                              channel_multiplier=256,
                              allow_small_or_imprecise_dtypes=True),
           writes=[iotap], reads=[])
        pc(lambda E_: E_.iota(iotaj[:, :], [[1, 256]], base=CAP,
                              channel_multiplier=0,
                              allow_small_or_imprecise_dtypes=True),
           writes=[iotaj], reads=[])
        dve(lambda E_: E_.tensor_scalar(out=scat[:, :], in0=Spool[:, :],
                                        scalar1=cid_s[:, 0:1], scalar2=None,
                                        op0=ALU.is_equal),
            writes=[scat], reads=[Spool, cid_s])
        dve(lambda E_: E_.tensor_tensor(out=scat[:, :], in0=scat[:, :],
                                        in1=keepp[:, :], op=ALU.mult),
            writes=[scat], reads=[scat, keepp])
        dve(lambda E_: E_.tensor_scalar(out=mske[:, :], in0=srcp[:, :],
                                        scalar1=cid_s[:, 2:3], scalar2=None,
                                        op0=ALU.subtract),
            writes=[mske], reads=[srcp, cid_s])
        dve(lambda E_: E_.tensor_tensor(out=mske[:, :], in0=mske[:, :],
                                        in1=iotaj[:, :], op=ALU.subtract),
            writes=[mske], reads=[mske, iotaj])
        dve(lambda E_: E_.tensor_tensor(out=mske[:, :], in0=mske[:, :],
                                        in1=scat[:, :], op=ALU.mult),
            writes=[mske], reads=[mske, scat])
        dve(lambda E_: E_.tensor_tensor(out=scat[:, :], in0=mske[:, :],
                                        in1=iotaj[:, :], op=ALU.add),
            writes=[scat], reads=[scat, mske, iotaj])
        dve(lambda E_: E_.tensor_copy(out=scat32[:, :], in_=scat[:, :]),
            writes=[scat32], reads=[scat])
        dve(lambda E_: E_.memset(tmpi[:, 0:64], 0), writes=[tmpi], reads=[])
        dma("sp", bass.AP(ddisp.tensor, 0, [[34, P], [1, 34]]),
            tmpi[:, 0:34], writes=[], reads=[tmpi])
        cx.fence("p1")
        for j in range(256):
            cx.op("p1", lambda E_, j=j: E_.indirect_dma_start(
                out=ddisp, out_offset=bass.IndirectOffsetOnAxis(
                    scat32[:, j:j + 1], 0),
                in_=iotap[:, j:j + 1], in_offset=None),
                writes=[], reads=[scat32, iotap], inc=16)
        cx.fence("sp")
        for h in range(8):
            dma("sp", D2[16 * h:16 * h + 16, 0:32],
                bass.AP(ddisp.tensor, 512 * h, [[1, 16], [16, 32]]),
                writes=[D2], reads=[], slow=True)
        mm(psM[0:16, 0:256], ownsel_s[:, :], srcp[:, :], writes=[psM], pre=[psM],
           reads=[ownsel_s, srcp])
        dve(lambda E_: E_.tensor_copy(out=ownv[0:16, 0:256], in_=psM[0:16, 0:256]),
            writes=[ownv], reads=[psM])
        mm(psM[0:16, 0:256], ownsel_s[:, :], keepp[:, :], writes=[psM], pre=[psM],
           reads=[ownsel_s, keepp])
        dve(lambda E_: E_.tensor_copy(out=ownv[0:16, 256:512], in_=psM[0:16, 0:256]),
            writes=[ownv], reads=[psM])
        for g in range(2):
            tr_pe(psM[0:P, 0:16], ownv[0:16, g * P:(g + 1) * P],
                  idenf[0:16, 0:16], writes=[psM], reads=[ownv, idenf])
            dve(lambda E_, g=g: E_.tensor_copy(
                out=bass.AP(sidx, g, [[sidx.shape[1], P], [2, 16]]),
                in_=psM[0:P, 0:16]),
                writes=[sidx], reads=[psM])
            tr_pe(psM[0:P, 16:32], ownv[0:16, 256 + g * P:256 + (g + 1) * P],
                  idenf[0:16, 0:16], writes=[psM], reads=[ownv, idenf])
            dve(lambda E_, g=g: E_.tensor_copy(
                out=bass.AP(gate, g, [[gate.shape[1], P], [2, 16]]),
                in_=psM[0:P, 16:32]),
                writes=[gate], reads=[psM])
        dve(lambda E_: E_.tensor_tensor(out=gate[:, :], in0=gate[:, :],
                                        in1=ptok[:, :], op=ALU.mult),
            writes=[gate], reads=[gate, ptok])
        for st in range(8):
            xe_ = xetm if st % 2 == 0 else xetmB
            for kk in range(4):
                cx.op("p1", lambda E_, st=st, kk=kk, xe_=xe_: E_.indirect_dma_start(
                    out=xe_[:, kk * D:(kk + 1) * D], out_offset=None,
                    in_=pool,
                    in_offset=bass.IndirectOffsetOnAxis(
                        D2[:, st * 4 + kk:st * 4 + kk + 1], 0)),
                    writes=[xe_], reads=[D2, pres], inc=16)
            for g in range(DC):
                for kk in range(4):
                    dma("sp", bass.AP(xefm_g[g], kk * P, [[TT, P], [1, P]]),
                        xe_[:, kk * D + g * P:kk * D + (g + 1) * P],
                        writes=[xefm_g[g]], reads=[xe_], transpose=True,
                        ch="a1" if g % 2 == 0 else "s1")
            for m in range(DHC):
                w_ = wsl[m % 2]
                dma("pop", w_[:, 0:1024],
                    bass.AP(W["W1_my"].tensor, m * P, [[DH, P], [P * DH, DC], [1, P]]),
                    writes=[w_], reads=[])
                ps_ = psA if m % 2 == 0 else psB
                for g in range(DC):
                    mm(ps_[:, 0:TT], w_[:, g * P:(g + 1) * P],
                       bass.AP(xefm_g[g], 0, [[TT, P], [1, TT]]),
                       writes=[ps_] if g == DC - 1 else [],
                       pre=[ps_] if g == 0 else (),
                       reads=([w_] + xefm_g) if g == DC - 1 else
                             [w_, xefm_g[g]],
                       start=(g == 0), stop=(g == DC - 1),
                       inc=1 if g == DC - 1 else 0)
                act(hexp[:, m * TT:(m + 1) * TT], ps_[:, 0:TT], AF.Relu,
                    writes=[hexp], reads=[ps_])
            for o in range(DC):
                for hh in range(2):
                    dma("pop", wsl[hh][:, 0:2048],
                        bass.AP(W["W2_my"].tensor, o * P + hh * 16 * P * D,
                                [[D, P], [P * D, 16], [1, P]]),
                        writes=[wsl[hh]], reads=[])
                ps_ = psA if o % 2 == 0 else psB
                for g in range(DHC):
                    mm(ps_[:, 0:TT], wsl[g // 16][:, (g % 16) * P:(g % 16 + 1) * P],
                       hexp[:, g * TT:(g + 1) * TT],
                       writes=[ps_] if g == DHC - 1 else [],
                       pre=[ps_] if g == 0 else (),
                       reads=[wsl[0], wsl[1], hexp] if g == DHC - 1 else
                             [wsl[g // 16], hexp],
                       start=(g == 0), stop=(g == DHC - 1),
                       inc=1 if g == DHC - 1 else 0)
                act(yefm[:, o * TT:(o + 1) * TT], ps_[:, 0:TT], AF.Identity,
                    writes=[yefm], reads=[ps_])
            for g in range(DC):
                for kk in range(4):
                    dma("sp",
                        bass.AP(xe_, kk * D + g * P, [[xe_.shape[1], P], [1, P]]),
                        yefm[:, g * TT + kk * P:g * TT + (kk + 1) * P],
                        writes=[xe_], reads=[yefm], transpose=True,
                        ch="a1" if kk % 2 == 0 else "s1")
            for h in range(8):
                dma("sp",
                    bass.AP(ye_own.tensor, (512 * h + 64 * st) * D,
                            [[D, 16], [16 * D, 4], [1, D]]),
                    bass.AP(xe_, (16 * h) * xe_.shape[1],
                            [[xe_.shape[1], 16], [D, 4], [1, D]]),
                    writes=[], reads=[xe_])
        cx.fence("pc")
        yres = cx.res("yepool_res")
        pc(lambda E_: E_.collective_compute(
            "AllGather", ALU.bypass, replica_groups=[list(range(8))],
            ins=[ye_own], outs=[ye_pool]), writes=[yres], reads=[], inc=1)
        for cc in range(2 * SC):
            xe_ = xetm if cc % 2 == 0 else xetmB
            ys_ = ystage if cc % 2 == 0 else ystageB
            x4_ = xt4 if cc % 2 == 0 else xt4B
            cx.op("p1", lambda E_, cc=cc, xe_=xe_: E_.indirect_dma_start(
                out=xe_[:, 0:D], out_offset=None,
                in_=ye_pool,
                in_offset=bass.IndirectOffsetOnAxis(sidx[:, cc:cc + 1], 0)),
                writes=[xe_], reads=[sidx, yres], inc=16)
            dma("sp", x4_[:, :],
                bass.AP(x4_dram.tensor, (cc * P) * D, [[D, P], [1, D]]),
                writes=[x4_], reads=[], pre=[x4res])
            dve(lambda E_, cc=cc, xe_=xe_, ys_=ys_: E_.tensor_scalar(
                out=ys_[:, :], in0=xe_[:, 0:D], scalar1=gate[:, cc:cc + 1],
                scalar2=None, op0=ALU.mult),
                writes=[ys_], reads=[xe_, gate])
            dve(lambda E_, ys_=ys_, x4_=x4_: E_.tensor_tensor(
                out=ys_[:, :], in0=ys_[:, :], in1=x4_[:, :], op=ALU.add),
                writes=[ys_], reads=[ys_, x4_])
            dma("sp", bass.AP(y_out.tensor, (cc * P) * D, [[D, P], [1, D]]),
                ys_[:, :], writes=[], reads=[ys_])

    stage_i = STAGES.index(UPTO)
    phase_p0()
    phase_quantw()
    for s in range(NSEQ):
        if stage_i < STAGES.index("b1"):
            break
        SEQ[0] = s
        dma("sp", bass.AP(stream, 0, [[stream.shape[1], P], [D, SC], [1, D]]),
            bass.AP(x_in.tensor, (s * S) * D, [[D, P], [P * D, SC], [1, D]]),
            writes=[stream], reads=[])
        block1()
        if stage_i >= STAGES.index("b2"):
            block2()
        if stage_i >= STAGES.index("b3"):
            block3()
        if stage_i >= STAGES.index("route"):
            route_and_spill(s)
        if UPTO in ("b1", "b2", "b3"):
            break
    if stage_i >= STAGES.index("moe"):
        moe()

    tapbufs = {"stream": stream, "stats": stats, "swrec": swrec, "tmpb": tmpb,
               "xnfm": xnfm, "kfm": kfm, "vt": vt, "qtile": qtile, "avfm": avfm,
               "ptok": ptok, "gate": gate, "sidx": sidx, "consts": consts}
    for tname, sbname, shape, dt in TAPS:
        t_out = nc.dram_tensor(tname, shape, dt, kind="ExternalOutput").ap()
        buf = tapbufs[sbname]
        cx.fence("sp")
        dma("sp", t_out, bass.AP(buf, 0, [[buf.shape[1], shape[0]], [1, shape[1]]]),
            writes=[], reads=[buf])

    cx.fence("sp")
    for ch in ("sp", "s1", "s2", "a1", "pop", "p1"):
        cx.q["sp"].append(lambda E_, sem=cx.sems[ch], v=cx.count[ch]:
                          E_.wait_ge(sem, v))

    cx.emit_all()
    cx.stack.close()
    return nc


_BUILT = {}


def _get_nc():
    key = (UPTO, tuple(t[0] for t in TAPS))
    if key not in _BUILT:
        nc = build()
        lower_extended_insts(nc)
        _BUILT[key] = nc
    return _BUILT[key]


def kernel(**inputs):
    bf = ml_dtypes.bfloat16
    hc = _host_consts()
    x = np.ascontiguousarray(np.asarray(inputs["L0_attn_x"], dtype=np.float32))
    ins = []
    for c in range(8):
        m = {"x": x[2 * c:2 * c + 2]}
        m["W_qkv"] = np.asarray(inputs["L0_attn_W_qkv"], np.float32)
        m["b_qkv"] = np.asarray(inputs["L0_attn_b_qkv"], np.float32)
        m["W_o"] = np.asarray(inputs["L0_attn_W_o"], np.float32)
        m["b_o"] = np.asarray(inputs["L0_attn_b_o"], np.float32)
        m["ffn_w1"] = np.asarray(inputs["L0_ffn_w1"], np.float32)
        m["conv_w"] = np.asarray(inputs["L0_ffn_conv_w"], np.float32)
        m["conv_b"] = np.asarray(inputs["L0_ffn_conv_b"], np.float32)
        m["ffn_w2"] = np.asarray(inputs["L0_ffn_w2"], np.float32)
        for k_src, k_dst in [("L1_attn_wq_lat", "wq_lat"), ("L1_attn_wk_lat", "wk_lat"),
                             ("L1_attn_wv_lat", "wv_lat"), ("L1_attn_wq_heads", "wq_heads"),
                             ("L1_attn_wk_heads", "wk_heads"), ("L1_attn_wv_heads", "wv_heads"),
                             ("L1_attn_w_proj", "w_proj"), ("L1_attn_b_proj", "b_proj"),
                             ("L1_ffn_Wr", "Wr")]:
            m[k_dst] = np.asarray(inputs[k_src], np.float32)
        m["W1_my"] = np.asarray(inputs["L1_ffn_W1"][c], np.float32)
        m["W2_my"] = np.asarray(inputs["L1_ffn_W2"][c], np.float32)
        m.update({k: v for k, v in hc.items()})
        osel = np.zeros((P, 16), np.float32)
        for q in range(16):
            osel[16 * c + q, q] = 1.0
        m["ownsel"] = osel
        cidm = np.zeros((P, 4), np.float32)
        cidm[:, 0] = c
        cidm[:, 1] = c * T
        cidm[:, 2] = c * CAP
        m["cid"] = cidm
        ins.append(m)

    nc = _get_nc()
    res = run_bass_kernel_spmd(nc, ins, core_ids=list(range(8)))
    out = np.concatenate([np.asarray(r["y"]) for r in res.results], axis=0)
    return out.reshape(16, S, D)


# version 20
# speedup vs baseline: 1.0254x; 1.0254x over previous
import contextlib
import numpy as np
import ml_dtypes

import concourse.bass as bass
import concourse.mybir as mybir
from concourse.bass_utils import run_bass_kernel_spmd
from concourse.library_config import all_libraries
from concourse.library_overlay import lower_extended_insts

F32 = mybir.dt.float32
BF16 = mybir.dt.bfloat16
I32 = mybir.dt.int32
I16 = mybir.dt.int16
U32 = mybir.dt.uint32
AF = mybir.ActivationFunctionType
ALU = mybir.AluOpType
AX = mybir.AxisListType
LIBS = {l.name: l for l in all_libraries}

P = 128
D = 1024
S = 2048
NSEQ = 2
T = NSEQ * S
NH = 16
HD = 64
DC = D // P
SC = S // P
TT = 512
NTT = S // TT
DH = 4096
DHC = DH // P
LAT = 256
E = 8
CAP = 4096
TTOT = 8 * T
C_L0 = 24.0
C_L1 = 6.0
RND = 256.0

ENGS = ("pe", "act", "dve", "pop", "p1", "pc", "pc2",
        "sp", "s1", "s2", "a1")
ENG_ATTR = {"pe": "tensor", "act": "scalar", "dve": "vector", "pop": "gpsimd",
            "p1": "gpsimd", "pc": "gpsimd", "pc2": "gpsimd",
            "sp": "sync", "s1": "sync", "s2": "sync", "a1": "scalar"}
QKEY = {"pc": "pop", "pc2": "pop", "p1": "pop", "s1": "sp", "s2": "sp",
        "a1": "act"}
QUEUES = ("pe", "act", "dve", "pop", "sp")

STAGES = ("p0", "b1", "b2", "b3", "route", "moe", "full")
UPTO = "full"
TAPS = []
DBG_B2 = False


class _Res:

    def __init__(self, name):
        self.name = name


class Ctx:

    def __init__(self, nc):
        self.nc = nc
        self.stack = contextlib.ExitStack()
        self.sems = {e: self.stack.enter_context(nc.semaphore(f"ch_{e}"))
                     for e in ENGS}
        self.count = {e: 0 for e in ENGS}
        self.q = {e: [] for e in QUEUES}
        self.waited = {e: {e2: 0 for e2 in ENGS} for e in ENGS}
        self.track = {}
        self.alias = {}
        self.extent = {}
        self.overl = {}
        self.bounds = {"sp": [0], "s1": [0], "s2": [0], "a1": [0],
                       "pop": [0], "p1": [0]}

    def _bytes(self, shape, dt):
        import concourse.mybir as _mb
        return shape[1] * _mb.dt.size(dt)

    def _reg(self, name, space, lo, hi):
        self.track[name] = [None, []]
        self.extent[name] = (space, lo, hi)
        self.overl[name] = []
        for o, (sp2, lo2, hi2) in self.extent.items():
            if o != name and sp2 == space and lo < hi2 and lo2 < hi:
                self.overl[name].append(o)
                self.overl[o].append(name)

    def res(self, name):
        r = _Res(name)
        self._reg(name, "res:" + name, 0, 1)
        return r

    def sb(self, name, shape, dt):
        t = self.stack.enter_context(self.nc.sbuf_tensor(name, shape, dt))
        base = self.nc.lookup_mloc(t).addr
        self._reg(t.name, "sb", base, base + self._bytes(shape, dt))
        return t

    def sb_at(self, name, shape, dt, backing, byte_off):
        base = self.nc.lookup_mloc(backing).addr
        v = self.nc.alloc_sbuf_tensor_at(name, shape, dt, offset=base + byte_off)
        self._reg(v.name, "sb", base + byte_off,
                  base + byte_off + self._bytes(shape, dt))
        self.alias[v.name] = v.name
        return v

    def ps(self, name, shape, dt=F32):
        t = self.stack.enter_context(self.nc.psum_tensor(name, shape, dt))
        base = self.nc.lookup_mloc(t).addr
        self._reg(t.name, "ps", base, base + self._bytes(shape, dt))
        return t

    def ps_at(self, name, shape, dt, backing, byte_off):
        v = self.nc.virtual_psum_tensor(name, shape, dt)
        self.nc.back_virtual_tensor_with(v, backing)
        base = self.nc.lookup_mloc(backing).addr
        self._reg(v.name, "ps", base + byte_off,
                  base + byte_off + self._bytes(shape, dt))
        return v

    def _req(self, eng, tok):
        if tok is None:
            return
        pe, pv = tok
        if pe in self.bounds:
            bs = self.bounds[pe]
            b = None
            for x in bs:
                if x >= pv:
                    b = x
                    break
            if b is None:
                cur = self.count[pe]
                sem0 = self.sems[pe]
                if self.waited[pe][pe] < cur:
                    self.waited[pe][pe] = cur
                    self.q[QKEY.get(pe, pe)].append(
                        lambda E_, sem0=sem0, cur=cur: E_.wait_ge(sem0, cur))
                bs.append(cur)
                b = cur
            pv = b
            if QKEY.get(pe, pe) == QKEY.get(eng, eng):
                return
        if self.waited[eng][pe] >= pv:
            return
        self.waited[eng][pe] = pv
        sem = self.sems[pe]
        self.q[QKEY.get(eng, eng)].append(
            lambda E_, sem=sem, pv=pv: E_.wait_ge(sem, pv))

    def _names(self, b):
        return (b.name,) + tuple(self.overl.get(b.name, ()))

    def op(self, eng, emit, writes=(), reads=(), inc=1, pre=()):
        for b in reads:
            for n in self._names(b):
                self._req(eng, self.track[n][0])
        for b in tuple(pre) + tuple(writes):
            for n in self._names(b):
                wt, rts = self.track[n]
                self._req(eng, wt)
                for t in rts:
                    self._req(eng, t)
        tok = None
        if inc:
            self.count[eng] += inc
            tok = (eng, self.count[eng])
            sem = self.sems[eng]
            self.q[QKEY.get(eng, eng)].append(
                lambda E_, emit=emit, sem=sem, inc=inc:
                emit(E_).then_inc(sem, inc))
        else:
            self.q[QKEY.get(eng, eng)].append(lambda E_, emit=emit: emit(E_))
        for b in writes:
            self.track[b.name] = [tok, []]
        if inc:
            for b in reads:
                self.track[b.name][1].append(tok)
        return tok

    def fence(self, eng):
        for e2 in ENGS:
            if e2 != eng:
                self._req(eng, (e2, self.count[e2]))

    def fence_all(self):
        for e in ENGS:
            self.fence(e)

    def emit_all(self):
        with self.nc.Block() as block:
            for e in QUEUES:
                fns = list(self.q[e])

                def make(fns):
                    def body(EE):
                        for fn in fns:
                            fn(EE)
                    return body
                getattr(block, ENG_ATTR[e])(make(fns))


def _host_consts():
    bf = ml_dtypes.bfloat16
    iden = np.eye(P, dtype=np.float32)
    mask = np.zeros((P, 4 * TT), np.float32)
    for i in range(4):
        for k in range(P):
            lo = 128 * i + k
            if lo < TT:
                mask[k, i * TT + lo:(i + 1) * TT] = 1.0
    return {
        "iden_b": iden.astype(bf),
        "iden_f": iden.copy(),
        "triu1_b": np.triu(np.ones((P, P), np.float32), 1).astype(bf),
        "maskl1_b": mask.astype(bf),
    }


def build():
    nc = bass.Bass("TRN2", target_bir_lowering=False)
    cx = Ctx(nc)

    x_in = nc.dram_tensor("x", [NSEQ, S, D], F32, kind="ExternalInput").ap()
    y_out = nc.dram_tensor("y", [NSEQ, S, D], F32, kind="ExternalOutput").ap()
    W = {}
    for name, shape in [
        ("W_qkv", [D, 3 * D]), ("b_qkv", [3 * D]), ("W_o", [D, D]), ("b_o", [D]),
        ("ffn_w1", [DH, D]), ("conv_w", [DH, 1, 3]), ("conv_b", [DH]),
        ("ffn_w2", [D, DH]),
        ("wq_lat", [D, LAT]), ("wk_lat", [D, LAT]), ("wv_lat", [D, LAT]),
        ("wq_heads", [LAT, D]), ("wk_heads", [LAT, D]), ("wv_heads", [LAT, D]),
        ("w_proj", [D, D]), ("b_proj", [D]),
        ("Wr", [D, E]), ("W1_my", [D, DH]), ("W2_my", [DH, D]),
    ]:
        W[name] = nc.dram_tensor(name, shape, F32, kind="ExternalInput").ap()
    HC = {}
    for name, dt, shape in [
        ("iden_b", BF16, [P, P]), ("iden_f", F32, [P, P]),
        ("triu1_b", BF16, [P, P]), ("maskl1_b", BF16, [P, 4 * TT]),
        ("ownsel", F32, [P, 16]), ("cid", F32, [P, 4]),
    ]:
        HC[name] = nc.dram_tensor(name, shape, dt, kind="ExternalInput").ap()

    x4_dram = nc.dram_tensor("x4s", [T, D], F32).ap()
    h_dram = nc.dram_tensor("hs", [DHC, P, S], BF16).ap()
    wq1T_dram = nc.dram_tensor("wq1T", [D, DH], BF16).ap()
    wq2T_dram = nc.dram_tensor("wq2T", [DH, D], BF16).ap()
    in_pool = nc.dram_tensor("in_pool", [T, D], BF16).ap()
    pool = nc.dram_tensor("pool", [TTOT, D], BF16).ap()
    e_own = nc.dram_tensor("e_own", [T, 1], BF16).ap()
    e_pool = nc.dram_tensor("e_pool", [TTOT, 1], BF16).ap()
    ye_own = nc.dram_tensor("ye_own", [CAP, D], BF16).ap()
    ddisp = nc.dram_tensor("ddisp", [CAP + 256, 1], I32).ap()
    ye_pool = nc.dram_tensor("ye_pool", [8 * CAP, D], BF16).ap()

    sb = cx.sb
    stream = sb("stream", [P, SC * D], F32)
    xnfm = sb("xnfm", [P, DC * S], BF16)
    kfm = sb("kfm", [P, DC * S], BF16)
    vt = sb("vt", [P, SC * NH * 65], BF16)
    qtile = sb("qtile", [P, DC * TT], BF16)
    avfm = sb("avfm", [P, DC * TT], BF16)
    wring = sb("wring", [P, 4096], BF16)
    ptile = sb("ptile", [P, 2 * 1024], BF16)
    tmp1 = sb("tmp1", [P, 512], F32)
    tmp2 = sb("tmp2", [P, 512], F32)
    tmpi = sb("tmpi", [P, 512], I32)
    tmpb = sb("tmpb", [P, 512], BF16)
    tmpb2 = sb("tmpb2", [P, 512], BF16)
    maskl1_s = sb("maskl1_s", [P, 4 * TT], BF16)
    bcastb = sb("bcastb", [P, TT], BF16)
    onesb = sb("onesb", [P, P], BF16)
    onesf = sb("onesf", [P, P], F32)
    stats = sb("stats", [P, 96], F32)
    ptok = sb("ptok", [P, NSEQ * SC], F32)
    gate = sb("gate", [P, NSEQ * SC], F32)
    sidx = sb("sidx", [P, NSEQ * SC], I32)
    consts = sb("consts", [P, 16], F32)
    idenb = sb("idenb", [P, P], BF16)
    idenf = sb("idenf", [P, P], F32)
    triu1 = sb("triu1", [P, P], BF16)
    ownsel_s = sb("ownsel_s", [P, 16], F32)
    cid_s = sb("cid_s", [P, 4], F32)
    biases = sb("biases", [P, 48], F32)
    convw = sb("convw", [P, 3 * DHC], F32)
    convb = sb("convb", [P, DHC], F32)
    wrs = sb("wrs", [P, DC * E], BF16)
    diag = sb("diag", [P, 3 * P], BF16)
    swrec = sb("swrec", [P, 2], F32)
    sdq = sb("sdq", [P, 4 * SC], F32)
    lnsq = cx.sb_at("lnsq", [P, D], F32, qtile, 0)
    qw = [cx.sb_at(f"qw{i}", [P, 512], F32, vt, 2048 * i) for i in range(4)]
    qt1 = cx.sb_at("qt1", [P, 512], F32, vt, 8192)
    qt2 = cx.sb_at("qt2", [P, 512], F32, vt, 10240)
    qti = cx.sb_at("qti", [P, 512], I32, vt, 12288)
    qb = [cx.sb_at(f"qb{i}", [P, 512], BF16, vt, 14336 + 1024 * i)
          for i in range(2)]
    qc = [cx.sb_at(f"qc{i}", [P, 512], BF16, vt, 16384 + 1024 * i)
          for i in range(2)]
    qstats = cx.sb_at("qstats", [P, 8], F32, vt, 18432)
    qcq = [[cx.sb_at(f"qcq{i}_{g}", [P, 128], BF16, vt, 16384 + 1024 * i + 256 * g)
            for g in range(4)] for i in range(2)]

    iotap = cx.sb_at("iotap", [P, 256], I32, vt, 0)
    scat = cx.sb_at("scat", [P, 256], F32, vt, 1024)
    scat32 = cx.sb_at("scat32", [P, 256], I32, vt, 2048)
    iotaj = cx.sb_at("iotaj", [P, 256], F32, vt, 3072)
    xetmB = cx.sb_at("xetmB", [P, 4 * D], BF16, vt, 4096)
    ystageB = cx.sb_at("ystageB", [P, D], F32, kfm, 8192)
    xt4B = cx.sb_at("xt4B", [P, D], F32, kfm, 12288)
    hexp = cx.sb_at("hexp", [P, DHC * TT], BF16, kfm, 0)
    xefm = cx.sb_at("xefm", [P, DC * TT], BF16, xnfm, 0)
    xetm = cx.sb_at("xetm", [P, 4 * D], BF16, xnfm, 8192)
    Spool = cx.sb_at("Spool", [P, TTOT // P], F32, xnfm, 16384)
    mske = cx.sb_at("mske", [P, 256], F32, xnfm, 17408)
    incl = cx.sb_at("incl", [P, 256], F32, xnfm, 18432)
    rnkp = cx.sb_at("rnkp", [P, 256], F32, xnfm, 19456)
    srcp = cx.sb_at("srcp", [P, 256], F32, xnfm, 20480)
    keepp = cx.sb_at("keepp", [P, 256], F32, xnfm, 21504)
    zz256 = cx.sb_at("zz256", [P, 256], F32, xnfm, 22528)
    ownv = cx.sb_at("ownv", [16, 2 * 256], F32, xnfm, 23552)
    totb = cx.sb_at("totb", [P, E], BF16, xnfm, 30208)
    offs = cx.sb_at("offs", [P, E], F32, xnfm, 30240)
    tote = cx.sb_at("tote", [P, 1], F32, xnfm, 30272)
    D2 = cx.sb_at("D2", [P, CAP // P], I32, xnfm, 30336)

    psS = cx.ps("psS", [P, 1024], F32)
    psS2 = cx.ps("psS2", [P, 1024], F32)
    psAV = cx.ps("psAV", [P, TT], F32)
    psA = cx.ps("psA", [P, TT], F32)
    psB = cx.ps("psB", [P, TT], F32)
    psM = cx.ps("psM", [P, TT], F32)


    hres = [cx.res("hres0"), cx.res("hres1")]
    hload = [cx.res("hload0"), cx.res("hload1")]
    qres = [cx.res("qres1"), cx.res("qres2")]
    x4res = cx.res("x4res")
    SEQ = [0]

    KNOWN_CONSTS = [-C_L0, -C_L1, 0.125, 1e-5, RND, -RND, 127.0, -1.0,
                    1.0 / D, 0.5, -0.5, 2.0]
    for i, v in enumerate(KNOWN_CONSTS):
        nc.const_aps.register(F32, v, consts[:, i:i + 1])

    def dma(qeng, out_ap, in_ap, writes, reads, slow=False, transpose=False,
            pre=(), ch=None):
        if ch is not None:
            qeng = ch
        elif qeng == "sp":
            if transpose:
                qeng = "s1"
            else:
                t = getattr(out_ap, "tensor", None)
                if t is not None and "DRam" in type(t).__name__:
                    qeng = "s2"

        def em(E_):
            return E_.dma_start(out=out_ap, in_=in_ap, transpose=transpose,
                                allow_slow_non_contiguous=slow)
        return cx.op(qeng, em, writes=writes, reads=reads, inc=16, pre=pre)

    def mm(out_ap, lhsT_ap, rhs_ap, writes, reads, start=True, stop=True, inc=1,
           pre=()):
        def em(E_):
            return E_.matmul(out_ap, lhsT_ap, rhs_ap, start=start, stop=stop)
        return cx.op("pe", em, writes=writes, reads=reads, inc=inc, pre=pre)

    def tr_pe(out_ap, in_ap, ident_ap, writes, reads):
        def em(E_):
            return E_.transpose(out=out_ap, in_=in_ap, identity=ident_ap)
        return cx.op("pe", em, writes=writes, reads=reads, inc=1)

    def act(out_ap, in_ap, func, writes, reads, bias=0.0, scale=1.0, accum=None):
        def em(E_):
            return E_.activation(out=out_ap, in_=in_ap, func=func, bias=bias,
                                 scale=scale, accum_out=accum)
        return cx.op("act", em, writes=writes, reads=reads, inc=1)

    def dve(emit, writes, reads, inc=1):
        return cx.op("dve", emit, writes=writes, reads=reads, inc=inc)

    def pop(emit, writes, reads, inc=1):
        return cx.op("pop", emit, writes=writes, reads=reads, inc=inc)

    def pc(emit, writes, reads, inc=1):
        return cx.op("pc", emit, writes=writes, reads=reads, inc=inc)

    def stream_ap(c, w=D):
        return bass.AP(stream, c * D, [[stream.shape[1], P], [1, w]])

    def quant_round(dst_bf, src_ap, sa, sb_plus_rnd, width, post=None,
                    wlist=(), rlist=()):
        dve(lambda E_: E_.tensor_scalar(
            out=tmp2[:, 0:width], in0=src_ap, scalar1=sa, scalar2=sb_plus_rnd,
            op0=ALU.mult, op1=ALU.add),
            writes=[tmp2], reads=list(rlist) + [stats])
        dve(lambda E_: E_.tensor_copy(out=tmpi[:, 0:width], in_=tmp2[:, 0:width]),
            writes=[tmpi], reads=[tmp2])
        dve(lambda E_: E_.scalar_tensor_tensor(
            out=tmp1[:, 0:width], in0=tmpi[:, 0:width], scalar=0.5,
            in1=tmp2[:, 0:width], op0=ALU.add, op1=ALU.is_le),
            writes=[tmp1], reads=[tmpi, tmp2])
        if post is None:
            dve(lambda E_: E_.scalar_tensor_tensor(
                out=dst_bf, in0=tmpi[:, 0:width], scalar=-RND,
                in1=tmp1[:, 0:width], op0=ALU.add, op1=ALU.add),
                writes=list(wlist), reads=[tmpi, tmp1])
        else:
            dve(lambda E_: E_.scalar_tensor_tensor(
                out=tmp2[:, 0:width], in0=tmpi[:, 0:width], scalar=-RND,
                in1=tmp1[:, 0:width], op0=ALU.add, op1=ALU.add),
                writes=[tmp2], reads=[tmpi, tmp1])
            dve(lambda E_: E_.tensor_scalar(
                out=dst_bf, in0=tmp2[:, 0:width], scalar1=post, scalar2=None,
                op0=ALU.mult),
                writes=list(wlist), reads=[tmp2, stats])

    def phase_p0():
        for i, v in enumerate(KNOWN_CONSTS):
            dve(lambda E_, v=v, i=i: E_.memset(consts[:, i:i + 1], v),
                writes=[consts], reads=[])
        dve(lambda E_: E_.memset(onesb[:, :], 1.0), writes=[onesb], reads=[])
        dve(lambda E_: E_.memset(onesf[:, :], 1.0), writes=[onesf], reads=[])
        for name, buf in [("iden_b", idenb), ("iden_f", idenf),
                          ("triu1_b", triu1), ("maskl1_b", maskl1_s),
                          ("ownsel", ownsel_s), ("cid", cid_s)]:
            dma("sp", buf[:, :], HC[name], writes=[buf], reads=[])
        dma("sp", bass.AP(biases, 0, [[biases.shape[1], P], [1, 24]]),
            bass.AP(W["b_qkv"].tensor, 0, [[1, P], [P, 24]]),
            writes=[biases], reads=[], slow=True)
        dma("sp", bass.AP(biases, 24, [[biases.shape[1], P], [1, 8]]),
            bass.AP(W["b_o"].tensor, 0, [[1, P], [P, 8]]),
            writes=[biases], reads=[], slow=True)
        dma("sp", bass.AP(biases, 32, [[biases.shape[1], P], [1, 8]]),
            bass.AP(W["b_proj"].tensor, 0, [[1, P], [P, 8]]),
            writes=[biases], reads=[], slow=True)
        dma("sp", bass.AP(convw, 0, [[convw.shape[1], P], [3, DHC], [1, 3]]),
            bass.AP(W["conv_w"].tensor, 0, [[3, P], [3 * P, DHC], [1, 3]]),
            writes=[convw], reads=[], slow=True)
        dma("sp", bass.AP(convb, 0, [[convb.shape[1], P], [1, DHC]]),
            bass.AP(W["conv_b"].tensor, 0, [[1, P], [P, DHC]]),
            writes=[convb], reads=[], slow=True)
        dma("pop", bass.AP(wrs, 0, [[wrs.shape[1], P], [E, DC], [1, E]]),
            bass.AP(W["Wr"].tensor, 0, [[E, P], [P * E, DC], [1, E]]),
            writes=[wrs], reads=[])
        vt_ones()

    def vt_ones():
        dve(lambda E_: E_.memset(
            bass.AP(vt, 64, [[vt.shape[1], P], [NH * 65, SC], [65, NH]]), 1.0),
            writes=[vt], reads=[])

    def quant_weight(w_ap, rows, cols, wqT_dram, sw_col):
        nstrip = rows // P
        CW = 512
        ncb = cols // CW
        cw = CW
        dve(lambda E_: E_.memset(qstats[:, 0:1], 0.0), writes=[qstats], reads=[])
        it = 0
        for m in range(nstrip):
            for cb in range(ncb):
                qw_ = qw[it % 4]
                it += 1
                dma("sp", qw_[:, 0:cw],
                    bass.AP(w_ap.tensor, (m * P) * cols + cb * cw,
                            [[cols, P], [1, cw]]),
                    writes=[qw_], reads=[])
                dve(lambda E_, qw_=qw_: E_.tensor_scalar(
                    out=qt2[:, 0:cw], in0=qw_[:, 0:cw], scalar1=-1.0,
                    scalar2=None, op0=ALU.mult), writes=[qt2], reads=[qw_])
                dve(lambda E_, qw_=qw_: E_.tensor_tensor(
                    out=qt2[:, 0:cw], in0=qw_[:, 0:cw], in1=qt2[:, 0:cw],
                    op=ALU.max), writes=[qt2], reads=[qw_, qt2])
                dve(lambda E_: E_.reduce_sum(
                    out=qstats[:, 1:2], in_=qt2[:, 0:cw], axis=AX.X),
                    writes=[qstats], reads=[qt2])
                dve(lambda E_: E_.tensor_tensor(
                    out=qstats[:, 0:1], in0=qstats[:, 0:1],
                    in1=qstats[:, 1:2], op=ALU.add), writes=[qstats],
                    reads=[qstats])
        mm(psM[0:1, 0:1], onesf[:, 0:1], qstats[:, 0:1],
           writes=[psM], pre=[psM], reads=[qstats, onesf])
        dve(lambda E_: E_.tensor_scalar(
            out=qstats[0:1, 2:3], in0=psM[0:1, 0:1],
            scalar1=1.0 / (rows * cols), scalar2=1e-5, op0=ALU.mult, op1=ALU.max),
            writes=[qstats], reads=[psM])
        mm(psM[:, 1:2], onesf[0:1, :], qstats[0:1, 2:3],
           writes=[psM], pre=[psM], reads=[qstats, onesf])
        dve(lambda E_, sw_col=sw_col: E_.tensor_copy(
            out=swrec[:, sw_col:sw_col + 1], in_=psM[:, 1:2]),
            writes=[swrec], reads=[psM])
        dve(lambda E_: E_.reciprocal(out=qstats[:, 3:4],
                                     in_=swrec[:, sw_col:sw_col + 1]),
            writes=[qstats], reads=[swrec])
        for m in range(nstrip):
            for cb in range(ncb):
                qw_ = qw[it % 4]
                qb_ = qb[it % 2]
                qcq_ = qcq[it % 2]
                it += 1
                dma("sp", qw_[:, 0:cw],
                    bass.AP(w_ap.tensor, (m * P) * cols + cb * cw,
                            [[cols, P], [1, cw]]),
                    writes=[qw_], reads=[])
                dve(lambda E_, qw_=qw_: E_.tensor_scalar(
                    out=qt2[:, 0:cw], in0=qw_[:, 0:cw],
                    scalar1=qstats[:, 3:4], scalar2=RND,
                    op0=ALU.mult, op1=ALU.add), writes=[qt2],
                    reads=[qw_, qstats])
                dve(lambda E_: E_.tensor_copy(out=qti[:, 0:cw], in_=qt2[:, 0:cw]),
                    writes=[qti], reads=[qt2])
                dve(lambda E_: E_.scalar_tensor_tensor(
                    out=qt1[:, 0:cw], in0=qti[:, 0:cw], scalar=0.5,
                    in1=qt2[:, 0:cw], op0=ALU.add, op1=ALU.is_le),
                    writes=[qt1], reads=[qti, qt2])
                dve(lambda E_: E_.scalar_tensor_tensor(
                    out=qt2[:, 0:cw], in0=qti[:, 0:cw], scalar=-RND,
                    in1=qt1[:, 0:cw], op0=ALU.add, op1=ALU.add),
                    writes=[qt2], reads=[qti, qt1])
                dve(lambda E_, qb_=qb_: E_.tensor_scalar(
                    out=qb_[:, 0:cw], in0=qt2[:, 0:cw], scalar1=1.0,
                    scalar2=-1.0, op0=ALU.min, op1=ALU.max),
                    writes=[qb_], reads=[qt2])
                for g in range(cw // P):
                    dma("sp", qcq_[g][:, :],
                        qb_[:, g * P:(g + 1) * P],
                        writes=[qcq_[g]], reads=[qb_], transpose=True,
                        ch="a1" if g % 2 == 0 else "s1")
                for g in range(cw // P):
                    dma("sp",
                        bass.AP(wqT_dram.tensor, (cb * cw + g * P) * rows + m * P,
                                [[rows, P], [1, P]]),
                        qcq_[g][:, :],
                        writes=[], reads=[qcq_[g], qres[sw_col]])

    def phase_quantw():
        quant_weight(W["ffn_w1"], DH, D, wq1T_dram, 0)
        quant_weight(W["ffn_w2"], D, DH, wq2T_dram, 1)


    wringA = cx.sb_at("wringA", [P, 2048], BF16, wring, 0)
    wringB = cx.sb_at("wringB", [P, 2048], BF16, wring, 4096)
    wsl = (wringA, wringB)
    xnfm_g = [cx.sb_at(f"xnfm_g{g}", [P, S], BF16, xnfm, g * 2 * S)
              for g in range(DC)]
    xefm_g = [cx.sb_at(f"xefm_g{g}", [P, TT], BF16, xnfm, g * 2 * TT)
              for g in range(DC)]
    xstage = cx.sb_at("xstage", [P, D], BF16, avfm, 0)
    xstage2 = cx.sb_at("xstage2", [P, D], BF16, avfm, 2048)
    ostage = cx.sb_at("ostage", [P, TT], BF16, ptile, 0)
    tstage = cx.sb_at("tstage", [P, TT], BF16, ptile, 2048)
    ostage2 = cx.sb_at("ostage2", [P, TT], BF16, ptile, 1024)
    tstage2 = cx.sb_at("tstage2", [P, TT], BF16, ptile, 3072)
    bvrow = cx.sb_at("bvrow", [1, TT], BF16, tmpb, 0)
    bvb = cx.sb_at("bvb", [P, TT], BF16, tmpb2, 0)
    recipb = cx.sb_at("recipb", [1, TT], BF16, tmpb, 0)

    def layer_norm(quant=False, sw_col=0, dq_block=0):
        for c in range(SC):
            xc = stream_ap(c)
            dve(lambda E_, xc=xc: E_.reduce_sum(out=stats[:, 80:81], in_=xc,
                                                axis=AX.X),
                writes=[stats], reads=[stream])
            dve(lambda E_, xc=xc: E_.scalar_tensor_tensor(
                out=lnsq[:, 0:D], in0=xc, scalar=1.0, in1=xc,
                op0=ALU.mult, op1=ALU.mult, accum_out=stats[:, 81:82]),
                writes=[lnsq, stats], reads=[stream])
            if quant:
                dve(lambda E_, xc=xc: E_.reduce_max(out=stats[:, 82:83], in_=xc,
                                                    axis=AX.X),
                    writes=[stats], reads=[stream])
                dve(lambda E_, xc=xc: E_.tensor_reduce(out=stats[:, 83:84], in_=xc,
                                                       axis=AX.X, op=ALU.min),
                    writes=[stats], reads=[stream])
            dve(lambda E_: E_.tensor_scalar(out=stats[:, 84:85], in0=stats[:, 80:81],
                                            scalar1=1.0 / D, scalar2=None,
                                            op0=ALU.mult),
                writes=[stats], reads=[stats])
            dve(lambda E_: E_.tensor_scalar(out=stats[:, 85:86], in0=stats[:, 81:82],
                                            scalar1=1.0 / D, scalar2=None,
                                            op0=ALU.mult),
                writes=[stats], reads=[stats])
            dve(lambda E_: E_.scalar_tensor_tensor(
                out=stats[:, 86:87], in0=stats[:, 84:85], scalar=1.0,
                in1=stats[:, 84:85], op0=ALU.mult, op1=ALU.mult),
                writes=[stats], reads=[stats])
            dve(lambda E_: E_.tensor_tensor(out=stats[:, 86:87], in0=stats[:, 85:86],
                                            in1=stats[:, 86:87], op=ALU.subtract),
                writes=[stats], reads=[stats])
            act(stats[:, 87:88], stats[:, 86:87], AF.Sqrt, bias=1e-5, scale=1.0,
                writes=[stats], reads=[stats])
            dve(lambda E_: E_.reciprocal(out=stats[:, 88:89], in_=stats[:, 87:88]),
                writes=[stats], reads=[stats])
            dve(lambda E_, c=c: E_.tensor_copy(out=stats[:, c:c + 1],
                                               in_=stats[:, 88:89]),
                writes=[stats], reads=[stats])
            dve(lambda E_, c=c: E_.scalar_tensor_tensor(
                out=stats[:, 16 + c:17 + c], in0=stats[:, 84:85], scalar=-1.0,
                in1=stats[:, 88:89], op0=ALU.mult, op1=ALU.mult),
                writes=[stats], reads=[stats])
            if quant:
                dve(lambda E_: E_.tensor_tensor(out=stats[:, 89:90],
                                                in0=stats[:, 82:83],
                                                in1=stats[:, 84:85],
                                                op=ALU.subtract),
                    writes=[stats], reads=[stats])
                dve(lambda E_: E_.tensor_tensor(out=stats[:, 90:91],
                                                in0=stats[:, 84:85],
                                                in1=stats[:, 83:84],
                                                op=ALU.subtract),
                    writes=[stats], reads=[stats])
                dve(lambda E_: E_.tensor_tensor(out=stats[:, 89:90],
                                                in0=stats[:, 89:90],
                                                in1=stats[:, 90:91], op=ALU.max),
                    writes=[stats], reads=[stats])
                dve(lambda E_: E_.tensor_tensor(out=stats[:, 89:90],
                                                in0=stats[:, 89:90],
                                                in1=stats[:, 88:89], op=ALU.mult),
                    writes=[stats], reads=[stats])
                dve(lambda E_: E_.tensor_scalar(out=stats[:, 89:90],
                                                in0=stats[:, 89:90], scalar1=1e-5,
                                                scalar2=None, op0=ALU.max),
                    writes=[stats], reads=[stats])
                dve(lambda E_: E_.reciprocal(out=stats[:, 90:91],
                                             in_=stats[:, 89:90]),
                    writes=[stats], reads=[stats])
                dve(lambda E_: E_.tensor_scalar(out=stats[:, 91:92],
                                                in0=stats[:, 90:91], scalar1=127.0,
                                                scalar2=None, op0=ALU.mult),
                    writes=[stats], reads=[stats])
                dve(lambda E_, c=c: E_.tensor_tensor(
                    out=stats[:, 32 + c:33 + c], in0=stats[:, c:c + 1],
                    in1=stats[:, 91:92], op=ALU.mult),
                    writes=[stats], reads=[stats])
                dve(lambda E_, c=c: E_.scalar_tensor_tensor(
                    out=stats[:, 48 + c:49 + c], in0=stats[:, 16 + c:17 + c],
                    scalar=1.0, in1=stats[:, 91:92], op0=ALU.mult, op1=ALU.mult),
                    writes=[stats], reads=[stats])
                dve(lambda E_, c=c: E_.tensor_scalar(
                    out=stats[:, 48 + c:49 + c], in0=stats[:, 48 + c:49 + c],
                    scalar1=RND, scalar2=None, op0=ALU.add),
                    writes=[stats], reads=[stats])
                dve(lambda E_, c=c: E_.tensor_scalar(
                    out=stats[:, 92:93], in0=stats[:, 89:90], scalar1=1.0 / 127.0,
                    scalar2=None, op0=ALU.mult), writes=[stats], reads=[stats])
                dve(lambda E_, c=c, sw_col=sw_col, dq_block=dq_block:
                    E_.tensor_tensor(
                        out=sdq[:, dq_block * SC + c:dq_block * SC + c + 1],
                        in0=stats[:, 92:93], in1=swrec[:, sw_col:sw_col + 1],
                        op=ALU.mult),
                    writes=[sdq], reads=[stats, swrec])

    DQBLK = [0]

    def normalize_to_fm(quant=False):
        for c in range(SC):
            xs_ = xstage if c % 2 == 0 else xstage2
            xc = stream_ap(c)
            if not quant:
                dve(lambda E_, xc=xc, c=c, xs_=xs_: E_.tensor_scalar(
                    out=xs_[:, :], in0=xc, scalar1=stats[:, c:c + 1],
                    scalar2=stats[:, 16 + c:17 + c], op0=ALU.mult, op1=ALU.add),
                    writes=[xs_], reads=[stream, stats])
            else:
                for half in range(2):
                    sl = slice(half * TT, (half + 1) * TT)
                    xs_ap = bass.AP(stream, c * D + half * TT,
                                    [[stream.shape[1], P], [1, TT]])
                    dve(lambda E_, xs_ap=xs_ap, c=c: E_.tensor_scalar(
                        out=tmp2[:, 0:TT], in0=xs_ap, scalar1=stats[:, 32 + c:33 + c],
                        scalar2=stats[:, 48 + c:49 + c], op0=ALU.mult, op1=ALU.add),
                        writes=[tmp2], reads=[stream, stats])
                    dve(lambda E_: E_.tensor_copy(out=tmpi[:, 0:TT],
                                                  in_=tmp2[:, 0:TT]),
                        writes=[tmpi], reads=[tmp2])
                    dve(lambda E_: E_.scalar_tensor_tensor(
                        out=tmp1[:, 0:TT], in0=tmpi[:, 0:TT], scalar=0.5,
                        in1=tmp2[:, 0:TT], op0=ALU.add, op1=ALU.is_le),
                        writes=[tmp1], reads=[tmpi, tmp2])
                    dve(lambda E_, sl=sl: E_.scalar_tensor_tensor(
                        out=tmp2[:, 0:TT], in0=tmpi[:, 0:TT], scalar=-RND,
                        in1=tmp1[:, 0:TT], op0=ALU.add, op1=ALU.add),
                        writes=[tmp2], reads=[tmpi, tmp1])
                    dve(lambda E_, sl=sl, c=c, xs_=xs_: E_.tensor_scalar(
                        out=xs_[:, sl], in0=tmp2[:, 0:TT],
                        scalar1=sdq[:, DQBLK[0] * SC + c:DQBLK[0] * SC + c + 1],
                        scalar2=None, op0=ALU.mult),
                        writes=[xs_], reads=[tmp2, sdq])
            for g in range(DC):
                dma("sp", bass.AP(xnfm_g[g], c * P, [[S, P], [1, P]]),
                    bass.AP(xs_, g * P, [[xs_.shape[1], P], [1, P]]),
                    writes=[xnfm_g[g]], reads=[xs_], transpose=True,
                    ch="a1" if g % 2 == 0 else "s1")

    def wcast_strip(src_ap, slot, width=1024):
        return dma("pop", slot[:, 0:width], src_ap,
                   writes=[slot], reads=[])

    def proj_strip_ap(wten, wcols, o, nblk=8):
        return bass.AP(wten.tensor, o * P, [[wcols, P], [P * wcols, nblk], [1, P]])

    def attn_common(is_l0):
        pass

    def block1():
        layer_norm(quant=False)
        normalize_to_fm(quant=False)
        vt_ones()
        for o in range(DC):
            w_ = wsl[o % 2]
            wcast_strip(proj_strip_ap(W["W_qkv"], 3 * D, 8 + o), w_)
            for tt in range(NTT):
                ps_ = psA if tt % 2 == 0 else psB
                for g in range(DC):
                    mm(ps_[:, 0:TT], w_[:, g * P:(g + 1) * P],
                       bass.AP(xnfm_g[g], tt * TT, [[S, P], [1, TT]]),
                       writes=[ps_] if g == DC - 1 else [],
                       pre=[ps_] if g == 0 else (),
                       reads=([w_] + xnfm_g) if g == DC - 1 else
                             [w_, xnfm_g[g]],
                       start=(g == 0), stop=(g == DC - 1),
                       inc=1 if g == DC - 1 else 0)
                act(bass.AP(kfm, o * S + tt * TT, [[kfm.shape[1], P], [1, TT]]),
                    ps_[:, 0:TT], AF.Identity, bias=biases[:, 8 + o:9 + o],
                    writes=[kfm], reads=[ps_, biases])
        for vcb in range(2):
            for hh in range(2):
                dma("pop", wsl[hh][:, 0:2048],
                    bass.AP(W["W_qkv"].tensor, 2 * D + vcb * TT + hh * 4 * P * 3 * D,
                            [[3 * D, P], [P * 3 * D, 4], [1, TT]]),
                    writes=[wsl[hh]], reads=[])
            dma("pop", bvrow[0:1, 0:TT],
                bass.AP(W["b_qkv"].tensor, 2 * D + vcb * TT, [[1, 1], [1, TT]]),
                writes=[bvrow], reads=[], slow=True)
            mm(psM[0:P, 0:TT], onesb[0:1, 0:P], bvrow[0:1, 0:TT],
               writes=[psM], reads=[bvrow, onesb])
            act(bvb[:, 0:TT], psM[:, 0:TT], AF.Identity, writes=[bvb], reads=[psM])
            for tc in range(SC):
                ps_ = psA if tc % 2 == 0 else psB
                for g in range(DC):
                    mm(ps_[:, 0:TT],
                       bass.AP(xnfm_g[g], tc * P, [[S, P], [1, P]]),
                       wsl[g // 4][:, (g % 4) * TT:(g % 4 + 1) * TT],
                       writes=[ps_] if g == DC - 1 else [],
                       pre=[ps_] if g == 0 else (),
                       reads=([wsl[0], wsl[1]] + xnfm_g) if g == DC - 1 else
                             [wsl[g // 4], xnfm_g[g]],
                       start=(g == 0), stop=(g == DC - 1),
                       inc=1 if g == DC - 1 else 0)
                dve(lambda E_, tc=tc, vcb=vcb, ps_=ps_: E_.tensor_tensor(
                    out=bass.AP(vt, tc * (NH * 65) + vcb * 8 * 65,
                                [[vt.shape[1], P], [65, 8], [1, 64]]),
                    in0=ps_[:, 0:TT], in1=bvb[:, 0:TT], op=ALU.add),
                    writes=[vt], reads=[ps_, bvb])
        for qt in range(NTT):
            for o in range(DC):
                w_ = wsl[o % 2]
                wcast_strip(proj_strip_ap(W["W_qkv"], 3 * D, o), w_)
                ps_ = psA if o % 2 == 0 else psB
                for g in range(DC):
                    mm(ps_[:, 0:TT], w_[:, g * P:(g + 1) * P],
                       bass.AP(xnfm_g[g], qt * TT, [[S, P], [1, TT]]),
                       writes=[ps_] if g == DC - 1 else [],
                       pre=[ps_] if g == 0 else (),
                       reads=([w_] + xnfm_g) if g == DC - 1 else
                             [w_, xnfm_g[g]],
                       start=(g == 0), stop=(g == DC - 1),
                       inc=1 if g == DC - 1 else 0)
                act(qtile[:, o * TT:(o + 1) * TT], ps_[:, 0:TT], AF.Identity,
                    bias=biases[:, o:o + 1], writes=[qtile], reads=[ps_, biases])
            flash_heads(qt, causal=False)
            out_projection(qt, W["W_o"], D, 24, add_dq=None)

    def flash_heads(qt, causal):
        nkc = SC if not causal else 4 * qt + 4
        CC = C_L0 if not causal else C_L1
        npairs = nkc // 2
        pctr = [0]

        def emit_av(h, kp, pt):
            for sub in range(2):
                kc = 2 * kp + sub
                lhsv = bass.AP(vt, kc * (NH * 65) + h * 65,
                               [[vt.shape[1], P], [1, 65]])
                mm(psAV[0:65, 0:TT], lhsv, pt[:, sub * TT:(sub + 1) * TT],
                   writes=[psAV] if kc == nkc - 1 else [],
                   pre=[psAV] if kc == 0 else (),
                   reads=[vt, pt],
                   start=(kc == 0), stop=(kc == nkc - 1),
                   inc=1 if kc == nkc - 1 else 0)

        def emit_norm(h):
            def _em_recip(E_):
                with nc.allow_low_precision(reason="softmax denom recip bf16"):
                    return E_.reciprocal(out=recipb[0:1, 0:TT],
                                         in_=tmp1[64:65, 0:TT])
            dve(_em_recip, writes=[recipb], reads=[tmp1])
            mm(psM[0:64, 0:TT], onesb[0:1, 0:64], recipb[0:1, 0:TT],
               writes=[psM], pre=[psM], reads=[recipb, onesb])
            act(bvb[0:64, 0:TT], psM[0:64, 0:TT], AF.Identity, writes=[bvb],
                reads=[psM])
            dve(lambda E_, h=h: E_.tensor_tensor(
                out=bass.AP(avfm, (64 * (h % 2)) * avfm.shape[1] + (h // 2) * TT,
                            [[avfm.shape[1], 64], [1, TT]]),
                in0=tmp1[0:64, 0:TT], in1=bvb[0:64, 0:TT], op=ALU.mult),
                writes=[avfm], reads=[tmp1, bvb])

        prev_head = None
        for h in range(NH):
            pend = None
            norm_at = min(3, npairs - 1)
            for kp in range(npairs):
                psX = psS if pctr[0] % 2 == 0 else psS2
                pctr[0] += 1
                for sub in range(2):
                    kc = 2 * kp + sub
                    lhs = bass.AP(kfm, (64 * (h % 2)) * kfm.shape[1]
                                  + (h // 2) * S + kc * P,
                                  [[kfm.shape[1], 64], [1, P]])
                    rhs = bass.AP(qtile, (64 * (h % 2)) * qtile.shape[1]
                                  + (h // 2) * TT,
                                  [[qtile.shape[1], 64], [1, TT]])
                    mm(psX[:, sub * TT:(sub + 1) * TT], lhs, rhs,
                       writes=[psX] if sub == 1 else [],
                       pre=[psX] if sub == 0 else (),
                       reads=[kfm, qtile],
                       start=True, stop=True, inc=1 if sub == 1 else 0)
                if pend is not None:
                    emit_av(h, *pend)
                if kp == norm_at and prev_head is not None:
                    emit_norm(prev_head)
                    prev_head = None
                pt = ptile if kp % 2 == 0 else ptile2
                act(pt[:, 0:1024], psX[:, 0:1024], AF.Exp, bias=-CC, scale=0.125,
                    writes=[pt], reads=[psX])
                if causal and 2 * kp + 1 >= 4 * qt:
                    for sub in range(2):
                        kc = 2 * kp + sub
                        i = kc - 4 * qt
                        if i >= 0:
                            dve(lambda E_, pt=pt, sub=sub, i=i: E_.tensor_tensor(
                                out=pt[:, sub * TT:(sub + 1) * TT],
                                in0=pt[:, sub * TT:(sub + 1) * TT],
                                in1=maskl1_s[:, i * TT:(i + 1) * TT], op=ALU.mult),
                                writes=[pt], reads=[pt, maskl1_s])
                pend = (kp, pt)
            emit_av(h, *pend)
            if prev_head is not None:
                emit_norm(prev_head)
            act(tmp1[0:65, 0:TT], psAV[0:65, 0:TT], AF.Identity,
                writes=[tmp1], reads=[psAV])
            prev_head = h
        emit_norm(prev_head)

    ptile2 = cx.sb_at("ptile2", [P, 1024], BF16, ptile, 2048)

    def out_projection(qt, wten, wcols, bias_col, add_dq):
        for o in range(DC):
            w_ = wsl[o % 2]
            wcast_strip(proj_strip_ap(wten, wcols, o), w_)
            ps_ = psA if o % 2 == 0 else psB
            for g in range(DC):
                mm(ps_[:, 0:TT], w_[:, g * P:(g + 1) * P],
                   avfm[:, g * TT:(g + 1) * TT],
                   writes=[ps_] if g == DC - 1 else [],
                   pre=[ps_] if g == 0 else (),
                   reads=[w_, avfm], start=(g == 0), stop=(g == DC - 1),
                   inc=1 if g == DC - 1 else 0)
            os_ = ostage if o % 2 == 0 else ostage2
            ts_ = tstage if o % 2 == 0 else tstage2
            act(os_[:, 0:TT], ps_[:, 0:TT], AF.Identity,
                bias=biases[:, bias_col + o:bias_col + o + 1],
                writes=[os_], reads=[ps_, biases])
            for j in range(4):
                dma("sp", ts_[:, j * P:(j + 1) * P],
                    os_[:, j * P:(j + 1) * P],
                    writes=[ts_], reads=[os_], transpose=True,
                    ch="a1" if j % 2 == 0 else "s1")
            dve(lambda E_, qt=qt, o=o, ts_=ts_: E_.tensor_tensor(
                out=bass.AP(stream, (qt * 4) * D + o * P,
                            [[stream.shape[1], P], [D, 4], [1, P]]),
                in0=bass.AP(stream, (qt * 4) * D + o * P,
                            [[stream.shape[1], P], [D, 4], [1, P]]),
                in1=ts_[:, 0:TT],
                op=ALU.add),
                writes=[stream], reads=[stream, ts_])


    hraw = cx.sb_at("hraw", [P, S + 4], BF16, vt, 0)
    hM = cx.sb_at("hM", [P, S], BF16, kfm, 0)
    hm = cx.sb_at("hm", [P, S], BF16, kfm, 4096)
    habs = cx.sb_at("habs", [P, S], BF16, kfm, 8192)
    hsb = cx.sb_at("hsb", [P, TT], BF16, vt, 4128)
    s2sb = cx.sb_at("s2sb", [16, 256], BF16, vt, 5152)
    s2row = cx.sb_at("s2row", [1, 4096], BF16, vt, 5664)
    s2bc = cx.sb_at("s2bc", [P, TT], F32, vt, 13856)
    dq2bc = cx.sb_at("dq2bc", [P, TT], F32, vt, 15904)
    hqA = cx.sb_at("hqA", [P, TT], BF16, vt, 17952)
    hqB = cx.sb_at("hqB", [P, TT], BF16, vt, 18976)
    xq2fm = cx.sb_at("xq2fm", [P, DHC * TT], BF16, kfm, 0)
    io8 = cx.sb_at("io8", [P, 8], F32, diag, 0)
    rev8 = cx.sb_at("rev8", [P, 8], F32, diag, 32)
    eidst = cx.sb_at("eidst", [P, SC], BF16, diag, 64)
    negmax = cx.sb_at("negmax", [P, 1], F32, diag, 96)
    ystage = cx.sb_at("ystage", [P, D], F32, kfm, 0)
    xt4 = cx.sb_at("xt4", [P, D], F32, kfm, 4096)
    yefm = cx.sb_at("yefm", [P, DC * TT], BF16, avfm, 0)

    def block2():
        layer_norm(quant=True, sw_col=0, dq_block=0)
        DQBLK[0] = 0
        normalize_to_fm(quant=True)
        if DBG_B2:
            xq1s = nc.dram_tensor("xq1s", [P, DC * S], BF16,
                                  kind="ExternalOutput").ap()
            hpre = nc.dram_tensor("hpre", [DHC, P, S], BF16,
                                  kind="ExternalOutput").ap()
            hcv = nc.dram_tensor("hcv", [DHC, P, S], BF16,
                                 kind="ExternalOutput").ap()
            hrawd = nc.dram_tensor("hrawd", [DHC, P, S + 4], BF16,
                                   kind="ExternalOutput").ap()
            diagd = nc.dram_tensor("diagd", [DHC, P, 384], BF16,
                                   kind="ExternalOutput").ap()
            dma("sp", xq1s, xnfm[:, :], writes=[], reads=[xnfm])
        else:
            hpre = hcv = hrawd = diagd = None
        first = [True]
        for m in range(DHC):
            w_ = wsl[m % 2]
            dma("sp", w_[:, 0:1024],
                bass.AP(wq1T_dram.tensor, m * P, [[DH, P], [P * DH, DC], [1, P]]),
                writes=[w_], reads=[], pre=[qres[0]])
            for j in range(3):
                dve(lambda E_, m=m, j=j: E_.tensor_scalar(
                    out=diag[:, j * P:(j + 1) * P],
                    in0=idenb[:, :], scalar1=convw[:, 3 * m + j:3 * m + j + 1],
                    scalar2=None, op0=ALU.mult),
                    writes=[diag], reads=[idenb, convw])
            dve(lambda E_: E_.memset(hraw[:, 0:1], 0.0), writes=[hraw], reads=[])
            dve(lambda E_: E_.memset(hraw[:, S + 1:S + 4], 0.0),
                writes=[hraw], reads=[])
            for tt in range(NTT):
                ps_ = psA if tt % 2 == 0 else psB
                for g in range(DC):
                    mm(ps_[:, 0:TT], w_[:, g * P:(g + 1) * P],
                       bass.AP(xnfm_g[g], tt * TT, [[S, P], [1, TT]]),
                       writes=[ps_] if g == DC - 1 else [],
                       pre=[ps_] if g == 0 else (),
                       reads=([w_] + xnfm_g) if g == DC - 1 else
                             [w_, xnfm_g[g]],
                       start=(g == 0), stop=(g == DC - 1),
                       inc=1 if g == DC - 1 else 0)
                act(hraw[:, 1 + tt * TT:1 + (tt + 1) * TT], ps_[:, 0:TT],
                    AF.Identity, writes=[hraw], reads=[ps_])
                if DBG_B2:
                    dma("sp", bass.AP(hpre.tensor, (m * P) * S + tt * TT,
                                      [[S, P], [1, TT]]),
                        hraw[:, 1 + tt * TT:1 + (tt + 1) * TT],
                        writes=[], reads=[hraw])
            if DBG_B2:
                dma("sp", bass.AP(hrawd.tensor, (m * P) * (S + 4),
                                  [[S + 4, P], [1, S + 4]]),
                    hraw[:, :], writes=[], reads=[hraw])
            if DBG_B2:
                cx.fence_all()
                dma("sp", bass.AP(diagd.tensor, (m * P) * 384,
                                  [[384, P], [1, 384]]),
                    diag[:, :], writes=[], reads=[diag])
            for tt in range(NTT):
                ps_ = psS
                for j in range(3):
                    mm(ps_[:, 0:TT], diag[:, j * P:(j + 1) * P],
                       hraw[:, tt * TT + j:tt * TT + j + TT],
                       writes=[ps_] if j == 2 else [],
                       pre=[ps_] if j == 0 else (),
                       reads=[diag, hraw], start=(j == 0), stop=(j == 2),
                       inc=1 if j == 2 else 0)
                if DBG_B2:
                    act(tmpb2[:, 0:TT], ps_[:, 0:TT], AF.Identity,
                        writes=[tmpb2], reads=[ps_])
                    dma("sp", bass.AP(hcv.tensor, (m * P) * S + tt * TT,
                                      [[S, P], [1, TT]]),
                        tmpb2[:, 0:TT], writes=[], reads=[tmpb2])
                act(hsb[:, 0:TT], ps_[:, 0:TT], AF.Silu,
                    bias=convb[:, m:m + 1], writes=[hsb], reads=[ps_, convb])
                dma("pop", bass.AP(h_dram.tensor, (m * P) * S + tt * TT,
                                   [[S, P], [1, TT]]),
                    hsb[:, 0:TT], writes=[], reads=[hsb, hres[SEQ[0]]],
                    pre=[hload[1 - SEQ[0]]])
                if first[0]:
                    dve(lambda E_, tt=tt: E_.tensor_copy(
                        out=hM[:, tt * TT:(tt + 1) * TT], in_=hsb[:, 0:TT]),
                        writes=[hM], reads=[hsb])
                    dve(lambda E_, tt=tt: E_.tensor_copy(
                        out=hm[:, tt * TT:(tt + 1) * TT], in_=hsb[:, 0:TT]),
                        writes=[hM], reads=[hsb])
                else:
                    dve(lambda E_, tt=tt: E_.tensor_tensor(
                        out=hM[:, tt * TT:(tt + 1) * TT],
                        in0=hM[:, tt * TT:(tt + 1) * TT], in1=hsb[:, 0:TT],
                        op=ALU.max), writes=[hM], reads=[hsb])
                    dve(lambda E_, tt=tt: E_.tensor_tensor(
                        out=hm[:, tt * TT:(tt + 1) * TT],
                        in0=hm[:, tt * TT:(tt + 1) * TT], in1=hsb[:, 0:TT],
                        op=ALU.min), writes=[hM], reads=[hsb])
            first[0] = False
        dve(lambda E_: E_.tensor_scalar(out=habs[:, :], in0=hm[:, 0:S],
                                        scalar1=-1.0, scalar2=None, op0=ALU.mult),
            writes=[hM], reads=[hM])
        dve(lambda E_: E_.tensor_tensor(out=habs[:, :], in0=habs[:, :],
                                        in1=hM[:, 0:S], op=ALU.max),
            writes=[hM], reads=[hM])
        for k in range(SC):
            tb_ = tmpb if k % 2 == 0 else tmpb2
            dma("sp", tb_[:, 0:P], habs[:, k * P:(k + 1) * P],
                writes=[tb_], reads=[hM], transpose=True,
                ch="a1" if k % 2 == 0 else "s1")
            dve(lambda E_, k=k, tb_=tb_: E_.reduce_max(
                out=sdq[:, 16 + k:17 + k],
                in_=tb_[:, 0:P], axis=AX.X),
                writes=[sdq], reads=[tb_])
        for k in range(SC):
            dve(lambda E_, k=k: E_.tensor_scalar(
                out=sdq[:, 16 + k:17 + k], in0=sdq[:, 16 + k:17 + k],
                scalar1=1e-5, scalar2=None, op0=ALU.max),
                writes=[sdq], reads=[sdq])
            dve(lambda E_, k=k: E_.tensor_scalar(
                out=sdq[:, 32 + k:33 + k], in0=sdq[:, 16 + k:17 + k],
                scalar1=1.0 / 127.0, scalar2=None, op0=ALU.mult),
                writes=[sdq], reads=[sdq])
            dve(lambda E_, k=k: E_.tensor_tensor(
                out=sdq[:, 32 + k:33 + k], in0=sdq[:, 32 + k:33 + k],
                in1=swrec[:, 1:2], op=ALU.mult), writes=[sdq], reads=[sdq, swrec])
            dve(lambda E_, k=k: E_.reciprocal(
                out=sdq[:, 16 + k:17 + k], in_=sdq[:, 16 + k:17 + k]),
                writes=[sdq], reads=[sdq])
            dve(lambda E_, k=k: E_.tensor_scalar(
                out=sdq[:, 16 + k:17 + k], in0=sdq[:, 16 + k:17 + k],
                scalar1=127.0, scalar2=None, op0=ALU.mult),
                writes=[sdq], reads=[sdq])
        tr_pe(psM[0:16, 0:P], sdq[:, 16:32], idenf[:, :],
              writes=[psM], reads=[sdq, idenf])
        act(s2sb[0:16, 0:P], psM[0:16, 0:P], AF.Identity,
            writes=[s2sb], reads=[psM])
        tr_pe(psM[0:16, 0:P], sdq[:, 32:48], idenf[:, :],
              writes=[psM], reads=[sdq, idenf])
        act(s2sb[0:16, P:2 * P], psM[0:16, 0:P], AF.Identity,
            writes=[s2sb], reads=[psM])
        dma("sp", bass.AP(s2row, 0, [[4096, 1], [128, 16], [1, P]]),
            bass.AP(s2sb, 0, [[256, 16], [1, P]]),
            writes=[s2row], reads=[s2sb], slow=True)
        dma("sp", bass.AP(s2row, 2048, [[4096, 1], [128, 16], [1, P]]),
            bass.AP(s2sb, P, [[256, 16], [1, P]]),
            writes=[s2row], reads=[s2sb], slow=True)
        for tt in range(NTT):
            for j4 in range(4):
                k = 4 * tt + j4
                mm(psS[:, j4 * P:(j4 + 1) * P], onesb[0:1, 0:P],
                   s2row[0:1, k * P:(k + 1) * P],
                   writes=[psS] if j4 == 3 else [], pre=[psS] if j4 == 0 else (),
                   reads=[onesb, s2row], inc=1 if j4 == 3 else 0)
                mm(psS[:, TT + j4 * P:TT + (j4 + 1) * P], onesb[0:1, 0:P],
                   s2row[0:1, 2048 + k * P:2048 + (k + 1) * P],
                   writes=[psS] if j4 == 3 else [], reads=[onesb, s2row],
                   inc=1 if j4 == 3 else 0)
            act(s2bc[:, 0:TT], psS[:, 0:TT], AF.Identity,
                writes=[s2bc], reads=[psS])
            act(dq2bc[:, 0:TT], psS[:, TT:2 * TT], AF.Identity,
                writes=[dq2bc], reads=[psS])
            for m in range(DHC):
                hq = hqA if m % 2 == 0 else hqB
                dma("pop", hq[:, 0:TT],
                    bass.AP(h_dram.tensor, (m * P) * S + tt * TT, [[S, P], [1, TT]]),
                    writes=[hq], reads=[hload[SEQ[0]]], pre=[hres[SEQ[0]]])
                dve(lambda E_, hq=hq: E_.tensor_tensor(
                    out=tmp2[:, 0:TT], in0=hq[:, 0:TT], in1=s2bc[:, 0:TT],
                    op=ALU.mult), writes=[tmp2], reads=[hq, s2bc])
                dve(lambda E_: E_.tensor_scalar(
                    out=tmp2[:, 0:TT], in0=tmp2[:, 0:TT], scalar1=RND,
                    scalar2=None, op0=ALU.add), writes=[tmp2], reads=[tmp2])
                dve(lambda E_: E_.tensor_copy(out=tmpi[:, 0:TT],
                                              in_=tmp2[:, 0:TT]),
                    writes=[tmpi], reads=[tmp2])
                dve(lambda E_: E_.scalar_tensor_tensor(
                    out=tmp1[:, 0:TT], in0=tmpi[:, 0:TT], scalar=0.5,
                    in1=tmp2[:, 0:TT], op0=ALU.add, op1=ALU.is_le),
                    writes=[tmp1], reads=[tmpi, tmp2])
                dve(lambda E_: E_.scalar_tensor_tensor(
                    out=tmp2[:, 0:TT], in0=tmpi[:, 0:TT], scalar=-RND,
                    in1=tmp1[:, 0:TT], op0=ALU.add, op1=ALU.add),
                    writes=[tmp2], reads=[tmpi, tmp1])
                dve(lambda E_, m=m: E_.tensor_tensor(
                    out=bass.AP(kfm, m * TT, [[kfm.shape[1], P], [1, TT]]),
                    in0=tmp2[:, 0:TT], in1=dq2bc[:, 0:TT], op=ALU.mult),
                    writes=[xq2fm], reads=[tmp2, dq2bc])
            for o in range(DC):
                for hh in range(2):
                    dma("sp", wsl[hh][:, 0:2048],
                        bass.AP(wq2T_dram.tensor, o * P + hh * 16 * P * D,
                                [[D, P], [P * D, 16], [1, P]]),
                        writes=[wsl[hh]], reads=[], pre=[qres[1]])
                ps_ = psA if o % 2 == 0 else psB
                for g in range(DHC):
                    mm(ps_[:, 0:TT], wsl[g // 16][:, (g % 16) * P:(g % 16 + 1) * P],
                       bass.AP(kfm, g * TT, [[kfm.shape[1], P], [1, TT]]),
                       writes=[ps_] if g == DHC - 1 else [],
                       pre=[ps_] if g == 0 else (),
                       reads=[wsl[0], wsl[1], xq2fm] if g == DHC - 1 else
                             [wsl[g // 16], xq2fm],
                       start=(g == 0), stop=(g == DHC - 1),
                       inc=1 if g == DHC - 1 else 0)
                os_ = ostage if o % 2 == 0 else ostage2
                ts_ = tstage if o % 2 == 0 else tstage2
                act(os_[:, 0:TT], ps_[:, 0:TT], AF.Identity,
                    writes=[os_], reads=[ps_])
                for j in range(4):
                    dma("sp", ts_[:, j * P:(j + 1) * P],
                        os_[:, j * P:(j + 1) * P],
                        writes=[ts_], reads=[os_], transpose=True,
                        ch="a1" if j % 2 == 0 else "s1")
                dve(lambda E_, tt=tt, o=o, ts_=ts_: E_.tensor_tensor(
                    out=bass.AP(stream, (tt * 4) * D + o * P,
                                [[stream.shape[1], P], [D, 4], [1, P]]),
                    in0=bass.AP(stream, (tt * 4) * D + o * P,
                                [[stream.shape[1], P], [D, 4], [1, P]]),
                    in1=ts_[:, 0:TT], op=ALU.add),
                    writes=[stream], reads=[stream, ts_])

    latbuf = cx.sb_at("latbuf", [P, 2 * S], BF16, qtile, 0)
    latq = cx.sb_at("latq", [P, 2 * TT], BF16, avfm, 0)

    def lat_proj(wname, dst, dst_off_fn):
        for lm in range(2):
            w_ = wsl[lm % 2]
            dma("pop", w_[:, 0:1024],
                bass.AP(W[wname].tensor, lm * P, [[LAT, P], [P * LAT, DC], [1, P]]),
                writes=[w_], reads=[])
            for tt in range(NTT):
                ps_ = psA if tt % 2 == 0 else psB
                for g in range(DC):
                    mm(ps_[:, 0:TT], w_[:, g * P:(g + 1) * P],
                       bass.AP(xnfm_g[g], tt * TT, [[S, P], [1, TT]]),
                       writes=[ps_] if g == DC - 1 else [],
                       pre=[ps_] if g == 0 else (),
                       reads=([w_] + xnfm_g) if g == DC - 1 else
                             [w_, xnfm_g[g]],
                       start=(g == 0), stop=(g == DC - 1),
                       inc=1 if g == DC - 1 else 0)
                act(dst_off_fn(lm, tt), ps_[:, 0:TT], AF.Identity,
                    writes=[dst], reads=[ps_])

    def block3():
        layer_norm(quant=False)
        normalize_to_fm(quant=False)
        vt_ones()
        lat_proj("wk_lat", latbuf,
                 lambda lm, tt: latbuf[:, lm * S + tt * TT:lm * S + (tt + 1) * TT])
        for o in range(DC):
            w_ = wsl[o % 2]
            dma("pop", w_[:, 0:256],
                bass.AP(W["wk_heads"].tensor, o * P, [[D, P], [P * D, 2], [1, P]]),
                writes=[w_], reads=[])
            for tt in range(NTT):
                ps_ = psA if tt % 2 == 0 else psB
                for lg in range(2):
                    mm(ps_[:, 0:TT], w_[:, lg * P:(lg + 1) * P],
                       bass.AP(latbuf, lg * S + tt * TT, [[latbuf.shape[1], P], [1, TT]]),
                       writes=[ps_] if lg == 1 else [],
                       pre=[ps_] if lg == 0 else (),
                       reads=[w_, latbuf], start=(lg == 0), stop=(lg == 1),
                       inc=1 if lg == 1 else 0)
                act(bass.AP(kfm, o * S + tt * TT, [[kfm.shape[1], P], [1, TT]]),
                    ps_[:, 0:TT], AF.Identity, writes=[kfm], reads=[ps_])
        lat_proj("wv_lat", latbuf,
                 lambda lm, tt: latbuf[:, lm * S + tt * TT:lm * S + (tt + 1) * TT])
        for vcb in range(2):
            w_ = wsl[vcb % 2]
            dma("pop", w_[:, 0:1024],
                bass.AP(W["wv_heads"].tensor, vcb * TT,
                        [[D, P], [P * D, 2], [1, TT]]),
                writes=[w_], reads=[])
            for tc in range(SC):
                ps_ = psA if tc % 2 == 0 else psB
                for lg in range(2):
                    mm(ps_[:, 0:TT],
                       bass.AP(latbuf, lg * S + tc * P, [[latbuf.shape[1], P], [1, P]]),
                       w_[:, lg * TT:(lg + 1) * TT],
                       writes=[ps_] if lg == 1 else [],
                       pre=[ps_] if lg == 0 else (),
                       reads=[w_, latbuf], start=(lg == 0), stop=(lg == 1),
                       inc=1 if lg == 1 else 0)
                dve(lambda E_, tc=tc, vcb=vcb, ps_=ps_: E_.tensor_copy(
                    out=bass.AP(vt, tc * (NH * 65) + vcb * 8 * 65,
                                [[vt.shape[1], P], [65, 8], [1, 64]]),
                    in_=ps_[:, 0:TT]),
                    writes=[vt], reads=[ps_])
        for qt in range(NTT):
            for lm in range(2):
                w_ = wsl[lm % 2]
                dma("pop", w_[:, 0:1024],
                    bass.AP(W["wq_lat"].tensor, lm * P,
                            [[LAT, P], [P * LAT, DC], [1, P]]),
                    writes=[w_], reads=[])
                ps_ = psA if lm % 2 == 0 else psB
                for g in range(DC):
                    mm(ps_[:, 0:TT], w_[:, g * P:(g + 1) * P],
                       bass.AP(xnfm_g[g], qt * TT, [[S, P], [1, TT]]),
                       writes=[ps_] if g == DC - 1 else [],
                       pre=[ps_] if g == 0 else (),
                       reads=([w_] + xnfm_g) if g == DC - 1 else
                             [w_, xnfm_g[g]],
                       start=(g == 0), stop=(g == DC - 1),
                       inc=1 if g == DC - 1 else 0)
                act(latq[:, lm * TT:(lm + 1) * TT], ps_[:, 0:TT], AF.Identity,
                    writes=[latq], reads=[ps_])
            for o in range(DC):
                w_ = wsl[o % 2]
                dma("pop", w_[:, 0:256],
                    bass.AP(W["wq_heads"].tensor, o * P,
                            [[D, P], [P * D, 2], [1, P]]),
                    writes=[w_], reads=[])
                ps_ = psA if o % 2 == 0 else psB
                for lg in range(2):
                    mm(ps_[:, 0:TT], w_[:, lg * P:(lg + 1) * P],
                       latq[:, lg * TT:(lg + 1) * TT],
                       writes=[ps_] if lg == 1 else [],
                       pre=[ps_] if lg == 0 else (),
                       reads=[w_, latq], start=(lg == 0), stop=(lg == 1),
                       inc=1 if lg == 1 else 0)
                act(qtile[:, o * TT:(o + 1) * TT], ps_[:, 0:TT], AF.Identity,
                    writes=[qtile], reads=[ps_])
            flash_heads(qt, causal=True)
            out_projection(qt, W["w_proj"], D, 32, add_dq=None)

    def route_and_spill(s):
        layer_norm(quant=False)
        pc(lambda E_: E_.iota(io8[:, :], [[1, 8]], base=0, channel_multiplier=0,
                              allow_small_or_imprecise_dtypes=True),
           writes=[diag], reads=[])
        dve(lambda E_: E_.tensor_scalar(out=rev8[:, :], in0=io8[:, :],
                                        scalar1=-1.0, scalar2=8.0,
                                        op0=ALU.mult, op1=ALU.add),
            writes=[diag], reads=[diag])
        for c in range(SC):
            xs_ = xstage if c % 2 == 0 else xstage2
            xc = stream_ap(c)
            dve(lambda E_, xc=xc, c=c, xs_=xs_: E_.tensor_scalar(
                out=xs_[:, :], in0=xc, scalar1=stats[:, c:c + 1],
                scalar2=stats[:, 16 + c:17 + c], op0=ALU.mult, op1=ALU.add),
                writes=[xs_], reads=[stream, stats])
            dma("sp", bass.AP(in_pool.tensor, (s * S + c * P) * D,
                              [[D, P], [1, D]]),
                xs_[:, :], writes=[], reads=[xs_])
            for g in range(DC):
                dma("sp", bass.AP(xnfm_g[g], c * P, [[S, P], [1, P]]),
                    bass.AP(xs_, g * P, [[xs_.shape[1], P], [1, P]]),
                    writes=[xnfm_g[g]], reads=[xs_], transpose=True,
                    ch="a1" if g % 2 == 0 else "s1")
        for c in range(SC):
            for g in range(DC):
                mm(psM[:, 0:8],
                   bass.AP(xnfm_g[g], c * P, [[S, P], [1, P]]),
                   wrs[:, g * 8:(g + 1) * 8],
                   writes=[psM] if g == DC - 1 else [],
                   pre=[psM] if g == 0 else (),
                   reads=(xnfm_g + [wrs]) if g == DC - 1 else
                         [xnfm_g[g], wrs],
                   start=(g == 0), stop=(g == DC - 1),
                   inc=1 if g == DC - 1 else 0)
            dve(lambda E_: E_.reduce_max(out=stats[:, 64:65], in_=psM[:, 0:8],
                                         axis=AX.X),
                writes=[stats], reads=[psM])
            dve(lambda E_: E_.tensor_scalar(out=negmax[:, 0:1], in0=stats[:, 64:65],
                                            scalar1=-1.0, scalar2=None,
                                            op0=ALU.mult),
                writes=[diag], reads=[stats])
            act(tmp2[:, 0:8], psM[:, 0:8], AF.Exp, bias=negmax[:, 0:1],
                accum=stats[:, 65:66], writes=[tmp2, stats], reads=[psM, diag])
            dve(lambda E_, s=s, c=c: E_.reciprocal(
                out=ptok[:, s * SC + c:s * SC + c + 1], in_=stats[:, 65:66]),
                writes=[ptok], reads=[stats])
            dve(lambda E_: E_.tensor_scalar(out=tmp1[:, 0:8], in0=psM[:, 0:8],
                                            scalar1=stats[:, 64:65], scalar2=None,
                                            op0=ALU.is_equal),
                writes=[tmp1], reads=[psM, stats])
            dve(lambda E_: E_.tensor_tensor(out=tmp1[:, 0:8], in0=tmp1[:, 0:8],
                                            in1=rev8[:, :], op=ALU.mult),
                writes=[tmp1], reads=[tmp1, diag])
            dve(lambda E_: E_.reduce_max(out=stats[:, 66:67], in_=tmp1[:, 0:8],
                                         axis=AX.X),
                writes=[stats], reads=[tmp1])
            dve(lambda E_, c=c: E_.tensor_scalar(
                out=eidst[:, c:c + 1], in0=stats[:, 66:67], scalar1=-1.0,
                scalar2=8.0, op0=ALU.mult, op1=ALU.add),
                writes=[diag], reads=[stats])
        dma("sp", bass.AP(e_own.tensor, s * S, [[1, P], [P, SC]]),
            eidst[:, 0:SC], writes=[], reads=[diag], slow=True)
        dma("sp", bass.AP(x4_dram.tensor, (s * S) * D, [[D, P], [P * D, SC], [1, D]]),
            bass.AP(stream, 0, [[stream.shape[1], P], [D, SC], [1, D]]),
            writes=[], reads=[stream, x4res])

    def moe():
        cx.fence_all()
        eres = cx.res("epool_res")
        pres = cx.res("pool_res")
        pc(lambda E_: E_.collective_compute(
            "AllGather", ALU.bypass, replica_groups=[list(range(8))],
            ins=[e_own], outs=[e_pool]), writes=[eres], reads=[], inc=1)
        cx.op("pc2", lambda E_: E_.collective_compute(
            "AllGather", ALU.bypass, replica_groups=[list(range(8))],
            ins=[in_pool], outs=[pool]), writes=[pres], reads=[], inc=1)
        dma("pop", Spool[:, :], bass.AP(e_pool.tensor, 0, [[256, P], [1, 256]]),
            writes=[Spool], reads=[eres])
        dve(lambda E_: E_.memset(zz256[:, :], 0.0), writes=[zz256], reads=[])
        dve(lambda E_: E_.memset(rnkp[:, :], 0.0), writes=[rnkp], reads=[])
        for e in range(E):
            dve(lambda E_, e=e: E_.tensor_scalar(
                out=mske[:, :], in0=Spool[:, :], scalar1=float(e), scalar2=0.0,
                op0=ALU.is_equal, op1=ALU.add, accum_out=tote[:, :]),
                writes=[mske, tote], reads=[Spool])
            dve(lambda E_, e=e: E_.tensor_copy(out=totb[:, e:e + 1], in_=tote[:, :]),
                writes=[totb], reads=[tote])
            dve(lambda E_: E_.tensor_tensor_scan(
                out=incl[:, :], data0=mske[:, :], data1=zz256[:, :], initial=0.0,
                op0=ALU.add, op1=ALU.add),
                writes=[incl], reads=[mske, zz256])
            dve(lambda E_: E_.tensor_tensor(out=incl[:, :], in0=incl[:, :],
                                            in1=mske[:, :], op=ALU.subtract),
                writes=[incl], reads=[incl, mske])
            dve(lambda E_: E_.tensor_tensor(out=incl[:, :], in0=incl[:, :],
                                            in1=mske[:, :], op=ALU.mult),
                writes=[incl], reads=[incl, mske])
            dve(lambda E_: E_.tensor_tensor(out=rnkp[:, :], in0=rnkp[:, :],
                                            in1=incl[:, :], op=ALU.add),
                writes=[rnkp], reads=[rnkp, incl])
        mm(psM[:, 0:8], triu1[:, :], totb[:, :], writes=[psM], pre=[psM],
           reads=[triu1, totb])
        dve(lambda E_: E_.tensor_copy(out=offs[:, :], in_=psM[:, 0:8]),
            writes=[offs], reads=[psM])
        for e in range(E):
            dve(lambda E_, e=e: E_.tensor_scalar(
                out=mske[:, :], in0=Spool[:, :], scalar1=float(e), scalar2=None,
                op0=ALU.is_equal),
                writes=[mske], reads=[Spool])
            dve(lambda E_, e=e: E_.tensor_scalar(
                out=incl[:, :], in0=mske[:, :], scalar1=offs[:, e:e + 1],
                scalar2=None, op0=ALU.mult),
                writes=[incl], reads=[mske, offs])
            dve(lambda E_: E_.tensor_tensor(out=rnkp[:, :], in0=rnkp[:, :],
                                            in1=incl[:, :], op=ALU.add),
                writes=[rnkp], reads=[rnkp, incl])
        dve(lambda E_: E_.tensor_scalar(out=keepp[:, :], in0=rnkp[:, :],
                                        scalar1=float(CAP) - 0.5, scalar2=None,
                                        op0=ALU.is_lt),
            writes=[keepp], reads=[rnkp])
        dve(lambda E_: E_.scalar_tensor_tensor(
            out=srcp[:, :], in0=Spool[:, :], scalar=float(CAP), in1=rnkp[:, :],
            op0=ALU.mult, op1=ALU.add),
            writes=[srcp], reads=[Spool, rnkp])
        dve(lambda E_: E_.tensor_tensor(out=srcp[:, :], in0=srcp[:, :],
                                        in1=keepp[:, :], op=ALU.mult),
            writes=[srcp], reads=[srcp, keepp])
        pc(lambda E_: E_.iota(iotap[:, :], [[1, 256]], base=0,
                              channel_multiplier=256,
                              allow_small_or_imprecise_dtypes=True),
           writes=[iotap], reads=[])
        pc(lambda E_: E_.iota(iotaj[:, :], [[1, 256]], base=CAP,
                              channel_multiplier=0,
                              allow_small_or_imprecise_dtypes=True),
           writes=[iotaj], reads=[])
        dve(lambda E_: E_.tensor_scalar(out=scat[:, :], in0=Spool[:, :],
                                        scalar1=cid_s[:, 0:1], scalar2=None,
                                        op0=ALU.is_equal),
            writes=[scat], reads=[Spool, cid_s])
        dve(lambda E_: E_.tensor_tensor(out=scat[:, :], in0=scat[:, :],
                                        in1=keepp[:, :], op=ALU.mult),
            writes=[scat], reads=[scat, keepp])
        dve(lambda E_: E_.tensor_scalar(out=mske[:, :], in0=srcp[:, :],
                                        scalar1=cid_s[:, 2:3], scalar2=None,
                                        op0=ALU.subtract),
            writes=[mske], reads=[srcp, cid_s])
        dve(lambda E_: E_.tensor_tensor(out=mske[:, :], in0=mske[:, :],
                                        in1=iotaj[:, :], op=ALU.subtract),
            writes=[mske], reads=[mske, iotaj])
        dve(lambda E_: E_.tensor_tensor(out=mske[:, :], in0=mske[:, :],
                                        in1=scat[:, :], op=ALU.mult),
            writes=[mske], reads=[mske, scat])
        dve(lambda E_: E_.tensor_tensor(out=scat[:, :], in0=mske[:, :],
                                        in1=iotaj[:, :], op=ALU.add),
            writes=[scat], reads=[scat, mske, iotaj])
        dve(lambda E_: E_.tensor_copy(out=scat32[:, :], in_=scat[:, :]),
            writes=[scat32], reads=[scat])
        dve(lambda E_: E_.memset(tmpi[:, 0:64], 0), writes=[tmpi], reads=[])
        dma("sp", bass.AP(ddisp.tensor, 0, [[34, P], [1, 34]]),
            tmpi[:, 0:34], writes=[], reads=[tmpi])
        cx.fence("p1")
        for j in range(256):
            cx.op("p1", lambda E_, j=j: E_.indirect_dma_start(
                out=ddisp, out_offset=bass.IndirectOffsetOnAxis(
                    scat32[:, j:j + 1], 0),
                in_=iotap[:, j:j + 1], in_offset=None),
                writes=[], reads=[scat32, iotap], inc=16)
        cx.fence("sp")
        for h in range(8):
            dma("sp", D2[16 * h:16 * h + 16, 0:32],
                bass.AP(ddisp.tensor, 512 * h, [[1, 16], [16, 32]]),
                writes=[D2], reads=[], slow=True)
        mm(psM[0:16, 0:256], ownsel_s[:, :], srcp[:, :], writes=[psM], pre=[psM],
           reads=[ownsel_s, srcp])
        dve(lambda E_: E_.tensor_copy(out=ownv[0:16, 0:256], in_=psM[0:16, 0:256]),
            writes=[ownv], reads=[psM])
        mm(psM[0:16, 0:256], ownsel_s[:, :], keepp[:, :], writes=[psM], pre=[psM],
           reads=[ownsel_s, keepp])
        dve(lambda E_: E_.tensor_copy(out=ownv[0:16, 256:512], in_=psM[0:16, 0:256]),
            writes=[ownv], reads=[psM])
        for g in range(2):
            tr_pe(psM[0:P, 0:16], ownv[0:16, g * P:(g + 1) * P],
                  idenf[0:16, 0:16], writes=[psM], reads=[ownv, idenf])
            dve(lambda E_, g=g: E_.tensor_copy(
                out=bass.AP(sidx, g, [[sidx.shape[1], P], [2, 16]]),
                in_=psM[0:P, 0:16]),
                writes=[sidx], reads=[psM])
            tr_pe(psM[0:P, 16:32], ownv[0:16, 256 + g * P:256 + (g + 1) * P],
                  idenf[0:16, 0:16], writes=[psM], reads=[ownv, idenf])
            dve(lambda E_, g=g: E_.tensor_copy(
                out=bass.AP(gate, g, [[gate.shape[1], P], [2, 16]]),
                in_=psM[0:P, 16:32]),
                writes=[gate], reads=[psM])
        dve(lambda E_: E_.tensor_tensor(out=gate[:, :], in0=gate[:, :],
                                        in1=ptok[:, :], op=ALU.mult),
            writes=[gate], reads=[gate, ptok])
        for st in range(8):
            xe_ = xetm if st % 2 == 0 else xetmB
            for kk in range(4):
                cx.op("p1", lambda E_, st=st, kk=kk, xe_=xe_: E_.indirect_dma_start(
                    out=xe_[:, kk * D:(kk + 1) * D], out_offset=None,
                    in_=pool,
                    in_offset=bass.IndirectOffsetOnAxis(
                        D2[:, st * 4 + kk:st * 4 + kk + 1], 0)),
                    writes=[xe_], reads=[D2, pres], inc=16)
            for g in range(DC):
                for kk in range(4):
                    dma("sp", bass.AP(xefm_g[g], kk * P, [[TT, P], [1, P]]),
                        xe_[:, kk * D + g * P:kk * D + (g + 1) * P],
                        writes=[xefm_g[g]], reads=[xe_], transpose=True,
                        ch="a1" if g % 2 == 0 else "s1")
            for m in range(DHC):
                w_ = wsl[m % 2]
                dma("pop", w_[:, 0:1024],
                    bass.AP(W["W1_my"].tensor, m * P, [[DH, P], [P * DH, DC], [1, P]]),
                    writes=[w_], reads=[])
                ps_ = psA if m % 2 == 0 else psB
                for g in range(DC):
                    mm(ps_[:, 0:TT], w_[:, g * P:(g + 1) * P],
                       bass.AP(xefm_g[g], 0, [[TT, P], [1, TT]]),
                       writes=[ps_] if g == DC - 1 else [],
                       pre=[ps_] if g == 0 else (),
                       reads=([w_] + xefm_g) if g == DC - 1 else
                             [w_, xefm_g[g]],
                       start=(g == 0), stop=(g == DC - 1),
                       inc=1 if g == DC - 1 else 0)
                act(hexp[:, m * TT:(m + 1) * TT], ps_[:, 0:TT], AF.Relu,
                    writes=[hexp], reads=[ps_])
            for o in range(DC):
                for hh in range(2):
                    dma("pop", wsl[hh][:, 0:2048],
                        bass.AP(W["W2_my"].tensor, o * P + hh * 16 * P * D,
                                [[D, P], [P * D, 16], [1, P]]),
                        writes=[wsl[hh]], reads=[])
                ps_ = psA if o % 2 == 0 else psB
                for g in range(DHC):
                    mm(ps_[:, 0:TT], wsl[g // 16][:, (g % 16) * P:(g % 16 + 1) * P],
                       hexp[:, g * TT:(g + 1) * TT],
                       writes=[ps_] if g == DHC - 1 else [],
                       pre=[ps_] if g == 0 else (),
                       reads=[wsl[0], wsl[1], hexp] if g == DHC - 1 else
                             [wsl[g // 16], hexp],
                       start=(g == 0), stop=(g == DHC - 1),
                       inc=1 if g == DHC - 1 else 0)
                act(yefm[:, o * TT:(o + 1) * TT], ps_[:, 0:TT], AF.Identity,
                    writes=[yefm], reads=[ps_])
            for g in range(DC):
                for kk in range(4):
                    dma("sp",
                        bass.AP(xe_, kk * D + g * P, [[xe_.shape[1], P], [1, P]]),
                        yefm[:, g * TT + kk * P:g * TT + (kk + 1) * P],
                        writes=[xe_], reads=[yefm], transpose=True,
                        ch="a1" if kk % 2 == 0 else "s1")
            for h in range(8):
                dma("sp",
                    bass.AP(ye_own.tensor, (512 * h + 64 * st) * D,
                            [[D, 16], [16 * D, 4], [1, D]]),
                    bass.AP(xe_, (16 * h) * xe_.shape[1],
                            [[xe_.shape[1], 16], [D, 4], [1, D]]),
                    writes=[], reads=[xe_])
        cx.fence("pc")
        yres = cx.res("yepool_res")
        pc(lambda E_: E_.collective_compute(
            "AllGather", ALU.bypass, replica_groups=[list(range(8))],
            ins=[ye_own], outs=[ye_pool]), writes=[yres], reads=[], inc=1)
        for cc in range(2 * SC):
            xe_ = xetm if cc % 2 == 0 else xetmB
            ys_ = ystage if cc % 2 == 0 else ystageB
            x4_ = xt4 if cc % 2 == 0 else xt4B
            cx.op("p1", lambda E_, cc=cc, xe_=xe_: E_.indirect_dma_start(
                out=xe_[:, 0:D], out_offset=None,
                in_=ye_pool,
                in_offset=bass.IndirectOffsetOnAxis(sidx[:, cc:cc + 1], 0)),
                writes=[xe_], reads=[sidx, yres], inc=16)
            dma("sp", x4_[:, :],
                bass.AP(x4_dram.tensor, (cc * P) * D, [[D, P], [1, D]]),
                writes=[x4_], reads=[], pre=[x4res])
            dve(lambda E_, cc=cc, xe_=xe_, ys_=ys_: E_.tensor_scalar(
                out=ys_[:, :], in0=xe_[:, 0:D], scalar1=gate[:, cc:cc + 1],
                scalar2=None, op0=ALU.mult),
                writes=[ys_], reads=[xe_, gate])
            dve(lambda E_, ys_=ys_, x4_=x4_: E_.tensor_tensor(
                out=ys_[:, :], in0=ys_[:, :], in1=x4_[:, :], op=ALU.add),
                writes=[ys_], reads=[ys_, x4_])
            dma("sp", bass.AP(y_out.tensor, (cc * P) * D, [[D, P], [1, D]]),
                ys_[:, :], writes=[], reads=[ys_])

    stage_i = STAGES.index(UPTO)
    phase_p0()
    phase_quantw()
    for s in range(NSEQ):
        if stage_i < STAGES.index("b1"):
            break
        SEQ[0] = s
        dma("sp", bass.AP(stream, 0, [[stream.shape[1], P], [D, SC], [1, D]]),
            bass.AP(x_in.tensor, (s * S) * D, [[D, P], [P * D, SC], [1, D]]),
            writes=[stream], reads=[])
        block1()
        if stage_i >= STAGES.index("b2"):
            block2()
        if stage_i >= STAGES.index("b3"):
            block3()
        if stage_i >= STAGES.index("route"):
            route_and_spill(s)
        if UPTO in ("b1", "b2", "b3"):
            break
    if stage_i >= STAGES.index("moe"):
        moe()

    tapbufs = {"stream": stream, "stats": stats, "swrec": swrec, "tmpb": tmpb,
               "xnfm": xnfm, "kfm": kfm, "vt": vt, "qtile": qtile, "avfm": avfm,
               "ptok": ptok, "gate": gate, "sidx": sidx, "consts": consts}
    for tname, sbname, shape, dt in TAPS:
        t_out = nc.dram_tensor(tname, shape, dt, kind="ExternalOutput").ap()
        buf = tapbufs[sbname]
        cx.fence("sp")
        dma("sp", t_out, bass.AP(buf, 0, [[buf.shape[1], shape[0]], [1, shape[1]]]),
            writes=[], reads=[buf])

    cx.fence("sp")
    for ch in ("sp", "s1", "s2", "a1", "pop", "p1"):
        cx.q["sp"].append(lambda E_, sem=cx.sems[ch], v=cx.count[ch]:
                          E_.wait_ge(sem, v))

    cx.emit_all()
    cx.stack.close()
    return nc


_BUILT = {}


def _get_nc():
    key = (UPTO, tuple(t[0] for t in TAPS))
    if key not in _BUILT:
        nc = build()
        lower_extended_insts(nc)
        _BUILT[key] = nc
    return _BUILT[key]


def kernel(**inputs):
    bf = ml_dtypes.bfloat16
    hc = _host_consts()
    x = np.ascontiguousarray(np.asarray(inputs["L0_attn_x"], dtype=np.float32))
    ins = []
    for c in range(8):
        m = {"x": x[2 * c:2 * c + 2]}
        m["W_qkv"] = np.asarray(inputs["L0_attn_W_qkv"], np.float32)
        m["b_qkv"] = np.asarray(inputs["L0_attn_b_qkv"], np.float32)
        m["W_o"] = np.asarray(inputs["L0_attn_W_o"], np.float32)
        m["b_o"] = np.asarray(inputs["L0_attn_b_o"], np.float32)
        m["ffn_w1"] = np.asarray(inputs["L0_ffn_w1"], np.float32)
        m["conv_w"] = np.asarray(inputs["L0_ffn_conv_w"], np.float32)
        m["conv_b"] = np.asarray(inputs["L0_ffn_conv_b"], np.float32)
        m["ffn_w2"] = np.asarray(inputs["L0_ffn_w2"], np.float32)
        for k_src, k_dst in [("L1_attn_wq_lat", "wq_lat"), ("L1_attn_wk_lat", "wk_lat"),
                             ("L1_attn_wv_lat", "wv_lat"), ("L1_attn_wq_heads", "wq_heads"),
                             ("L1_attn_wk_heads", "wk_heads"), ("L1_attn_wv_heads", "wv_heads"),
                             ("L1_attn_w_proj", "w_proj"), ("L1_attn_b_proj", "b_proj"),
                             ("L1_ffn_Wr", "Wr")]:
            m[k_dst] = np.asarray(inputs[k_src], np.float32)
        m["W1_my"] = np.asarray(inputs["L1_ffn_W1"][c], np.float32)
        m["W2_my"] = np.asarray(inputs["L1_ffn_W2"][c], np.float32)
        m.update({k: v for k, v in hc.items()})
        osel = np.zeros((P, 16), np.float32)
        for q in range(16):
            osel[16 * c + q, q] = 1.0
        m["ownsel"] = osel
        cidm = np.zeros((P, 4), np.float32)
        cidm[:, 0] = c
        cidm[:, 1] = c * T
        cidm[:, 2] = c * CAP
        m["cid"] = cidm
        ins.append(m)

    nc = _get_nc()
    res = run_bass_kernel_spmd(nc, ins, core_ids=list(range(8)))
    out = np.concatenate([np.asarray(r["y"]) for r in res.results], axis=0)
    return out.reshape(16, S, D)


# version 22 (final)
# speedup vs baseline: 1.0383x; 1.0126x over previous
import contextlib
import numpy as np
import ml_dtypes

import concourse.bass as bass
import concourse.mybir as mybir
from concourse.bass_utils import run_bass_kernel_spmd
from concourse.library_config import all_libraries
from concourse.library_overlay import lower_extended_insts

F32 = mybir.dt.float32
BF16 = mybir.dt.bfloat16
I32 = mybir.dt.int32
I16 = mybir.dt.int16
U32 = mybir.dt.uint32
AF = mybir.ActivationFunctionType
ALU = mybir.AluOpType
AX = mybir.AxisListType
LIBS = {l.name: l for l in all_libraries}

P = 128
D = 1024
S = 2048
NSEQ = 2
T = NSEQ * S
NH = 16
HD = 64
DC = D // P
SC = S // P
TT = 512
NTT = S // TT
DH = 4096
DHC = DH // P
LAT = 256
E = 8
CAP = 4096
TTOT = 8 * T
C_L0 = 24.0
C_L1 = 6.0
RND = 256.0

ENGS = ("pe", "act", "dve", "pop", "p1", "pc", "pc2",
        "sp", "s1", "s2", "a1")
ENG_ATTR = {"pe": "tensor", "act": "scalar", "dve": "vector", "pop": "gpsimd",
            "p1": "gpsimd", "pc": "gpsimd", "pc2": "gpsimd",
            "sp": "sync", "s1": "sync", "s2": "sync", "a1": "scalar"}
QKEY = {"pc": "pop", "pc2": "pop", "p1": "pop", "s1": "sp", "s2": "sp",
        "a1": "act"}
QUEUES = ("pe", "act", "dve", "pop", "sp")

STAGES = ("p0", "b1", "b2", "b3", "route", "moe", "full")
UPTO = "full"
TAPS = []
DBG_B2 = False


class _Res:

    def __init__(self, name):
        self.name = name


class Ctx:

    def __init__(self, nc):
        self.nc = nc
        self.stack = contextlib.ExitStack()
        self.sems = {e: self.stack.enter_context(nc.semaphore(f"ch_{e}"))
                     for e in ENGS}
        self.count = {e: 0 for e in ENGS}
        self.q = {e: [] for e in QUEUES}
        self.waited = {e: {e2: 0 for e2 in ENGS} for e in ENGS}
        self.track = {}
        self.alias = {}
        self.extent = {}
        self.overl = {}
        self.bounds = {"sp": [0], "s1": [0], "s2": [0], "a1": [0],
                       "pop": [0], "p1": [0]}

    def _bytes(self, shape, dt):
        import concourse.mybir as _mb
        return shape[1] * _mb.dt.size(dt)

    def _reg(self, name, space, lo, hi):
        self.track[name] = [None, []]
        self.extent[name] = (space, lo, hi)
        self.overl[name] = []
        for o, (sp2, lo2, hi2) in self.extent.items():
            if o != name and sp2 == space and lo < hi2 and lo2 < hi:
                self.overl[name].append(o)
                self.overl[o].append(name)

    def res(self, name):
        r = _Res(name)
        self._reg(name, "res:" + name, 0, 1)
        return r

    def sb(self, name, shape, dt):
        t = self.stack.enter_context(self.nc.sbuf_tensor(name, shape, dt))
        base = self.nc.lookup_mloc(t).addr
        self._reg(t.name, "sb", base, base + self._bytes(shape, dt))
        return t

    def sb_at(self, name, shape, dt, backing, byte_off):
        base = self.nc.lookup_mloc(backing).addr
        v = self.nc.alloc_sbuf_tensor_at(name, shape, dt, offset=base + byte_off)
        self._reg(v.name, "sb", base + byte_off,
                  base + byte_off + self._bytes(shape, dt))
        self.alias[v.name] = v.name
        return v

    def ps(self, name, shape, dt=F32):
        t = self.stack.enter_context(self.nc.psum_tensor(name, shape, dt))
        base = self.nc.lookup_mloc(t).addr
        self._reg(t.name, "ps", base, base + self._bytes(shape, dt))
        return t

    def ps_at(self, name, shape, dt, backing, byte_off):
        v = self.nc.virtual_psum_tensor(name, shape, dt)
        self.nc.back_virtual_tensor_with(v, backing)
        base = self.nc.lookup_mloc(backing).addr
        self._reg(v.name, "ps", base + byte_off,
                  base + byte_off + self._bytes(shape, dt))
        return v

    def _req(self, eng, tok):
        if tok is None:
            return
        pe, pv = tok
        if pe in self.bounds:
            bs = self.bounds[pe]
            b = None
            for x in bs:
                if x >= pv:
                    b = x
                    break
            if b is None:
                cur = self.count[pe]
                sem0 = self.sems[pe]
                if self.waited[pe][pe] < cur:
                    self.waited[pe][pe] = cur
                    self.q[QKEY.get(pe, pe)].append(
                        lambda E_, sem0=sem0, cur=cur: E_.wait_ge(sem0, cur))
                bs.append(cur)
                b = cur
            pv = b
            if QKEY.get(pe, pe) == QKEY.get(eng, eng):
                return
        if self.waited[eng][pe] >= pv:
            return
        self.waited[eng][pe] = pv
        sem = self.sems[pe]
        self.q[QKEY.get(eng, eng)].append(
            lambda E_, sem=sem, pv=pv: E_.wait_ge(sem, pv))

    def _names(self, b):
        return (b.name,) + tuple(self.overl.get(b.name, ()))

    def op(self, eng, emit, writes=(), reads=(), inc=1, pre=()):
        for b in reads:
            for n in self._names(b):
                self._req(eng, self.track[n][0])
        for b in tuple(pre) + tuple(writes):
            for n in self._names(b):
                wt, rts = self.track[n]
                self._req(eng, wt)
                for t in rts:
                    self._req(eng, t)
        tok = None
        if inc:
            self.count[eng] += inc
            tok = (eng, self.count[eng])
            sem = self.sems[eng]
            self.q[QKEY.get(eng, eng)].append(
                lambda E_, emit=emit, sem=sem, inc=inc:
                emit(E_).then_inc(sem, inc))
        else:
            self.q[QKEY.get(eng, eng)].append(lambda E_, emit=emit: emit(E_))
        for b in writes:
            self.track[b.name] = [tok, []]
        if inc:
            for b in reads:
                self.track[b.name][1].append(tok)
        return tok

    def fence(self, eng):
        for e2 in ENGS:
            if e2 != eng:
                self._req(eng, (e2, self.count[e2]))

    def fence_all(self):
        for e in ENGS:
            self.fence(e)

    def emit_all(self):
        with self.nc.Block() as block:
            for e in QUEUES:
                fns = list(self.q[e])

                def make(fns):
                    def body(EE):
                        for fn in fns:
                            fn(EE)
                    return body
                getattr(block, ENG_ATTR[e])(make(fns))


def _host_consts():
    bf = ml_dtypes.bfloat16
    iden = np.eye(P, dtype=np.float32)
    mask = np.zeros((P, 4 * TT), np.float32)
    for i in range(4):
        for k in range(P):
            lo = 128 * i + k
            if lo < TT:
                mask[k, i * TT + lo:(i + 1) * TT] = 1.0
    return {
        "iden_b": iden.astype(bf),
        "iden_f": iden.copy(),
        "triu1_b": np.triu(np.ones((P, P), np.float32), 1).astype(bf),
        "maskl1_b": mask.astype(bf),
    }


def build():
    nc = bass.Bass("TRN2", target_bir_lowering=False)
    cx = Ctx(nc)

    x_in = nc.dram_tensor("x", [NSEQ, S, D], F32, kind="ExternalInput").ap()
    y_out = nc.dram_tensor("y", [NSEQ, S, D], F32, kind="ExternalOutput").ap()
    W = {}
    for name, shape in [
        ("W_qkv", [D, 3 * D]), ("b_qkv", [3 * D]), ("W_o", [D, D]), ("b_o", [D]),
        ("ffn_w1", [DH, D]), ("conv_w", [DH, 1, 3]), ("conv_b", [DH]),
        ("ffn_w2", [D, DH]),
        ("wq_lat", [D, LAT]), ("wk_lat", [D, LAT]), ("wv_lat", [D, LAT]),
        ("wq_heads", [LAT, D]), ("wk_heads", [LAT, D]), ("wv_heads", [LAT, D]),
        ("w_proj", [D, D]), ("b_proj", [D]),
        ("Wr", [D, E]), ("W1_my", [D, DH]), ("W2_my", [DH, D]),
    ]:
        W[name] = nc.dram_tensor(name, shape, F32, kind="ExternalInput").ap()
    HC = {}
    for name, dt, shape in [
        ("iden_b", BF16, [P, P]), ("iden_f", F32, [P, P]),
        ("triu1_b", BF16, [P, P]), ("maskl1_b", BF16, [P, 4 * TT]),
        ("ownsel", F32, [P, 16]), ("cid", F32, [P, 4]),
    ]:
        HC[name] = nc.dram_tensor(name, shape, dt, kind="ExternalInput").ap()

    x4_dram = nc.dram_tensor("x4s", [T, D], F32).ap()
    h_dram = nc.dram_tensor("hs", [DHC, P, S], BF16).ap()
    wq1T_dram = nc.dram_tensor("wq1T", [D, DH], BF16).ap()
    wq2T_dram = nc.dram_tensor("wq2T", [DH, D], BF16).ap()
    in_pool = nc.dram_tensor("in_pool", [T, D], BF16).ap()
    pool = nc.dram_tensor("pool", [TTOT, D], BF16).ap()
    e_own = nc.dram_tensor("e_own", [T, 1], BF16).ap()
    e_pool = nc.dram_tensor("e_pool", [TTOT, 1], BF16).ap()
    ye_own = nc.dram_tensor("ye_own", [CAP, D], BF16).ap()
    ddisp = nc.dram_tensor("ddisp", [CAP + 256, 1], I32).ap()
    ye_pool = nc.dram_tensor("ye_pool", [8 * CAP, D], BF16).ap()

    sb = cx.sb
    stream = sb("stream", [P, SC * D], F32)
    xnfm = sb("xnfm", [P, DC * S], BF16)
    kfm = sb("kfm", [P, DC * S], BF16)
    vt = sb("vt", [P, SC * NH * 65], BF16)
    qtile = sb("qtile", [P, DC * TT], BF16)
    avfm = sb("avfm", [P, DC * TT], BF16)
    wring = sb("wring", [P, 4096], BF16)
    ptile = sb("ptile", [P, 2 * 1024], BF16)
    tmp1 = sb("tmp1", [P, 512], F32)
    tmp2 = sb("tmp2", [P, 512], F32)
    tmpi = sb("tmpi", [P, 512], I32)
    tmpb = sb("tmpb", [P, 512], BF16)
    tmpb2 = sb("tmpb2", [P, 512], BF16)
    maskl1_s = sb("maskl1_s", [P, 4 * TT], BF16)
    bcastb = sb("bcastb", [P, TT], BF16)
    onesb = sb("onesb", [P, P], BF16)
    onesf = sb("onesf", [P, P], F32)
    stats = sb("stats", [P, 96], F32)
    ptok = sb("ptok", [P, NSEQ * SC], F32)
    gate = sb("gate", [P, NSEQ * SC], F32)
    sidx = sb("sidx", [P, NSEQ * SC], I32)
    consts = sb("consts", [P, 16], F32)
    idenb = sb("idenb", [P, P], BF16)
    idenf = sb("idenf", [P, P], F32)
    triu1 = sb("triu1", [P, P], BF16)
    ownsel_s = sb("ownsel_s", [P, 16], F32)
    cid_s = sb("cid_s", [P, 4], F32)
    biases = sb("biases", [P, 48], F32)
    convw = sb("convw", [P, 3 * DHC], F32)
    convb = sb("convb", [P, DHC], F32)
    wrs = sb("wrs", [P, DC * E], BF16)
    diag = sb("diag", [P, 3 * P], BF16)
    swrec = sb("swrec", [P, 2], F32)
    sdq = sb("sdq", [P, 4 * SC], F32)
    lnsq = cx.sb_at("lnsq", [P, D], F32, qtile, 0)
    qw = [cx.sb_at(f"qw{i}", [P, 512], F32, vt, 2048 * i) for i in range(4)]
    qt1 = cx.sb_at("qt1", [P, 512], F32, vt, 8192)
    qt2 = cx.sb_at("qt2", [P, 512], F32, vt, 10240)
    qti = cx.sb_at("qti", [P, 512], I32, vt, 12288)
    qb = [cx.sb_at(f"qb{i}", [P, 512], BF16, vt, 14336 + 1024 * i)
          for i in range(2)]
    qc = [cx.sb_at(f"qc{i}", [P, 512], BF16, vt, 16384 + 1024 * i)
          for i in range(2)]
    qstats = cx.sb_at("qstats", [P, 8], F32, vt, 18432)
    qcq = [[cx.sb_at(f"qcq{i}_{g}", [P, 128], BF16, vt, 16384 + 1024 * i + 256 * g)
            for g in range(4)] for i in range(2)]

    iotap = cx.sb_at("iotap", [P, 256], I32, vt, 0)
    scat = cx.sb_at("scat", [P, 256], F32, vt, 1024)
    scat32 = cx.sb_at("scat32", [P, 256], I32, vt, 2048)
    iotaj = cx.sb_at("iotaj", [P, 256], F32, vt, 3072)
    xetmB = cx.sb_at("xetmB", [P, 4 * D], BF16, vt, 4096)
    ystageB = cx.sb_at("ystageB", [P, D], F32, kfm, 8192)
    xt4B = cx.sb_at("xt4B", [P, D], F32, kfm, 12288)
    hexp = cx.sb_at("hexp", [P, DHC * TT], BF16, kfm, 0)
    xefm = cx.sb_at("xefm", [P, DC * TT], BF16, xnfm, 0)
    xetm = cx.sb_at("xetm", [P, 4 * D], BF16, xnfm, 8192)
    Spool = cx.sb_at("Spool", [P, TTOT // P], F32, xnfm, 16384)
    mske = cx.sb_at("mske", [P, 256], F32, xnfm, 17408)
    incl = cx.sb_at("incl", [P, 256], F32, xnfm, 18432)
    rnkp = cx.sb_at("rnkp", [P, 256], F32, xnfm, 19456)
    srcp = cx.sb_at("srcp", [P, 256], F32, xnfm, 20480)
    keepp = cx.sb_at("keepp", [P, 256], F32, xnfm, 21504)
    zz256 = cx.sb_at("zz256", [P, 256], F32, xnfm, 22528)
    ownv = cx.sb_at("ownv", [16, 2 * 256], F32, xnfm, 23552)
    totb = cx.sb_at("totb", [P, E], BF16, xnfm, 30208)
    offs = cx.sb_at("offs", [P, E], F32, xnfm, 30240)
    tote = cx.sb_at("tote", [P, 1], F32, xnfm, 30272)
    D2 = cx.sb_at("D2", [P, CAP // P], I32, xnfm, 30336)

    psS = cx.ps("psS", [P, 1024], F32)
    psS2 = cx.ps("psS2", [P, 1024], F32)
    psAV = cx.ps("psAV", [P, TT], F32)
    psA = cx.ps("psA", [P, TT], F32)
    psB = cx.ps("psB", [P, TT], F32)
    psM = cx.ps("psM", [P, TT], F32)


    hres = [cx.res("hres0"), cx.res("hres1")]
    hload = [cx.res("hload0"), cx.res("hload1")]
    qres = [cx.res("qres1"), cx.res("qres2")]
    x4res = cx.res("x4res")
    SEQ = [0]

    KNOWN_CONSTS = [-C_L0, -C_L1, 0.125, 1e-5, RND, -RND, 127.0, -1.0,
                    1.0 / D, 0.5, -0.5, 2.0]
    for i, v in enumerate(KNOWN_CONSTS):
        nc.const_aps.register(F32, v, consts[:, i:i + 1])

    def dma(qeng, out_ap, in_ap, writes, reads, slow=False, transpose=False,
            pre=(), ch=None):
        if ch is not None:
            qeng = ch
        elif qeng == "sp":
            if transpose:
                qeng = "s1"
            else:
                t = getattr(out_ap, "tensor", None)
                if t is not None and "DRam" in type(t).__name__:
                    qeng = "s2"

        def em(E_):
            return E_.dma_start(out=out_ap, in_=in_ap, transpose=transpose,
                                allow_slow_non_contiguous=slow)
        return cx.op(qeng, em, writes=writes, reads=reads, inc=16, pre=pre)

    def mm(out_ap, lhsT_ap, rhs_ap, writes, reads, start=True, stop=True, inc=1,
           pre=()):
        def em(E_):
            return E_.matmul(out_ap, lhsT_ap, rhs_ap, start=start, stop=stop)
        return cx.op("pe", em, writes=writes, reads=reads, inc=inc, pre=pre)

    def tr_pe(out_ap, in_ap, ident_ap, writes, reads):
        def em(E_):
            return E_.transpose(out=out_ap, in_=in_ap, identity=ident_ap)
        return cx.op("pe", em, writes=writes, reads=reads, inc=1)

    def act(out_ap, in_ap, func, writes, reads, bias=0.0, scale=1.0, accum=None):
        def em(E_):
            return E_.activation(out=out_ap, in_=in_ap, func=func, bias=bias,
                                 scale=scale, accum_out=accum)
        return cx.op("act", em, writes=writes, reads=reads, inc=1)

    def dve(emit, writes, reads, inc=1):
        return cx.op("dve", emit, writes=writes, reads=reads, inc=inc)

    def pop(emit, writes, reads, inc=1):
        return cx.op("pop", emit, writes=writes, reads=reads, inc=inc)

    def pc(emit, writes, reads, inc=1):
        return cx.op("pc", emit, writes=writes, reads=reads, inc=inc)

    def stream_ap(c, w=D):
        return bass.AP(stream, c * D, [[stream.shape[1], P], [1, w]])

    def quant_round(dst_bf, src_ap, sa, sb_plus_rnd, width, post=None,
                    wlist=(), rlist=()):
        dve(lambda E_: E_.tensor_scalar(
            out=tmp2[:, 0:width], in0=src_ap, scalar1=sa, scalar2=sb_plus_rnd,
            op0=ALU.mult, op1=ALU.add),
            writes=[tmp2], reads=list(rlist) + [stats])
        dve(lambda E_: E_.tensor_copy(out=tmpi[:, 0:width], in_=tmp2[:, 0:width]),
            writes=[tmpi], reads=[tmp2])
        dve(lambda E_: E_.scalar_tensor_tensor(
            out=tmp1[:, 0:width], in0=tmpi[:, 0:width], scalar=0.5,
            in1=tmp2[:, 0:width], op0=ALU.add, op1=ALU.is_le),
            writes=[tmp1], reads=[tmpi, tmp2])
        if post is None:
            dve(lambda E_: E_.scalar_tensor_tensor(
                out=dst_bf, in0=tmpi[:, 0:width], scalar=-RND,
                in1=tmp1[:, 0:width], op0=ALU.add, op1=ALU.add),
                writes=list(wlist), reads=[tmpi, tmp1])
        else:
            dve(lambda E_: E_.scalar_tensor_tensor(
                out=tmp2[:, 0:width], in0=tmpi[:, 0:width], scalar=-RND,
                in1=tmp1[:, 0:width], op0=ALU.add, op1=ALU.add),
                writes=[tmp2], reads=[tmpi, tmp1])
            dve(lambda E_: E_.tensor_scalar(
                out=dst_bf, in0=tmp2[:, 0:width], scalar1=post, scalar2=None,
                op0=ALU.mult),
                writes=list(wlist), reads=[tmp2, stats])

    def phase_p0():
        for i, v in enumerate(KNOWN_CONSTS):
            dve(lambda E_, v=v, i=i: E_.memset(consts[:, i:i + 1], v),
                writes=[consts], reads=[])
        dve(lambda E_: E_.memset(onesb[:, :], 1.0), writes=[onesb], reads=[])
        dve(lambda E_: E_.memset(onesf[:, :], 1.0), writes=[onesf], reads=[])
        for name, buf in [("iden_b", idenb), ("iden_f", idenf),
                          ("triu1_b", triu1), ("maskl1_b", maskl1_s),
                          ("ownsel", ownsel_s), ("cid", cid_s)]:
            dma("sp", buf[:, :], HC[name], writes=[buf], reads=[])
        dma("sp", bass.AP(biases, 0, [[biases.shape[1], P], [1, 24]]),
            bass.AP(W["b_qkv"].tensor, 0, [[1, P], [P, 24]]),
            writes=[biases], reads=[], slow=True)
        dma("sp", bass.AP(biases, 24, [[biases.shape[1], P], [1, 8]]),
            bass.AP(W["b_o"].tensor, 0, [[1, P], [P, 8]]),
            writes=[biases], reads=[], slow=True)
        dma("sp", bass.AP(biases, 32, [[biases.shape[1], P], [1, 8]]),
            bass.AP(W["b_proj"].tensor, 0, [[1, P], [P, 8]]),
            writes=[biases], reads=[], slow=True)
        dma("sp", bass.AP(convw, 0, [[convw.shape[1], P], [3, DHC], [1, 3]]),
            bass.AP(W["conv_w"].tensor, 0, [[3, P], [3 * P, DHC], [1, 3]]),
            writes=[convw], reads=[], slow=True)
        dma("sp", bass.AP(convb, 0, [[convb.shape[1], P], [1, DHC]]),
            bass.AP(W["conv_b"].tensor, 0, [[1, P], [P, DHC]]),
            writes=[convb], reads=[], slow=True)
        dma("pop", bass.AP(wrs, 0, [[wrs.shape[1], P], [E, DC], [1, E]]),
            bass.AP(W["Wr"].tensor, 0, [[E, P], [P * E, DC], [1, E]]),
            writes=[wrs], reads=[])
        vt_ones()

    def vt_ones():
        dve(lambda E_: E_.memset(
            bass.AP(vt, 64, [[vt.shape[1], P], [NH * 65, SC], [65, NH]]), 1.0),
            writes=[vt], reads=[])

    def quant_weight(w_ap, rows, cols, wqT_dram, sw_col):
        nstrip = rows // P
        CW = 512
        ncb = cols // CW
        cw = CW
        dve(lambda E_: E_.memset(qstats[:, 0:1], 0.0), writes=[qstats], reads=[])
        it = 0
        for m in range(nstrip):
            for cb in range(ncb):
                qw_ = qw[it % 4]
                it += 1
                dma("sp", qw_[:, 0:cw],
                    bass.AP(w_ap.tensor, (m * P) * cols + cb * cw,
                            [[cols, P], [1, cw]]),
                    writes=[qw_], reads=[])
                dve(lambda E_, qw_=qw_: E_.tensor_scalar(
                    out=qt2[:, 0:cw], in0=qw_[:, 0:cw], scalar1=-1.0,
                    scalar2=None, op0=ALU.mult), writes=[qt2], reads=[qw_])
                dve(lambda E_, qw_=qw_: E_.tensor_tensor(
                    out=qt2[:, 0:cw], in0=qw_[:, 0:cw], in1=qt2[:, 0:cw],
                    op=ALU.max), writes=[qt2], reads=[qw_, qt2])
                dve(lambda E_: E_.reduce_sum(
                    out=qstats[:, 1:2], in_=qt2[:, 0:cw], axis=AX.X),
                    writes=[qstats], reads=[qt2])
                dve(lambda E_: E_.tensor_tensor(
                    out=qstats[:, 0:1], in0=qstats[:, 0:1],
                    in1=qstats[:, 1:2], op=ALU.add), writes=[qstats],
                    reads=[qstats])
        mm(psM[0:1, 0:1], onesf[:, 0:1], qstats[:, 0:1],
           writes=[psM], pre=[psM], reads=[qstats, onesf])
        dve(lambda E_: E_.tensor_scalar(
            out=qstats[0:1, 2:3], in0=psM[0:1, 0:1],
            scalar1=1.0 / (rows * cols), scalar2=1e-5, op0=ALU.mult, op1=ALU.max),
            writes=[qstats], reads=[psM])
        mm(psM[:, 1:2], onesf[0:1, :], qstats[0:1, 2:3],
           writes=[psM], pre=[psM], reads=[qstats, onesf])
        dve(lambda E_, sw_col=sw_col: E_.tensor_copy(
            out=swrec[:, sw_col:sw_col + 1], in_=psM[:, 1:2]),
            writes=[swrec], reads=[psM])
        dve(lambda E_: E_.reciprocal(out=qstats[:, 3:4],
                                     in_=swrec[:, sw_col:sw_col + 1]),
            writes=[qstats], reads=[swrec])
        for m in range(nstrip):
            for cb in range(ncb):
                qw_ = qw[it % 4]
                qb_ = qb[it % 2]
                qcq_ = qcq[it % 2]
                it += 1
                dma("sp", qw_[:, 0:cw],
                    bass.AP(w_ap.tensor, (m * P) * cols + cb * cw,
                            [[cols, P], [1, cw]]),
                    writes=[qw_], reads=[])
                dve(lambda E_, qw_=qw_: E_.tensor_scalar(
                    out=qt2[:, 0:cw], in0=qw_[:, 0:cw],
                    scalar1=qstats[:, 3:4], scalar2=RND,
                    op0=ALU.mult, op1=ALU.add), writes=[qt2],
                    reads=[qw_, qstats])
                dve(lambda E_: E_.tensor_copy(out=qti[:, 0:cw], in_=qt2[:, 0:cw]),
                    writes=[qti], reads=[qt2])
                dve(lambda E_: E_.scalar_tensor_tensor(
                    out=qt1[:, 0:cw], in0=qti[:, 0:cw], scalar=0.5,
                    in1=qt2[:, 0:cw], op0=ALU.add, op1=ALU.is_le),
                    writes=[qt1], reads=[qti, qt2])
                dve(lambda E_: E_.scalar_tensor_tensor(
                    out=qt2[:, 0:cw], in0=qti[:, 0:cw], scalar=-RND,
                    in1=qt1[:, 0:cw], op0=ALU.add, op1=ALU.add),
                    writes=[qt2], reads=[qti, qt1])
                dve(lambda E_, qb_=qb_: E_.tensor_scalar(
                    out=qb_[:, 0:cw], in0=qt2[:, 0:cw], scalar1=1.0,
                    scalar2=-1.0, op0=ALU.min, op1=ALU.max),
                    writes=[qb_], reads=[qt2])
                for g in range(cw // P):
                    dma("sp", qcq_[g][:, :],
                        qb_[:, g * P:(g + 1) * P],
                        writes=[qcq_[g]], reads=[qb_], transpose=True,
                        ch="a1" if g % 2 == 0 else "s1")
                for g in range(cw // P):
                    dma("pop",
                        bass.AP(wqT_dram.tensor, (cb * cw + g * P) * rows + m * P,
                                [[rows, P], [1, P]]),
                        qcq_[g][:, :],
                        writes=[], reads=[qcq_[g], qres[sw_col]])

    def phase_quantw():
        quant_weight(W["ffn_w1"], DH, D, wq1T_dram, 0)
        quant_weight(W["ffn_w2"], D, DH, wq2T_dram, 1)


    wringA = cx.sb_at("wringA", [P, 2048], BF16, wring, 0)
    wringB = cx.sb_at("wringB", [P, 2048], BF16, wring, 4096)
    wsl = (wringA, wringB)
    xnfm_g = [cx.sb_at(f"xnfm_g{g}", [P, S], BF16, xnfm, g * 2 * S)
              for g in range(DC)]
    xefm_g = [cx.sb_at(f"xefm_g{g}", [P, TT], BF16, xnfm, g * 2 * TT)
              for g in range(DC)]
    xstage = cx.sb_at("xstage", [P, D], BF16, avfm, 0)
    xstage2 = cx.sb_at("xstage2", [P, D], BF16, avfm, 2048)
    ostage = cx.sb_at("ostage", [P, TT], BF16, ptile, 0)
    tstage = cx.sb_at("tstage", [P, TT], BF16, ptile, 2048)
    ostage2 = cx.sb_at("ostage2", [P, TT], BF16, ptile, 1024)
    tstage2 = cx.sb_at("tstage2", [P, TT], BF16, ptile, 3072)
    bvrow = cx.sb_at("bvrow", [1, TT], BF16, tmpb, 0)
    bvb = cx.sb_at("bvb", [P, TT], BF16, tmpb2, 0)
    recipb = cx.sb_at("recipb", [1, TT], BF16, tmpb, 0)

    def layer_norm(quant=False, sw_col=0, dq_block=0):
        for c in range(SC):
            xc = stream_ap(c)
            dve(lambda E_, xc=xc: E_.reduce_sum(out=stats[:, 80:81], in_=xc,
                                                axis=AX.X),
                writes=[stats], reads=[stream])
            dve(lambda E_, xc=xc: E_.scalar_tensor_tensor(
                out=lnsq[:, 0:D], in0=xc, scalar=1.0, in1=xc,
                op0=ALU.mult, op1=ALU.mult, accum_out=stats[:, 81:82]),
                writes=[lnsq, stats], reads=[stream])
            if quant:
                dve(lambda E_, xc=xc: E_.reduce_max(out=stats[:, 82:83], in_=xc,
                                                    axis=AX.X),
                    writes=[stats], reads=[stream])
                dve(lambda E_, xc=xc: E_.tensor_reduce(out=stats[:, 83:84], in_=xc,
                                                       axis=AX.X, op=ALU.min),
                    writes=[stats], reads=[stream])
            dve(lambda E_: E_.tensor_scalar(out=stats[:, 84:85], in0=stats[:, 80:81],
                                            scalar1=1.0 / D, scalar2=None,
                                            op0=ALU.mult),
                writes=[stats], reads=[stats])
            dve(lambda E_: E_.tensor_scalar(out=stats[:, 85:86], in0=stats[:, 81:82],
                                            scalar1=1.0 / D, scalar2=None,
                                            op0=ALU.mult),
                writes=[stats], reads=[stats])
            dve(lambda E_: E_.scalar_tensor_tensor(
                out=stats[:, 86:87], in0=stats[:, 84:85], scalar=1.0,
                in1=stats[:, 84:85], op0=ALU.mult, op1=ALU.mult),
                writes=[stats], reads=[stats])
            dve(lambda E_: E_.tensor_tensor(out=stats[:, 86:87], in0=stats[:, 85:86],
                                            in1=stats[:, 86:87], op=ALU.subtract),
                writes=[stats], reads=[stats])
            act(stats[:, 87:88], stats[:, 86:87], AF.Sqrt, bias=1e-5, scale=1.0,
                writes=[stats], reads=[stats])
            dve(lambda E_: E_.reciprocal(out=stats[:, 88:89], in_=stats[:, 87:88]),
                writes=[stats], reads=[stats])
            dve(lambda E_, c=c: E_.tensor_copy(out=stats[:, c:c + 1],
                                               in_=stats[:, 88:89]),
                writes=[stats], reads=[stats])
            dve(lambda E_, c=c: E_.scalar_tensor_tensor(
                out=stats[:, 16 + c:17 + c], in0=stats[:, 84:85], scalar=-1.0,
                in1=stats[:, 88:89], op0=ALU.mult, op1=ALU.mult),
                writes=[stats], reads=[stats])
            if quant:
                dve(lambda E_: E_.tensor_tensor(out=stats[:, 89:90],
                                                in0=stats[:, 82:83],
                                                in1=stats[:, 84:85],
                                                op=ALU.subtract),
                    writes=[stats], reads=[stats])
                dve(lambda E_: E_.tensor_tensor(out=stats[:, 90:91],
                                                in0=stats[:, 84:85],
                                                in1=stats[:, 83:84],
                                                op=ALU.subtract),
                    writes=[stats], reads=[stats])
                dve(lambda E_: E_.tensor_tensor(out=stats[:, 89:90],
                                                in0=stats[:, 89:90],
                                                in1=stats[:, 90:91], op=ALU.max),
                    writes=[stats], reads=[stats])
                dve(lambda E_: E_.tensor_tensor(out=stats[:, 89:90],
                                                in0=stats[:, 89:90],
                                                in1=stats[:, 88:89], op=ALU.mult),
                    writes=[stats], reads=[stats])
                dve(lambda E_: E_.tensor_scalar(out=stats[:, 89:90],
                                                in0=stats[:, 89:90], scalar1=1e-5,
                                                scalar2=None, op0=ALU.max),
                    writes=[stats], reads=[stats])
                dve(lambda E_: E_.reciprocal(out=stats[:, 90:91],
                                             in_=stats[:, 89:90]),
                    writes=[stats], reads=[stats])
                dve(lambda E_: E_.tensor_scalar(out=stats[:, 91:92],
                                                in0=stats[:, 90:91], scalar1=127.0,
                                                scalar2=None, op0=ALU.mult),
                    writes=[stats], reads=[stats])
                dve(lambda E_, c=c: E_.tensor_tensor(
                    out=stats[:, 32 + c:33 + c], in0=stats[:, c:c + 1],
                    in1=stats[:, 91:92], op=ALU.mult),
                    writes=[stats], reads=[stats])
                dve(lambda E_, c=c: E_.scalar_tensor_tensor(
                    out=stats[:, 48 + c:49 + c], in0=stats[:, 16 + c:17 + c],
                    scalar=1.0, in1=stats[:, 91:92], op0=ALU.mult, op1=ALU.mult),
                    writes=[stats], reads=[stats])
                dve(lambda E_, c=c: E_.tensor_scalar(
                    out=stats[:, 48 + c:49 + c], in0=stats[:, 48 + c:49 + c],
                    scalar1=RND, scalar2=None, op0=ALU.add),
                    writes=[stats], reads=[stats])
                dve(lambda E_, c=c: E_.tensor_scalar(
                    out=stats[:, 92:93], in0=stats[:, 89:90], scalar1=1.0 / 127.0,
                    scalar2=None, op0=ALU.mult), writes=[stats], reads=[stats])
                dve(lambda E_, c=c, sw_col=sw_col, dq_block=dq_block:
                    E_.tensor_tensor(
                        out=sdq[:, dq_block * SC + c:dq_block * SC + c + 1],
                        in0=stats[:, 92:93], in1=swrec[:, sw_col:sw_col + 1],
                        op=ALU.mult),
                    writes=[sdq], reads=[stats, swrec])

    DQBLK = [0]

    def normalize_to_fm(quant=False):
        for c in range(SC):
            xs_ = xstage if c % 2 == 0 else xstage2
            xc = stream_ap(c)
            if not quant:
                dve(lambda E_, xc=xc, c=c, xs_=xs_: E_.tensor_scalar(
                    out=xs_[:, :], in0=xc, scalar1=stats[:, c:c + 1],
                    scalar2=stats[:, 16 + c:17 + c], op0=ALU.mult, op1=ALU.add),
                    writes=[xs_], reads=[stream, stats])
            else:
                for half in range(2):
                    sl = slice(half * TT, (half + 1) * TT)
                    xs_ap = bass.AP(stream, c * D + half * TT,
                                    [[stream.shape[1], P], [1, TT]])
                    dve(lambda E_, xs_ap=xs_ap, c=c: E_.tensor_scalar(
                        out=tmp2[:, 0:TT], in0=xs_ap, scalar1=stats[:, 32 + c:33 + c],
                        scalar2=stats[:, 48 + c:49 + c], op0=ALU.mult, op1=ALU.add),
                        writes=[tmp2], reads=[stream, stats])
                    dve(lambda E_: E_.tensor_copy(out=tmpi[:, 0:TT],
                                                  in_=tmp2[:, 0:TT]),
                        writes=[tmpi], reads=[tmp2])
                    dve(lambda E_: E_.scalar_tensor_tensor(
                        out=tmp1[:, 0:TT], in0=tmpi[:, 0:TT], scalar=0.5,
                        in1=tmp2[:, 0:TT], op0=ALU.add, op1=ALU.is_le),
                        writes=[tmp1], reads=[tmpi, tmp2])
                    dve(lambda E_, sl=sl: E_.scalar_tensor_tensor(
                        out=tmp2[:, 0:TT], in0=tmpi[:, 0:TT], scalar=-RND,
                        in1=tmp1[:, 0:TT], op0=ALU.add, op1=ALU.add),
                        writes=[tmp2], reads=[tmpi, tmp1])
                    dve(lambda E_, sl=sl, c=c, xs_=xs_: E_.tensor_scalar(
                        out=xs_[:, sl], in0=tmp2[:, 0:TT],
                        scalar1=sdq[:, DQBLK[0] * SC + c:DQBLK[0] * SC + c + 1],
                        scalar2=None, op0=ALU.mult),
                        writes=[xs_], reads=[tmp2, sdq])
            for g in range(DC):
                dma("sp", bass.AP(xnfm_g[g], c * P, [[S, P], [1, P]]),
                    bass.AP(xs_, g * P, [[xs_.shape[1], P], [1, P]]),
                    writes=[xnfm_g[g]], reads=[xs_], transpose=True,
                    ch="a1" if g % 2 == 0 else "s1")

    def wcast_strip(src_ap, slot, width=1024):
        return dma("pop", slot[:, 0:width], src_ap,
                   writes=[slot], reads=[])

    def proj_strip_ap(wten, wcols, o, nblk=8):
        return bass.AP(wten.tensor, o * P, [[wcols, P], [P * wcols, nblk], [1, P]])

    def attn_common(is_l0):
        pass

    def block1():
        layer_norm(quant=False)
        normalize_to_fm(quant=False)
        vt_ones()
        for o in range(DC):
            w_ = wsl[o % 2]
            wcast_strip(proj_strip_ap(W["W_qkv"], 3 * D, 8 + o), w_)
            for tt in range(NTT):
                ps_ = psA if tt % 2 == 0 else psB
                for g in range(DC):
                    mm(ps_[:, 0:TT], w_[:, g * P:(g + 1) * P],
                       bass.AP(xnfm_g[g], tt * TT, [[S, P], [1, TT]]),
                       writes=[ps_] if g == DC - 1 else [],
                       pre=[ps_] if g == 0 else (),
                       reads=([w_] + xnfm_g) if g == DC - 1 else
                             [w_, xnfm_g[g]],
                       start=(g == 0), stop=(g == DC - 1),
                       inc=1 if g == DC - 1 else 0)
                act(bass.AP(kfm, o * S + tt * TT, [[kfm.shape[1], P], [1, TT]]),
                    ps_[:, 0:TT], AF.Identity, bias=biases[:, 8 + o:9 + o],
                    writes=[kfm], reads=[ps_, biases])
        for vcb in range(2):
            for hh in range(2):
                dma("pop", wsl[hh][:, 0:2048],
                    bass.AP(W["W_qkv"].tensor, 2 * D + vcb * TT + hh * 4 * P * 3 * D,
                            [[3 * D, P], [P * 3 * D, 4], [1, TT]]),
                    writes=[wsl[hh]], reads=[])
            dma("pop", bvrow[0:1, 0:TT],
                bass.AP(W["b_qkv"].tensor, 2 * D + vcb * TT, [[1, 1], [1, TT]]),
                writes=[bvrow], reads=[], slow=True)
            mm(psM[0:P, 0:TT], onesb[0:1, 0:P], bvrow[0:1, 0:TT],
               writes=[psM], reads=[bvrow, onesb])
            act(bvb[:, 0:TT], psM[:, 0:TT], AF.Identity, writes=[bvb], reads=[psM])
            for tc in range(SC):
                ps_ = psA if tc % 2 == 0 else psB
                for g in range(DC):
                    mm(ps_[:, 0:TT],
                       bass.AP(xnfm_g[g], tc * P, [[S, P], [1, P]]),
                       wsl[g // 4][:, (g % 4) * TT:(g % 4 + 1) * TT],
                       writes=[ps_] if g == DC - 1 else [],
                       pre=[ps_] if g == 0 else (),
                       reads=([wsl[0], wsl[1]] + xnfm_g) if g == DC - 1 else
                             [wsl[g // 4], xnfm_g[g]],
                       start=(g == 0), stop=(g == DC - 1),
                       inc=1 if g == DC - 1 else 0)
                dve(lambda E_, tc=tc, vcb=vcb, ps_=ps_: E_.tensor_tensor(
                    out=bass.AP(vt, tc * (NH * 65) + vcb * 8 * 65,
                                [[vt.shape[1], P], [65, 8], [1, 64]]),
                    in0=ps_[:, 0:TT], in1=bvb[:, 0:TT], op=ALU.add),
                    writes=[vt], reads=[ps_, bvb])
        for qt in range(NTT):
            for o in range(DC):
                w_ = wsl[o % 2]
                wcast_strip(proj_strip_ap(W["W_qkv"], 3 * D, o), w_)
                ps_ = psA if o % 2 == 0 else psB
                for g in range(DC):
                    mm(ps_[:, 0:TT], w_[:, g * P:(g + 1) * P],
                       bass.AP(xnfm_g[g], qt * TT, [[S, P], [1, TT]]),
                       writes=[ps_] if g == DC - 1 else [],
                       pre=[ps_] if g == 0 else (),
                       reads=([w_] + xnfm_g) if g == DC - 1 else
                             [w_, xnfm_g[g]],
                       start=(g == 0), stop=(g == DC - 1),
                       inc=1 if g == DC - 1 else 0)
                act(qtile[:, o * TT:(o + 1) * TT], ps_[:, 0:TT], AF.Identity,
                    bias=biases[:, o:o + 1], writes=[qtile], reads=[ps_, biases])
            flash_heads(qt, causal=False)
            out_projection(qt, W["W_o"], D, 24, add_dq=None)

    def flash_heads(qt, causal):
        nkc = SC if not causal else 4 * qt + 4
        CC = C_L0 if not causal else C_L1
        npairs = nkc // 2
        pctr = [0]

        def emit_av(h, kp, pt):
            for sub in range(2):
                kc = 2 * kp + sub
                lhsv = bass.AP(vt, kc * (NH * 65) + h * 65,
                               [[vt.shape[1], P], [1, 65]])
                mm(psAV[0:65, 0:TT], lhsv, pt[:, sub * TT:(sub + 1) * TT],
                   writes=[psAV] if kc == nkc - 1 else [],
                   pre=[psAV] if kc == 0 else (),
                   reads=[vt, pt],
                   start=(kc == 0), stop=(kc == nkc - 1),
                   inc=1 if kc == nkc - 1 else 0)

        def emit_norm(h):
            def _em_recip(E_):
                with nc.allow_low_precision(reason="softmax denom recip bf16"):
                    return E_.reciprocal(out=recipb[0:1, 0:TT],
                                         in_=tmp1[64:65, 0:TT])
            dve(_em_recip, writes=[recipb], reads=[tmp1])
            mm(psM[0:64, 0:TT], onesb[0:1, 0:64], recipb[0:1, 0:TT],
               writes=[psM], pre=[psM], reads=[recipb, onesb])
            act(bvb[0:64, 0:TT], psM[0:64, 0:TT], AF.Identity, writes=[bvb],
                reads=[psM])
            dve(lambda E_, h=h: E_.tensor_tensor(
                out=bass.AP(avfm, (64 * (h % 2)) * avfm.shape[1] + (h // 2) * TT,
                            [[avfm.shape[1], 64], [1, TT]]),
                in0=tmp1[0:64, 0:TT], in1=bvb[0:64, 0:TT], op=ALU.mult),
                writes=[avfm], reads=[tmp1, bvb])

        prev_head = None
        for h in range(NH):
            pend = None
            norm_at = min(3, npairs - 1)
            for kp in range(npairs):
                psX = psS if pctr[0] % 2 == 0 else psS2
                pctr[0] += 1
                for sub in range(2):
                    kc = 2 * kp + sub
                    lhs = bass.AP(kfm, (64 * (h % 2)) * kfm.shape[1]
                                  + (h // 2) * S + kc * P,
                                  [[kfm.shape[1], 64], [1, P]])
                    rhs = bass.AP(qtile, (64 * (h % 2)) * qtile.shape[1]
                                  + (h // 2) * TT,
                                  [[qtile.shape[1], 64], [1, TT]])
                    mm(psX[:, sub * TT:(sub + 1) * TT], lhs, rhs,
                       writes=[psX] if sub == 1 else [],
                       pre=[psX] if sub == 0 else (),
                       reads=[kfm, qtile],
                       start=True, stop=True, inc=1 if sub == 1 else 0)
                if pend is not None:
                    emit_av(h, *pend)
                if kp == norm_at and prev_head is not None:
                    emit_norm(prev_head)
                    prev_head = None
                pt = ptile if kp % 2 == 0 else ptile2
                act(pt[:, 0:1024], psX[:, 0:1024], AF.Exp, bias=-CC, scale=0.125,
                    writes=[pt], reads=[psX])
                if causal and 2 * kp + 1 >= 4 * qt:
                    for sub in range(2):
                        kc = 2 * kp + sub
                        i = kc - 4 * qt
                        if i >= 0:
                            dve(lambda E_, pt=pt, sub=sub, i=i: E_.tensor_tensor(
                                out=pt[:, sub * TT:(sub + 1) * TT],
                                in0=pt[:, sub * TT:(sub + 1) * TT],
                                in1=maskl1_s[:, i * TT:(i + 1) * TT], op=ALU.mult),
                                writes=[pt], reads=[pt, maskl1_s])
                pend = (kp, pt)
            emit_av(h, *pend)
            if prev_head is not None:
                emit_norm(prev_head)
            act(tmp1[0:65, 0:TT], psAV[0:65, 0:TT], AF.Identity,
                writes=[tmp1], reads=[psAV])
            prev_head = h
        emit_norm(prev_head)

    ptile2 = cx.sb_at("ptile2", [P, 1024], BF16, ptile, 2048)

    def out_projection(qt, wten, wcols, bias_col, add_dq):
        for o in range(DC):
            w_ = wsl[o % 2]
            wcast_strip(proj_strip_ap(wten, wcols, o), w_)
            ps_ = psA if o % 2 == 0 else psB
            for g in range(DC):
                mm(ps_[:, 0:TT], w_[:, g * P:(g + 1) * P],
                   avfm[:, g * TT:(g + 1) * TT],
                   writes=[ps_] if g == DC - 1 else [],
                   pre=[ps_] if g == 0 else (),
                   reads=[w_, avfm], start=(g == 0), stop=(g == DC - 1),
                   inc=1 if g == DC - 1 else 0)
            os_ = ostage if o % 2 == 0 else ostage2
            ts_ = tstage if o % 2 == 0 else tstage2
            act(os_[:, 0:TT], ps_[:, 0:TT], AF.Identity,
                bias=biases[:, bias_col + o:bias_col + o + 1],
                writes=[os_], reads=[ps_, biases])
            for j in range(4):
                dma("sp", ts_[:, j * P:(j + 1) * P],
                    os_[:, j * P:(j + 1) * P],
                    writes=[ts_], reads=[os_], transpose=True,
                    ch="a1" if j % 2 == 0 else "s1")
            dve(lambda E_, qt=qt, o=o, ts_=ts_: E_.tensor_tensor(
                out=bass.AP(stream, (qt * 4) * D + o * P,
                            [[stream.shape[1], P], [D, 4], [1, P]]),
                in0=bass.AP(stream, (qt * 4) * D + o * P,
                            [[stream.shape[1], P], [D, 4], [1, P]]),
                in1=ts_[:, 0:TT],
                op=ALU.add),
                writes=[stream], reads=[stream, ts_])


    hraw = cx.sb_at("hraw", [P, S + 4], BF16, vt, 0)
    hM = cx.sb_at("hM", [P, S], BF16, kfm, 0)
    hm = cx.sb_at("hm", [P, S], BF16, kfm, 4096)
    habs = cx.sb_at("habs", [P, S], BF16, kfm, 8192)
    hsb = cx.sb_at("hsb", [P, TT], BF16, vt, 4128)
    s2sb = cx.sb_at("s2sb", [16, 256], BF16, vt, 5152)
    s2row = cx.sb_at("s2row", [1, 4096], BF16, vt, 5664)
    s2bc = cx.sb_at("s2bc", [P, TT], F32, vt, 13856)
    dq2bc = cx.sb_at("dq2bc", [P, TT], F32, vt, 15904)
    hqA = cx.sb_at("hqA", [P, TT], BF16, vt, 17952)
    hqB = cx.sb_at("hqB", [P, TT], BF16, vt, 18976)
    xq2fm = cx.sb_at("xq2fm", [P, DHC * TT], BF16, kfm, 0)
    io8 = cx.sb_at("io8", [P, 8], F32, diag, 0)
    rev8 = cx.sb_at("rev8", [P, 8], F32, diag, 32)
    eidst = cx.sb_at("eidst", [P, SC], BF16, diag, 64)
    negmax = cx.sb_at("negmax", [P, 1], F32, diag, 96)
    ystage = cx.sb_at("ystage", [P, D], F32, kfm, 0)
    xt4 = cx.sb_at("xt4", [P, D], F32, kfm, 4096)
    yefm = cx.sb_at("yefm", [P, DC * TT], BF16, avfm, 0)

    def block2():
        layer_norm(quant=True, sw_col=0, dq_block=0)
        DQBLK[0] = 0
        normalize_to_fm(quant=True)
        if DBG_B2:
            xq1s = nc.dram_tensor("xq1s", [P, DC * S], BF16,
                                  kind="ExternalOutput").ap()
            hpre = nc.dram_tensor("hpre", [DHC, P, S], BF16,
                                  kind="ExternalOutput").ap()
            hcv = nc.dram_tensor("hcv", [DHC, P, S], BF16,
                                 kind="ExternalOutput").ap()
            hrawd = nc.dram_tensor("hrawd", [DHC, P, S + 4], BF16,
                                   kind="ExternalOutput").ap()
            diagd = nc.dram_tensor("diagd", [DHC, P, 384], BF16,
                                   kind="ExternalOutput").ap()
            dma("sp", xq1s, xnfm[:, :], writes=[], reads=[xnfm])
        else:
            hpre = hcv = hrawd = diagd = None
        first = [True]
        for m in range(DHC):
            w_ = wsl[m % 2]
            dma("sp", w_[:, 0:1024],
                bass.AP(wq1T_dram.tensor, m * P, [[DH, P], [P * DH, DC], [1, P]]),
                writes=[w_], reads=[], pre=[qres[0]])
            for j in range(3):
                dve(lambda E_, m=m, j=j: E_.tensor_scalar(
                    out=diag[:, j * P:(j + 1) * P],
                    in0=idenb[:, :], scalar1=convw[:, 3 * m + j:3 * m + j + 1],
                    scalar2=None, op0=ALU.mult),
                    writes=[diag], reads=[idenb, convw])
            dve(lambda E_: E_.memset(hraw[:, 0:1], 0.0), writes=[hraw], reads=[])
            dve(lambda E_: E_.memset(hraw[:, S + 1:S + 4], 0.0),
                writes=[hraw], reads=[])
            for tt in range(NTT):
                ps_ = psA if tt % 2 == 0 else psB
                for g in range(DC):
                    mm(ps_[:, 0:TT], w_[:, g * P:(g + 1) * P],
                       bass.AP(xnfm_g[g], tt * TT, [[S, P], [1, TT]]),
                       writes=[ps_] if g == DC - 1 else [],
                       pre=[ps_] if g == 0 else (),
                       reads=([w_] + xnfm_g) if g == DC - 1 else
                             [w_, xnfm_g[g]],
                       start=(g == 0), stop=(g == DC - 1),
                       inc=1 if g == DC - 1 else 0)
                act(hraw[:, 1 + tt * TT:1 + (tt + 1) * TT], ps_[:, 0:TT],
                    AF.Identity, writes=[hraw], reads=[ps_])
                if DBG_B2:
                    dma("sp", bass.AP(hpre.tensor, (m * P) * S + tt * TT,
                                      [[S, P], [1, TT]]),
                        hraw[:, 1 + tt * TT:1 + (tt + 1) * TT],
                        writes=[], reads=[hraw])
            if DBG_B2:
                dma("sp", bass.AP(hrawd.tensor, (m * P) * (S + 4),
                                  [[S + 4, P], [1, S + 4]]),
                    hraw[:, :], writes=[], reads=[hraw])
            if DBG_B2:
                cx.fence_all()
                dma("sp", bass.AP(diagd.tensor, (m * P) * 384,
                                  [[384, P], [1, 384]]),
                    diag[:, :], writes=[], reads=[diag])
            for tt in range(NTT):
                ps_ = psS
                for j in range(3):
                    mm(ps_[:, 0:TT], diag[:, j * P:(j + 1) * P],
                       hraw[:, tt * TT + j:tt * TT + j + TT],
                       writes=[ps_] if j == 2 else [],
                       pre=[ps_] if j == 0 else (),
                       reads=[diag, hraw], start=(j == 0), stop=(j == 2),
                       inc=1 if j == 2 else 0)
                if DBG_B2:
                    act(tmpb2[:, 0:TT], ps_[:, 0:TT], AF.Identity,
                        writes=[tmpb2], reads=[ps_])
                    dma("sp", bass.AP(hcv.tensor, (m * P) * S + tt * TT,
                                      [[S, P], [1, TT]]),
                        tmpb2[:, 0:TT], writes=[], reads=[tmpb2])
                act(hsb[:, 0:TT], ps_[:, 0:TT], AF.Silu,
                    bias=convb[:, m:m + 1], writes=[hsb], reads=[ps_, convb])
                dma("pop", bass.AP(h_dram.tensor, (m * P) * S + tt * TT,
                                   [[S, P], [1, TT]]),
                    hsb[:, 0:TT], writes=[], reads=[hsb, hres[SEQ[0]]],
                    pre=[hload[1 - SEQ[0]]])
                if first[0]:
                    dve(lambda E_, tt=tt: E_.tensor_copy(
                        out=hM[:, tt * TT:(tt + 1) * TT], in_=hsb[:, 0:TT]),
                        writes=[hM], reads=[hsb])
                    dve(lambda E_, tt=tt: E_.tensor_copy(
                        out=hm[:, tt * TT:(tt + 1) * TT], in_=hsb[:, 0:TT]),
                        writes=[hM], reads=[hsb])
                else:
                    dve(lambda E_, tt=tt: E_.tensor_tensor(
                        out=hM[:, tt * TT:(tt + 1) * TT],
                        in0=hM[:, tt * TT:(tt + 1) * TT], in1=hsb[:, 0:TT],
                        op=ALU.max), writes=[hM], reads=[hsb])
                    dve(lambda E_, tt=tt: E_.tensor_tensor(
                        out=hm[:, tt * TT:(tt + 1) * TT],
                        in0=hm[:, tt * TT:(tt + 1) * TT], in1=hsb[:, 0:TT],
                        op=ALU.min), writes=[hM], reads=[hsb])
            first[0] = False
        dve(lambda E_: E_.tensor_scalar(out=habs[:, :], in0=hm[:, 0:S],
                                        scalar1=-1.0, scalar2=None, op0=ALU.mult),
            writes=[hM], reads=[hM])
        dve(lambda E_: E_.tensor_tensor(out=habs[:, :], in0=habs[:, :],
                                        in1=hM[:, 0:S], op=ALU.max),
            writes=[hM], reads=[hM])
        for k in range(SC):
            tb_ = tmpb if k % 2 == 0 else tmpb2
            dma("sp", tb_[:, 0:P], habs[:, k * P:(k + 1) * P],
                writes=[tb_], reads=[hM], transpose=True,
                ch="a1" if k % 2 == 0 else "s1")
            dve(lambda E_, k=k, tb_=tb_: E_.reduce_max(
                out=sdq[:, 16 + k:17 + k],
                in_=tb_[:, 0:P], axis=AX.X),
                writes=[sdq], reads=[tb_])
        for k in range(SC):
            dve(lambda E_, k=k: E_.tensor_scalar(
                out=sdq[:, 16 + k:17 + k], in0=sdq[:, 16 + k:17 + k],
                scalar1=1e-5, scalar2=None, op0=ALU.max),
                writes=[sdq], reads=[sdq])
            dve(lambda E_, k=k: E_.tensor_scalar(
                out=sdq[:, 32 + k:33 + k], in0=sdq[:, 16 + k:17 + k],
                scalar1=1.0 / 127.0, scalar2=None, op0=ALU.mult),
                writes=[sdq], reads=[sdq])
            dve(lambda E_, k=k: E_.tensor_tensor(
                out=sdq[:, 32 + k:33 + k], in0=sdq[:, 32 + k:33 + k],
                in1=swrec[:, 1:2], op=ALU.mult), writes=[sdq], reads=[sdq, swrec])
            dve(lambda E_, k=k: E_.reciprocal(
                out=sdq[:, 16 + k:17 + k], in_=sdq[:, 16 + k:17 + k]),
                writes=[sdq], reads=[sdq])
            dve(lambda E_, k=k: E_.tensor_scalar(
                out=sdq[:, 16 + k:17 + k], in0=sdq[:, 16 + k:17 + k],
                scalar1=127.0, scalar2=None, op0=ALU.mult),
                writes=[sdq], reads=[sdq])
        tr_pe(psM[0:16, 0:P], sdq[:, 16:32], idenf[:, :],
              writes=[psM], reads=[sdq, idenf])
        act(s2sb[0:16, 0:P], psM[0:16, 0:P], AF.Identity,
            writes=[s2sb], reads=[psM])
        tr_pe(psM[0:16, 0:P], sdq[:, 32:48], idenf[:, :],
              writes=[psM], reads=[sdq, idenf])
        act(s2sb[0:16, P:2 * P], psM[0:16, 0:P], AF.Identity,
            writes=[s2sb], reads=[psM])
        dma("sp", bass.AP(s2row, 0, [[4096, 1], [128, 16], [1, P]]),
            bass.AP(s2sb, 0, [[256, 16], [1, P]]),
            writes=[s2row], reads=[s2sb], slow=True)
        dma("sp", bass.AP(s2row, 2048, [[4096, 1], [128, 16], [1, P]]),
            bass.AP(s2sb, P, [[256, 16], [1, P]]),
            writes=[s2row], reads=[s2sb], slow=True)
        for tt in range(NTT):
            for j4 in range(4):
                k = 4 * tt + j4
                mm(psS[:, j4 * P:(j4 + 1) * P], onesb[0:1, 0:P],
                   s2row[0:1, k * P:(k + 1) * P],
                   writes=[psS] if j4 == 3 else [], pre=[psS] if j4 == 0 else (),
                   reads=[onesb, s2row], inc=1 if j4 == 3 else 0)
                mm(psS[:, TT + j4 * P:TT + (j4 + 1) * P], onesb[0:1, 0:P],
                   s2row[0:1, 2048 + k * P:2048 + (k + 1) * P],
                   writes=[psS] if j4 == 3 else [], reads=[onesb, s2row],
                   inc=1 if j4 == 3 else 0)
            act(s2bc[:, 0:TT], psS[:, 0:TT], AF.Identity,
                writes=[s2bc], reads=[psS])
            act(dq2bc[:, 0:TT], psS[:, TT:2 * TT], AF.Identity,
                writes=[dq2bc], reads=[psS])
            for m in range(DHC):
                hq = hqA if m % 2 == 0 else hqB
                dma("pop", hq[:, 0:TT],
                    bass.AP(h_dram.tensor, (m * P) * S + tt * TT, [[S, P], [1, TT]]),
                    writes=[hq], reads=[hload[SEQ[0]]], pre=[hres[SEQ[0]]])
                dve(lambda E_, hq=hq: E_.tensor_tensor(
                    out=tmp2[:, 0:TT], in0=hq[:, 0:TT], in1=s2bc[:, 0:TT],
                    op=ALU.mult), writes=[tmp2], reads=[hq, s2bc])
                dve(lambda E_: E_.tensor_scalar(
                    out=tmp2[:, 0:TT], in0=tmp2[:, 0:TT], scalar1=RND,
                    scalar2=None, op0=ALU.add), writes=[tmp2], reads=[tmp2])
                dve(lambda E_: E_.tensor_copy(out=tmpi[:, 0:TT],
                                              in_=tmp2[:, 0:TT]),
                    writes=[tmpi], reads=[tmp2])
                dve(lambda E_: E_.scalar_tensor_tensor(
                    out=tmp1[:, 0:TT], in0=tmpi[:, 0:TT], scalar=0.5,
                    in1=tmp2[:, 0:TT], op0=ALU.add, op1=ALU.is_le),
                    writes=[tmp1], reads=[tmpi, tmp2])
                dve(lambda E_: E_.scalar_tensor_tensor(
                    out=tmp2[:, 0:TT], in0=tmpi[:, 0:TT], scalar=-RND,
                    in1=tmp1[:, 0:TT], op0=ALU.add, op1=ALU.add),
                    writes=[tmp2], reads=[tmpi, tmp1])
                dve(lambda E_, m=m: E_.tensor_tensor(
                    out=bass.AP(kfm, m * TT, [[kfm.shape[1], P], [1, TT]]),
                    in0=tmp2[:, 0:TT], in1=dq2bc[:, 0:TT], op=ALU.mult),
                    writes=[xq2fm], reads=[tmp2, dq2bc])
            for o in range(DC):
                for hh in range(2):
                    dma("sp", wsl[hh][:, 0:2048],
                        bass.AP(wq2T_dram.tensor, o * P + hh * 16 * P * D,
                                [[D, P], [P * D, 16], [1, P]]),
                        writes=[wsl[hh]], reads=[], pre=[qres[1]])
                ps_ = psA if o % 2 == 0 else psB
                for g in range(DHC):
                    mm(ps_[:, 0:TT], wsl[g // 16][:, (g % 16) * P:(g % 16 + 1) * P],
                       bass.AP(kfm, g * TT, [[kfm.shape[1], P], [1, TT]]),
                       writes=[ps_] if g == DHC - 1 else [],
                       pre=[ps_] if g == 0 else (),
                       reads=[wsl[0], wsl[1], xq2fm] if g == DHC - 1 else
                             [wsl[g // 16], xq2fm],
                       start=(g == 0), stop=(g == DHC - 1),
                       inc=1 if g == DHC - 1 else 0)
                os_ = ostage if o % 2 == 0 else ostage2
                ts_ = tstage if o % 2 == 0 else tstage2
                act(os_[:, 0:TT], ps_[:, 0:TT], AF.Identity,
                    writes=[os_], reads=[ps_])
                for j in range(4):
                    dma("sp", ts_[:, j * P:(j + 1) * P],
                        os_[:, j * P:(j + 1) * P],
                        writes=[ts_], reads=[os_], transpose=True,
                        ch="a1" if j % 2 == 0 else "s1")
                dve(lambda E_, tt=tt, o=o, ts_=ts_: E_.tensor_tensor(
                    out=bass.AP(stream, (tt * 4) * D + o * P,
                                [[stream.shape[1], P], [D, 4], [1, P]]),
                    in0=bass.AP(stream, (tt * 4) * D + o * P,
                                [[stream.shape[1], P], [D, 4], [1, P]]),
                    in1=ts_[:, 0:TT], op=ALU.add),
                    writes=[stream], reads=[stream, ts_])

    latbuf = cx.sb_at("latbuf", [P, 2 * S], BF16, qtile, 0)
    latq = cx.sb_at("latq", [P, 2 * TT], BF16, avfm, 0)

    def lat_proj(wname, dst, dst_off_fn):
        for lm in range(2):
            w_ = wsl[lm % 2]
            dma("pop", w_[:, 0:1024],
                bass.AP(W[wname].tensor, lm * P, [[LAT, P], [P * LAT, DC], [1, P]]),
                writes=[w_], reads=[])
            for tt in range(NTT):
                ps_ = psA if tt % 2 == 0 else psB
                for g in range(DC):
                    mm(ps_[:, 0:TT], w_[:, g * P:(g + 1) * P],
                       bass.AP(xnfm_g[g], tt * TT, [[S, P], [1, TT]]),
                       writes=[ps_] if g == DC - 1 else [],
                       pre=[ps_] if g == 0 else (),
                       reads=([w_] + xnfm_g) if g == DC - 1 else
                             [w_, xnfm_g[g]],
                       start=(g == 0), stop=(g == DC - 1),
                       inc=1 if g == DC - 1 else 0)
                act(dst_off_fn(lm, tt), ps_[:, 0:TT], AF.Identity,
                    writes=[dst], reads=[ps_])

    def block3():
        layer_norm(quant=False)
        normalize_to_fm(quant=False)
        vt_ones()
        lat_proj("wk_lat", latbuf,
                 lambda lm, tt: latbuf[:, lm * S + tt * TT:lm * S + (tt + 1) * TT])
        for o in range(DC):
            w_ = wsl[o % 2]
            dma("pop", w_[:, 0:256],
                bass.AP(W["wk_heads"].tensor, o * P, [[D, P], [P * D, 2], [1, P]]),
                writes=[w_], reads=[])
            for tt in range(NTT):
                ps_ = psA if tt % 2 == 0 else psB
                for lg in range(2):
                    mm(ps_[:, 0:TT], w_[:, lg * P:(lg + 1) * P],
                       bass.AP(latbuf, lg * S + tt * TT, [[latbuf.shape[1], P], [1, TT]]),
                       writes=[ps_] if lg == 1 else [],
                       pre=[ps_] if lg == 0 else (),
                       reads=[w_, latbuf], start=(lg == 0), stop=(lg == 1),
                       inc=1 if lg == 1 else 0)
                act(bass.AP(kfm, o * S + tt * TT, [[kfm.shape[1], P], [1, TT]]),
                    ps_[:, 0:TT], AF.Identity, writes=[kfm], reads=[ps_])
        lat_proj("wv_lat", latbuf,
                 lambda lm, tt: latbuf[:, lm * S + tt * TT:lm * S + (tt + 1) * TT])
        for vcb in range(2):
            w_ = wsl[vcb % 2]
            dma("pop", w_[:, 0:1024],
                bass.AP(W["wv_heads"].tensor, vcb * TT,
                        [[D, P], [P * D, 2], [1, TT]]),
                writes=[w_], reads=[])
            for tc in range(SC):
                ps_ = psA if tc % 2 == 0 else psB
                for lg in range(2):
                    mm(ps_[:, 0:TT],
                       bass.AP(latbuf, lg * S + tc * P, [[latbuf.shape[1], P], [1, P]]),
                       w_[:, lg * TT:(lg + 1) * TT],
                       writes=[ps_] if lg == 1 else [],
                       pre=[ps_] if lg == 0 else (),
                       reads=[w_, latbuf], start=(lg == 0), stop=(lg == 1),
                       inc=1 if lg == 1 else 0)
                dve(lambda E_, tc=tc, vcb=vcb, ps_=ps_: E_.tensor_copy(
                    out=bass.AP(vt, tc * (NH * 65) + vcb * 8 * 65,
                                [[vt.shape[1], P], [65, 8], [1, 64]]),
                    in_=ps_[:, 0:TT]),
                    writes=[vt], reads=[ps_])
        for qt in range(NTT):
            for lm in range(2):
                w_ = wsl[lm % 2]
                dma("pop", w_[:, 0:1024],
                    bass.AP(W["wq_lat"].tensor, lm * P,
                            [[LAT, P], [P * LAT, DC], [1, P]]),
                    writes=[w_], reads=[])
                ps_ = psA if lm % 2 == 0 else psB
                for g in range(DC):
                    mm(ps_[:, 0:TT], w_[:, g * P:(g + 1) * P],
                       bass.AP(xnfm_g[g], qt * TT, [[S, P], [1, TT]]),
                       writes=[ps_] if g == DC - 1 else [],
                       pre=[ps_] if g == 0 else (),
                       reads=([w_] + xnfm_g) if g == DC - 1 else
                             [w_, xnfm_g[g]],
                       start=(g == 0), stop=(g == DC - 1),
                       inc=1 if g == DC - 1 else 0)
                act(latq[:, lm * TT:(lm + 1) * TT], ps_[:, 0:TT], AF.Identity,
                    writes=[latq], reads=[ps_])
            for o in range(DC):
                w_ = wsl[o % 2]
                dma("pop", w_[:, 0:256],
                    bass.AP(W["wq_heads"].tensor, o * P,
                            [[D, P], [P * D, 2], [1, P]]),
                    writes=[w_], reads=[])
                ps_ = psA if o % 2 == 0 else psB
                for lg in range(2):
                    mm(ps_[:, 0:TT], w_[:, lg * P:(lg + 1) * P],
                       latq[:, lg * TT:(lg + 1) * TT],
                       writes=[ps_] if lg == 1 else [],
                       pre=[ps_] if lg == 0 else (),
                       reads=[w_, latq], start=(lg == 0), stop=(lg == 1),
                       inc=1 if lg == 1 else 0)
                act(qtile[:, o * TT:(o + 1) * TT], ps_[:, 0:TT], AF.Identity,
                    writes=[qtile], reads=[ps_])
            flash_heads(qt, causal=True)
            out_projection(qt, W["w_proj"], D, 32, add_dq=None)

    def route_and_spill(s):
        layer_norm(quant=False)
        pc(lambda E_: E_.iota(io8[:, :], [[1, 8]], base=0, channel_multiplier=0,
                              allow_small_or_imprecise_dtypes=True),
           writes=[diag], reads=[])
        dve(lambda E_: E_.tensor_scalar(out=rev8[:, :], in0=io8[:, :],
                                        scalar1=-1.0, scalar2=8.0,
                                        op0=ALU.mult, op1=ALU.add),
            writes=[diag], reads=[diag])
        for c in range(SC):
            xs_ = xstage if c % 2 == 0 else xstage2
            xc = stream_ap(c)
            dve(lambda E_, xc=xc, c=c, xs_=xs_: E_.tensor_scalar(
                out=xs_[:, :], in0=xc, scalar1=stats[:, c:c + 1],
                scalar2=stats[:, 16 + c:17 + c], op0=ALU.mult, op1=ALU.add),
                writes=[xs_], reads=[stream, stats])
            dma("sp", bass.AP(in_pool.tensor, (s * S + c * P) * D,
                              [[D, P], [1, D]]),
                xs_[:, :], writes=[], reads=[xs_])
            for g in range(DC):
                dma("sp", bass.AP(xnfm_g[g], c * P, [[S, P], [1, P]]),
                    bass.AP(xs_, g * P, [[xs_.shape[1], P], [1, P]]),
                    writes=[xnfm_g[g]], reads=[xs_], transpose=True,
                    ch="a1" if g % 2 == 0 else "s1")
        for c in range(SC):
            for g in range(DC):
                mm(psM[:, 0:8],
                   bass.AP(xnfm_g[g], c * P, [[S, P], [1, P]]),
                   wrs[:, g * 8:(g + 1) * 8],
                   writes=[psM] if g == DC - 1 else [],
                   pre=[psM] if g == 0 else (),
                   reads=(xnfm_g + [wrs]) if g == DC - 1 else
                         [xnfm_g[g], wrs],
                   start=(g == 0), stop=(g == DC - 1),
                   inc=1 if g == DC - 1 else 0)
            dve(lambda E_: E_.reduce_max(out=stats[:, 64:65], in_=psM[:, 0:8],
                                         axis=AX.X),
                writes=[stats], reads=[psM])
            dve(lambda E_: E_.tensor_scalar(out=negmax[:, 0:1], in0=stats[:, 64:65],
                                            scalar1=-1.0, scalar2=None,
                                            op0=ALU.mult),
                writes=[diag], reads=[stats])
            act(tmp2[:, 0:8], psM[:, 0:8], AF.Exp, bias=negmax[:, 0:1],
                accum=stats[:, 65:66], writes=[tmp2, stats], reads=[psM, diag])
            dve(lambda E_, s=s, c=c: E_.reciprocal(
                out=ptok[:, s * SC + c:s * SC + c + 1], in_=stats[:, 65:66]),
                writes=[ptok], reads=[stats])
            dve(lambda E_: E_.tensor_scalar(out=tmp1[:, 0:8], in0=psM[:, 0:8],
                                            scalar1=stats[:, 64:65], scalar2=None,
                                            op0=ALU.is_equal),
                writes=[tmp1], reads=[psM, stats])
            dve(lambda E_: E_.tensor_tensor(out=tmp1[:, 0:8], in0=tmp1[:, 0:8],
                                            in1=rev8[:, :], op=ALU.mult),
                writes=[tmp1], reads=[tmp1, diag])
            dve(lambda E_: E_.reduce_max(out=stats[:, 66:67], in_=tmp1[:, 0:8],
                                         axis=AX.X),
                writes=[stats], reads=[tmp1])
            dve(lambda E_, c=c: E_.tensor_scalar(
                out=eidst[:, c:c + 1], in0=stats[:, 66:67], scalar1=-1.0,
                scalar2=8.0, op0=ALU.mult, op1=ALU.add),
                writes=[diag], reads=[stats])
        dma("sp", bass.AP(e_own.tensor, s * S, [[1, P], [P, SC]]),
            eidst[:, 0:SC], writes=[], reads=[diag], slow=True)
        dma("sp", bass.AP(x4_dram.tensor, (s * S) * D, [[D, P], [P * D, SC], [1, D]]),
            bass.AP(stream, 0, [[stream.shape[1], P], [D, SC], [1, D]]),
            writes=[], reads=[stream, x4res])

    def moe():
        cx.fence_all()
        eres = cx.res("epool_res")
        pres = cx.res("pool_res")
        pc(lambda E_: E_.collective_compute(
            "AllGather", ALU.bypass, replica_groups=[list(range(8))],
            ins=[e_own], outs=[e_pool]), writes=[eres], reads=[], inc=1)
        cx.op("pc2", lambda E_: E_.collective_compute(
            "AllGather", ALU.bypass, replica_groups=[list(range(8))],
            ins=[in_pool], outs=[pool]), writes=[pres], reads=[], inc=1)
        dma("pop", Spool[:, :], bass.AP(e_pool.tensor, 0, [[256, P], [1, 256]]),
            writes=[Spool], reads=[eres])
        dve(lambda E_: E_.memset(zz256[:, :], 0.0), writes=[zz256], reads=[])
        dve(lambda E_: E_.memset(rnkp[:, :], 0.0), writes=[rnkp], reads=[])
        for e in range(E):
            dve(lambda E_, e=e: E_.tensor_scalar(
                out=mske[:, :], in0=Spool[:, :], scalar1=float(e), scalar2=0.0,
                op0=ALU.is_equal, op1=ALU.add, accum_out=tote[:, :]),
                writes=[mske, tote], reads=[Spool])
            dve(lambda E_, e=e: E_.tensor_copy(out=totb[:, e:e + 1], in_=tote[:, :]),
                writes=[totb], reads=[tote])
            dve(lambda E_: E_.tensor_tensor_scan(
                out=incl[:, :], data0=mske[:, :], data1=zz256[:, :], initial=0.0,
                op0=ALU.add, op1=ALU.add),
                writes=[incl], reads=[mske, zz256])
            dve(lambda E_: E_.tensor_tensor(out=incl[:, :], in0=incl[:, :],
                                            in1=mske[:, :], op=ALU.subtract),
                writes=[incl], reads=[incl, mske])
            dve(lambda E_: E_.tensor_tensor(out=incl[:, :], in0=incl[:, :],
                                            in1=mske[:, :], op=ALU.mult),
                writes=[incl], reads=[incl, mske])
            dve(lambda E_: E_.tensor_tensor(out=rnkp[:, :], in0=rnkp[:, :],
                                            in1=incl[:, :], op=ALU.add),
                writes=[rnkp], reads=[rnkp, incl])
        mm(psM[:, 0:8], triu1[:, :], totb[:, :], writes=[psM], pre=[psM],
           reads=[triu1, totb])
        dve(lambda E_: E_.tensor_copy(out=offs[:, :], in_=psM[:, 0:8]),
            writes=[offs], reads=[psM])
        for e in range(E):
            dve(lambda E_, e=e: E_.tensor_scalar(
                out=mske[:, :], in0=Spool[:, :], scalar1=float(e), scalar2=None,
                op0=ALU.is_equal),
                writes=[mske], reads=[Spool])
            dve(lambda E_, e=e: E_.tensor_scalar(
                out=incl[:, :], in0=mske[:, :], scalar1=offs[:, e:e + 1],
                scalar2=None, op0=ALU.mult),
                writes=[incl], reads=[mske, offs])
            dve(lambda E_: E_.tensor_tensor(out=rnkp[:, :], in0=rnkp[:, :],
                                            in1=incl[:, :], op=ALU.add),
                writes=[rnkp], reads=[rnkp, incl])
        dve(lambda E_: E_.tensor_scalar(out=keepp[:, :], in0=rnkp[:, :],
                                        scalar1=float(CAP) - 0.5, scalar2=None,
                                        op0=ALU.is_lt),
            writes=[keepp], reads=[rnkp])
        dve(lambda E_: E_.scalar_tensor_tensor(
            out=srcp[:, :], in0=Spool[:, :], scalar=float(CAP), in1=rnkp[:, :],
            op0=ALU.mult, op1=ALU.add),
            writes=[srcp], reads=[Spool, rnkp])
        dve(lambda E_: E_.tensor_tensor(out=srcp[:, :], in0=srcp[:, :],
                                        in1=keepp[:, :], op=ALU.mult),
            writes=[srcp], reads=[srcp, keepp])
        pc(lambda E_: E_.iota(iotap[:, :], [[1, 256]], base=0,
                              channel_multiplier=256,
                              allow_small_or_imprecise_dtypes=True),
           writes=[iotap], reads=[])
        pc(lambda E_: E_.iota(iotaj[:, :], [[1, 256]], base=CAP,
                              channel_multiplier=0,
                              allow_small_or_imprecise_dtypes=True),
           writes=[iotaj], reads=[])
        dve(lambda E_: E_.tensor_scalar(out=scat[:, :], in0=Spool[:, :],
                                        scalar1=cid_s[:, 0:1], scalar2=None,
                                        op0=ALU.is_equal),
            writes=[scat], reads=[Spool, cid_s])
        dve(lambda E_: E_.tensor_tensor(out=scat[:, :], in0=scat[:, :],
                                        in1=keepp[:, :], op=ALU.mult),
            writes=[scat], reads=[scat, keepp])
        dve(lambda E_: E_.tensor_scalar(out=mske[:, :], in0=srcp[:, :],
                                        scalar1=cid_s[:, 2:3], scalar2=None,
                                        op0=ALU.subtract),
            writes=[mske], reads=[srcp, cid_s])
        dve(lambda E_: E_.tensor_tensor(out=mske[:, :], in0=mske[:, :],
                                        in1=iotaj[:, :], op=ALU.subtract),
            writes=[mske], reads=[mske, iotaj])
        dve(lambda E_: E_.tensor_tensor(out=mske[:, :], in0=mske[:, :],
                                        in1=scat[:, :], op=ALU.mult),
            writes=[mske], reads=[mske, scat])
        dve(lambda E_: E_.tensor_tensor(out=scat[:, :], in0=mske[:, :],
                                        in1=iotaj[:, :], op=ALU.add),
            writes=[scat], reads=[scat, mske, iotaj])
        dve(lambda E_: E_.tensor_copy(out=scat32[:, :], in_=scat[:, :]),
            writes=[scat32], reads=[scat])
        dve(lambda E_: E_.memset(tmpi[:, 0:64], 0), writes=[tmpi], reads=[])
        dma("sp", bass.AP(ddisp.tensor, 0, [[34, P], [1, 34]]),
            tmpi[:, 0:34], writes=[], reads=[tmpi])
        cx.fence("p1")
        for j in range(256):
            cx.op("p1", lambda E_, j=j: E_.indirect_dma_start(
                out=ddisp, out_offset=bass.IndirectOffsetOnAxis(
                    scat32[:, j:j + 1], 0),
                in_=iotap[:, j:j + 1], in_offset=None),
                writes=[], reads=[scat32, iotap], inc=16)
        cx.fence("sp")
        for h in range(8):
            dma("sp", D2[16 * h:16 * h + 16, 0:32],
                bass.AP(ddisp.tensor, 512 * h, [[1, 16], [16, 32]]),
                writes=[D2], reads=[], slow=True)
        mm(psM[0:16, 0:256], ownsel_s[:, :], srcp[:, :], writes=[psM], pre=[psM],
           reads=[ownsel_s, srcp])
        dve(lambda E_: E_.tensor_copy(out=ownv[0:16, 0:256], in_=psM[0:16, 0:256]),
            writes=[ownv], reads=[psM])
        mm(psM[0:16, 0:256], ownsel_s[:, :], keepp[:, :], writes=[psM], pre=[psM],
           reads=[ownsel_s, keepp])
        dve(lambda E_: E_.tensor_copy(out=ownv[0:16, 256:512], in_=psM[0:16, 0:256]),
            writes=[ownv], reads=[psM])
        for g in range(2):
            tr_pe(psM[0:P, 0:16], ownv[0:16, g * P:(g + 1) * P],
                  idenf[0:16, 0:16], writes=[psM], reads=[ownv, idenf])
            dve(lambda E_, g=g: E_.tensor_copy(
                out=bass.AP(sidx, g, [[sidx.shape[1], P], [2, 16]]),
                in_=psM[0:P, 0:16]),
                writes=[sidx], reads=[psM])
            tr_pe(psM[0:P, 16:32], ownv[0:16, 256 + g * P:256 + (g + 1) * P],
                  idenf[0:16, 0:16], writes=[psM], reads=[ownv, idenf])
            dve(lambda E_, g=g: E_.tensor_copy(
                out=bass.AP(gate, g, [[gate.shape[1], P], [2, 16]]),
                in_=psM[0:P, 16:32]),
                writes=[gate], reads=[psM])
        dve(lambda E_: E_.tensor_tensor(out=gate[:, :], in0=gate[:, :],
                                        in1=ptok[:, :], op=ALU.mult),
            writes=[gate], reads=[gate, ptok])
        for st in range(8):
            xe_ = xetm if st % 2 == 0 else xetmB
            for kk in range(4):
                cx.op("p1", lambda E_, st=st, kk=kk, xe_=xe_: E_.indirect_dma_start(
                    out=xe_[:, kk * D:(kk + 1) * D], out_offset=None,
                    in_=pool,
                    in_offset=bass.IndirectOffsetOnAxis(
                        D2[:, st * 4 + kk:st * 4 + kk + 1], 0)),
                    writes=[xe_], reads=[D2, pres], inc=16)
            for g in range(DC):
                for kk in range(4):
                    dma("sp", bass.AP(xefm_g[g], kk * P, [[TT, P], [1, P]]),
                        xe_[:, kk * D + g * P:kk * D + (g + 1) * P],
                        writes=[xefm_g[g]], reads=[xe_], transpose=True,
                        ch="a1" if g % 2 == 0 else "s1")
            for m in range(DHC):
                w_ = wsl[m % 2]
                dma("pop", w_[:, 0:1024],
                    bass.AP(W["W1_my"].tensor, m * P, [[DH, P], [P * DH, DC], [1, P]]),
                    writes=[w_], reads=[])
                ps_ = psA if m % 2 == 0 else psB
                for g in range(DC):
                    mm(ps_[:, 0:TT], w_[:, g * P:(g + 1) * P],
                       bass.AP(xefm_g[g], 0, [[TT, P], [1, TT]]),
                       writes=[ps_] if g == DC - 1 else [],
                       pre=[ps_] if g == 0 else (),
                       reads=([w_] + xefm_g) if g == DC - 1 else
                             [w_, xefm_g[g]],
                       start=(g == 0), stop=(g == DC - 1),
                       inc=1 if g == DC - 1 else 0)
                act(hexp[:, m * TT:(m + 1) * TT], ps_[:, 0:TT], AF.Relu,
                    writes=[hexp], reads=[ps_])
            for o in range(DC):
                for hh in range(2):
                    dma("pop", wsl[hh][:, 0:2048],
                        bass.AP(W["W2_my"].tensor, o * P + hh * 16 * P * D,
                                [[D, P], [P * D, 16], [1, P]]),
                        writes=[wsl[hh]], reads=[])
                ps_ = psA if o % 2 == 0 else psB
                for g in range(DHC):
                    mm(ps_[:, 0:TT], wsl[g // 16][:, (g % 16) * P:(g % 16 + 1) * P],
                       hexp[:, g * TT:(g + 1) * TT],
                       writes=[ps_] if g == DHC - 1 else [],
                       pre=[ps_] if g == 0 else (),
                       reads=[wsl[0], wsl[1], hexp] if g == DHC - 1 else
                             [wsl[g // 16], hexp],
                       start=(g == 0), stop=(g == DHC - 1),
                       inc=1 if g == DHC - 1 else 0)
                act(yefm[:, o * TT:(o + 1) * TT], ps_[:, 0:TT], AF.Identity,
                    writes=[yefm], reads=[ps_])
            for g in range(DC):
                for kk in range(4):
                    dma("sp",
                        bass.AP(xe_, kk * D + g * P, [[xe_.shape[1], P], [1, P]]),
                        yefm[:, g * TT + kk * P:g * TT + (kk + 1) * P],
                        writes=[xe_], reads=[yefm], transpose=True,
                        ch="a1" if kk % 2 == 0 else "s1")
            for h in range(8):
                dma("sp",
                    bass.AP(ye_own.tensor, (512 * h + 64 * st) * D,
                            [[D, 16], [16 * D, 4], [1, D]]),
                    bass.AP(xe_, (16 * h) * xe_.shape[1],
                            [[xe_.shape[1], 16], [D, 4], [1, D]]),
                    writes=[], reads=[xe_])
        cx.fence("pc")
        yres = cx.res("yepool_res")
        pc(lambda E_: E_.collective_compute(
            "AllGather", ALU.bypass, replica_groups=[list(range(8))],
            ins=[ye_own], outs=[ye_pool]), writes=[yres], reads=[], inc=1)
        for cc in range(2 * SC):
            xe_ = xetm if cc % 2 == 0 else xetmB
            ys_ = ystage if cc % 2 == 0 else ystageB
            x4_ = xt4 if cc % 2 == 0 else xt4B
            cx.op("p1", lambda E_, cc=cc, xe_=xe_: E_.indirect_dma_start(
                out=xe_[:, 0:D], out_offset=None,
                in_=ye_pool,
                in_offset=bass.IndirectOffsetOnAxis(sidx[:, cc:cc + 1], 0)),
                writes=[xe_], reads=[sidx, yres], inc=16)
            dma("sp", x4_[:, :],
                bass.AP(x4_dram.tensor, (cc * P) * D, [[D, P], [1, D]]),
                writes=[x4_], reads=[], pre=[x4res])
            dve(lambda E_, cc=cc, xe_=xe_, ys_=ys_: E_.tensor_scalar(
                out=ys_[:, :], in0=xe_[:, 0:D], scalar1=gate[:, cc:cc + 1],
                scalar2=None, op0=ALU.mult),
                writes=[ys_], reads=[xe_, gate])
            dve(lambda E_, ys_=ys_, x4_=x4_: E_.tensor_tensor(
                out=ys_[:, :], in0=ys_[:, :], in1=x4_[:, :], op=ALU.add),
                writes=[ys_], reads=[ys_, x4_])
            dma("sp", bass.AP(y_out.tensor, (cc * P) * D, [[D, P], [1, D]]),
                ys_[:, :], writes=[], reads=[ys_])

    stage_i = STAGES.index(UPTO)
    phase_p0()
    phase_quantw()
    for s in range(NSEQ):
        if stage_i < STAGES.index("b1"):
            break
        SEQ[0] = s
        dma("sp", bass.AP(stream, 0, [[stream.shape[1], P], [D, SC], [1, D]]),
            bass.AP(x_in.tensor, (s * S) * D, [[D, P], [P * D, SC], [1, D]]),
            writes=[stream], reads=[])
        block1()
        if stage_i >= STAGES.index("b2"):
            block2()
        if stage_i >= STAGES.index("b3"):
            block3()
        if stage_i >= STAGES.index("route"):
            route_and_spill(s)
        if UPTO in ("b1", "b2", "b3"):
            break
    if stage_i >= STAGES.index("moe"):
        moe()

    tapbufs = {"stream": stream, "stats": stats, "swrec": swrec, "tmpb": tmpb,
               "xnfm": xnfm, "kfm": kfm, "vt": vt, "qtile": qtile, "avfm": avfm,
               "ptok": ptok, "gate": gate, "sidx": sidx, "consts": consts}
    for tname, sbname, shape, dt in TAPS:
        t_out = nc.dram_tensor(tname, shape, dt, kind="ExternalOutput").ap()
        buf = tapbufs[sbname]
        cx.fence("sp")
        dma("sp", t_out, bass.AP(buf, 0, [[buf.shape[1], shape[0]], [1, shape[1]]]),
            writes=[], reads=[buf])

    cx.fence("sp")
    for ch in ("sp", "s1", "s2", "a1", "pop", "p1"):
        cx.q["sp"].append(lambda E_, sem=cx.sems[ch], v=cx.count[ch]:
                          E_.wait_ge(sem, v))

    cx.emit_all()
    cx.stack.close()
    return nc


_BUILT = {}


def _get_nc():
    key = (UPTO, tuple(t[0] for t in TAPS))
    if key not in _BUILT:
        nc = build()
        lower_extended_insts(nc)
        _BUILT[key] = nc
    return _BUILT[key]


def kernel(**inputs):
    bf = ml_dtypes.bfloat16
    hc = _host_consts()
    x = np.ascontiguousarray(np.asarray(inputs["L0_attn_x"], dtype=np.float32))
    ins = []
    for c in range(8):
        m = {"x": x[2 * c:2 * c + 2]}
        m["W_qkv"] = np.asarray(inputs["L0_attn_W_qkv"], np.float32)
        m["b_qkv"] = np.asarray(inputs["L0_attn_b_qkv"], np.float32)
        m["W_o"] = np.asarray(inputs["L0_attn_W_o"], np.float32)
        m["b_o"] = np.asarray(inputs["L0_attn_b_o"], np.float32)
        m["ffn_w1"] = np.asarray(inputs["L0_ffn_w1"], np.float32)
        m["conv_w"] = np.asarray(inputs["L0_ffn_conv_w"], np.float32)
        m["conv_b"] = np.asarray(inputs["L0_ffn_conv_b"], np.float32)
        m["ffn_w2"] = np.asarray(inputs["L0_ffn_w2"], np.float32)
        for k_src, k_dst in [("L1_attn_wq_lat", "wq_lat"), ("L1_attn_wk_lat", "wk_lat"),
                             ("L1_attn_wv_lat", "wv_lat"), ("L1_attn_wq_heads", "wq_heads"),
                             ("L1_attn_wk_heads", "wk_heads"), ("L1_attn_wv_heads", "wv_heads"),
                             ("L1_attn_w_proj", "w_proj"), ("L1_attn_b_proj", "b_proj"),
                             ("L1_ffn_Wr", "Wr")]:
            m[k_dst] = np.asarray(inputs[k_src], np.float32)
        m["W1_my"] = np.asarray(inputs["L1_ffn_W1"][c], np.float32)
        m["W2_my"] = np.asarray(inputs["L1_ffn_W2"][c], np.float32)
        m.update({k: v for k, v in hc.items()})
        osel = np.zeros((P, 16), np.float32)
        for q in range(16):
            osel[16 * c + q, q] = 1.0
        m["ownsel"] = osel
        cidm = np.zeros((P, 4), np.float32)
        cidm[:, 0] = c
        cidm[:, 1] = c * T
        cidm[:, 2] = c * CAP
        m["cid"] = cidm
        ins.append(m)

    nc = _get_nc()
    res = run_bass_kernel_spmd(nc, ins, core_ids=list(range(8)))
    out = np.concatenate([np.asarray(r["y"]) for r in res.results], axis=0)
    return out.reshape(16, S, D)
